# version 02
# baseline (speedup 1.0000x reference)
import numpy as np
from contextlib import ExitStack

import concourse.bass as bass
import concourse.mybir as mybir
from concourse.bass_utils import run_bass_kernel_spmd

F32 = mybir.dt.float32
BF16 = mybir.dt.bfloat16
AF = mybir.ActivationFunctionType
ALU = mybir.AluOpType
AX = mybir.AxisListType

D = 4096
NCK = 32
DK = 1024
NG = 8
NH = 32
T = 2048
TB = 1024
NSUP = T // TB
NTC = T // 128
E = 3
VIS = 2457
SEQ = 8192
N_VIS = VIS * D
N_TXT = (SEQ - VIS) * D
EPS_LN = 1e-5
NTT2 = T // 512
ENGS = ("t", "v", "s", "g", "y")


class Sem:
    def __init__(self, nc, ctx, name):
        self.h = ctx.enter_context(nc.semaphore(name))
        self.n = 0

    def plan(self, k=1):
        self.n += k
        return self.n


class Bld:
    def __init__(self, nc, ctx):
        self.nc = nc
        self.ops = {e: [] for e in ENGS}
        self.trace = {e: [] for e in ENGS}
        for n in ("s_cst", "s_csy", "s_wkv", "s_xa", "s_wq0", "s_wq1", "s_xt",
                  "s_w10", "s_w11", "s_w20", "s_w21",
                  "s_x1w", "s_x1r", "s_aw0", "s_aw1", "s_ar0", "s_ar1", "s_ar2",
                  "s_y2w0", "s_y2w1", "s_y2r", "s_ow0", "s_ow1",
                  "s_ccd", "s_cc1", "s_cc2",
                  "s_pe", "s_dv", "s_ac", "s_gp", "s_ph"):
            setattr(self, n, Sem(nc, ctx, n))

    def wait(self, eng, sem, val):
        if val <= 0:
            return
        assert val <= sem.n, f"wait {val} > planned {sem.n}"
        self.trace[eng].append(("w", id(sem), val))
        self.ops[eng].append(lambda e, s=sem.h, v=val: e.wait_ge(s, v))

    def _inc(self, eng, sem, k):
        sem.plan(k)
        self.trace[eng].append(("i", id(sem), k))

    def mm(self, out_ap, lhsT_ap, rhs_ap, start, stop, inc=None):
        if inc is not None:
            self._inc("t", inc, 1)
            self.ops["t"].append(
                lambda e, o=out_ap, l=lhsT_ap, r=rhs_ap, a=start, bb=stop, s=inc.h:
                e.matmul(o, l, r, start=a, stop=bb).then_inc(s, 1))
        else:
            self.ops["t"].append(
                lambda e, o=out_ap, l=lhsT_ap, r=rhs_ap, a=start, bb=stop:
                e.matmul(o, l, r, start=a, stop=bb))
        return self.s_pe.n

    def transp(self, out_ap, in_ap, id_ap, inc=None):
        if inc is not None:
            self._inc("t", inc, 1)
            self.ops["t"].append(lambda e, o=out_ap, i=in_ap, d=id_ap, s=inc.h:
                                 e.transpose(o, i, d).then_inc(s, 1))
        else:
            self.ops["t"].append(lambda e, o=out_ap, i=in_ap, d=id_ap:
                                 e.transpose(o, i, d))

    def op(self, eng, meth, *args, inc=None, **kw):
        if inc is not None:
            self._inc(eng, inc, 1)
            self.ops[eng].append(lambda e, m=meth, a=args, k=kw, s=inc.h:
                                 getattr(e, m)(*a, **k).then_inc(s, 1))
        else:
            self.ops[eng].append(lambda e, m=meth, a=args, k=kw:
                                 getattr(e, m)(*a, **k))

    def dma(self, qeng, sem, out_ap, in_ap, **kw):
        self._inc(qeng, sem, 16)
        self.ops[qeng].append(lambda e, o=out_ap, i=in_ap, s=sem.h, k=kw:
                              e.dma_start(out=o, in_=i, **k).then_inc(s, 16))
        return sem.n // 16

    def cc(self, in_ap, out_ap, sem):
        self._inc("g", sem, 1)
        self.ops["g"].append(lambda e, i=in_ap, o=out_ap, s=sem.h:
                             e.collective_compute(
                                 "AllReduce", ALU.add,
                                 replica_groups=[[0, 1, 2, 3], [4, 5, 6, 7]],
                                 ins=[i], outs=[o]).then_inc(s, 1))

    def barrier(self):
        bb = self.barr
        self.wait("v", self.s_gp, self.barr_ready)
        self.wait("s", self.s_gp, self.barr_ready)
        self._inc("v", self.s_ph, 1)
        self.ops["v"].append(lambda en, o=bb, s=self.s_ph.h:
                             en.tensor_copy(o[0:1, 0:1], o[0:1, 1:2]).then_inc(s, 1))
        self._inc("s", self.s_ph, 1)
        self.ops["s"].append(lambda en, o=bb, s=self.s_ph.h:
                             en.copy(o[0:1, 2:3], o[0:1, 3:4]).then_inc(s, 1))
        self._inc("g", self.s_ph, 1)
        self.ops["g"].append(lambda en, o=bb, s=self.s_ph.h:
                             en.memset(o[0:1, 4:5], 0.0).then_inc(s, 1))
        for e in ("t", "y"):
            self._inc(e, self.s_ph, 1)
            self.ops[e].append(lambda en, s=self.s_ph.h: en.sem_inc(s, 1))
        tgt = self.s_ph.n
        for e in ENGS:
            self.wait(e, self.s_ph, tgt)

    def check_deadlock(self):
        pcs = {e: 0 for e in ENGS}
        vals = {}
        progress = True
        while progress:
            progress = False
            for e in ENGS:
                tr = self.trace[e]
                while pcs[e] < len(tr):
                    kind, sem, v = tr[pcs[e]]
                    if kind == "i":
                        vals[sem] = vals.get(sem, 0) + v
                        pcs[e] += 1
                        progress = True
                    elif vals.get(sem, 0) >= v:
                        pcs[e] += 1
                        progress = True
                    else:
                        break
        stuck = {e: (pcs[e], len(self.trace[e])) for e in ENGS
                 if pcs[e] < len(self.trace[e])}
        if stuck:
            raise RuntimeError(f"schedule deadlock: {stuck}")



def fm3(dram_slice, p=128):
    return dram_slice.rearrange("(c p) t -> p c t", p=p)


def sb3(sbuf_slice, w):
    return sbuf_slice.rearrange("p (c t) -> p c t", t=w)

def sap(handle, offset, pattern):
    return bass.AP(handle[:, :].tensor, offset, pattern)


def build_program():
    nc = bass.Bass("TRN2", target_bir_lowering=False)
    ctx = ExitStack()
    b = Bld(nc, ctx)

    def din(name, shape):
        return nc.dram_tensor(name, shape, F32, kind="ExternalInput").ap()

    x0T = din("x0T", [D, T])
    wqT = din("wqT", [D, D])
    wkT = din("wkT", [D, DK])
    wvT = din("wvT", [D, DK])
    woT = din("woT", [D, D])
    w1 = din("w1", [E, D, D])
    w2 = din("w2", [E, D, D])
    bqc_in = din("bqc", [128, NCK])
    bk_r = din("bk_r", [1, DK])
    bv_r = din("bv_r", [1, DK])
    boc_in = din("boc", [128, NCK])
    b1r = din("b1r", [128, E * NCK])
    b2_ = din("b2_", [E, D])
    gwr = din("gwr", [128, E * NCK])
    gwlast = din("gwlast", [1, E])
    gateb = din("gateb", [1, E])
    vism_in = din("vism", [1, T])
    g0s_in = din("g0s", [1, 1])
    ident_in = din("ident", [128, 128])

    outT = nc.dram_tensor("outT", [D, T], F32, kind="ExternalOutput").ap()

    x1b = nc.dram_tensor("x1b", [D, T], BF16, kind="Internal").ap()
    actsb = nc.dram_tensor("actsb", [E * D, T], BF16, kind="Internal").ap()
    y2b = nc.dram_tensor("y2b", [D, T], BF16, kind="Internal").ap()
    cc1_in = nc.dram_tensor("cc1_in", [128, 1032], F32, kind="Internal").ap()
    cc1_out = nc.dram_tensor("cc1_out", [128, 1032], F32, kind="Internal").ap()
    cc2_in = nc.dram_tensor("cc2_in", [1, 8], F32, kind="Internal").ap()
    cc2_out = nc.dram_tensor("cc2_out", [1, 8], F32, kind="Internal").ap()
    scr16 = nc.dram_tensor("scr16", [2, 16, 128], F32, kind="Internal").ap()
    w0d = nc.dram_tensor("w0d", [1, E * T], BF16, kind="Internal").ap()

    def sb(name, shape, dt, st=ctx):
        return st.enter_context(nc.sbuf_tensor(name, shape, dt))

    barr = sb("barr", [1, 8], F32)
    b.barr = barr
    b.op("g", "memset", barr[:, :], 0.0, inc=b.s_gp)
    b.barr_ready = b.s_gp.n
    ones_bfr = sb("ones_bfr", [1, 512], BF16)
    ones_f32 = sb("ones_f32", [1, 512], F32)
    onesc_bf = sb("onesc_bf", [128, 1], BF16)
    b1pp = sb("b1pp", [128, E * NCK], F32)
    gwb = sb("gwb", [128, E * NCK], BF16)
    gwl = sb("gwl", [1, E], F32)
    gbb = sb("gbb", [1, E], F32)
    g0s = sb("g0s_s", [1, 1], F32)
    idf = sb("idf", [128, 128], F32)
    mb_sb = sb("mb_sb", [128, NG * 128], BF16)
    recip = sb("recip", [128, NG], F32)
    g_pre = sb("g_pre", [128, NTC * E], F32)
    sacc = sb("sacc", [1, 8], F32)
    red2 = sb("red2", [1, 8], F32)
    small = sb("small", [1, 32], F32)
    m01 = sb("m01", [1, 8], F32)
    w3b = sb("w3b", [E, T], BF16)

    b.op("g", "memset", ones_bfr[:, :], 1.0)
    b.op("g", "memset", onesc_bf[:, :], 1.0)
    b.op("g", "memset", ones_f32[:, :], 1.0)
    b.op("g", "memset", sacc[:, :], 0.0, inc=b.s_gp)
    GP0 = b.s_gp.n
    b.dma("g", b.s_cst, gwb[:, :], gwr)
    for dst, src in ((b1pp, b1r), (gwl, gwlast), (gbb, gateb), (g0s, g0s_in),
                     (idf, ident_in)):
        b.dma("y", b.s_csy, dst[:, :], src)
    NCSTY = b.s_csy.n // 16

    pa = ExitStack()
    wk_sb = sb("wk_sb", [128, NCK * DK], BF16, pa)
    wv_sb = sb("wv_sb", [128, NCK * DK], BF16, pa)
    XAW = 256
    xa = [sb(f"xa{i}", [128, NCK * XAW], BF16, pa) for i in range(2)]
    kt = [sb(f"kt{i}", [128, DK], BF16, pa) for i in range(2)]
    vt = [sb(f"vt{i}", [128, NG * 129], BF16, pa) for i in range(2)]
    ph1 = [sb(f"ph1_{i}", [128, DK], BF16, pa) for i in range(2)]
    ph2 = [sb(f"ph2_{i}", [128, DK], BF16, pa) for i in range(2)]
    msb = sb("msb", [128, 1032], F32, pa)
    bkb = sb("bkb", [1, DK], BF16, pa)
    bvb = sb("bvb", [1, DK], BF16, pa)
    kp = pa.enter_context(nc.psum_tensor("kp", [128, DK], F32))
    vp = pa.enter_context(nc.psum_tensor("vp", [128, DK], F32))
    mps = [pa.enter_context(nc.psum_tensor(f"mps{i}", [128, 258], F32)) for i in range(4)]

    b.wait("g", b.s_cst, b.s_cst.n)
    b.dma("g", b.s_cst, bkb[:, :], bk_r)
    b.dma("g", b.s_cst, bvb[:, :], bv_r)
    NCST_A = b.s_cst.n // 16
    for i in range(2):
        b.op("g", "memset", vt[i][:, :], 1.0, inc=b.s_gp if i == 1 else None)
    GPV = b.s_gp.n

    for c in range(NCK):
        b.dma("g", b.s_wkv, wk_sb[:, c * DK:(c + 1) * DK], wkT[c * 128:(c + 1) * 128, :])
        b.dma("g", b.s_wkv, wv_sb[:, c * DK:(c + 1) * DK], wvT[c * 128:(c + 1) * 128, :])
    NWKV = b.s_wkv.n // 16

    def load_xa(tile, buf):
        b.wait("g", b.s_xa, b.s_xa.n)
        t0 = tile * XAW
        for q in range(4):
            src = fm3(x0T[q * 1024:(q + 1) * 1024, t0:t0 + XAW])
            b.dma("g", b.s_xa, sb3(buf[:, q * 8 * XAW:(q + 1) * 8 * XAW], XAW), src)
        return b.s_xa.n // 16

    NTA = T // XAW
    SUBS = XAW // 128
    pe_proj = [0] * NTC
    dv_k = [0] * NTC
    ac_v = [0] * NTC
    ac_exp = [0] * NTC
    pe_m = [0] * NTC

    xa_cnt = [load_xa(0, xa[0])]
    for tile in range(NTA):
        for j in range(SUBS):
            q = tile * SUBS + j
            bi = q % 2
            if q == 0:
                b.wait("t", b.s_wkv, 16 * NWKV)
                b.wait("t", b.s_cst, 16 * NCST_A)
                b.wait("t", b.s_gp, GPV)
            if j == 0:
                b.wait("t", b.s_xa, 16 * xa_cnt[tile])
            if q >= 1:
                b.wait("t", b.s_dv, dv_k[q - 1])
                b.wait("t", b.s_ac, ac_v[q - 1])
            lx = xa[tile % 2]
            toff = j * 128
            for dst_p, wsb, brow in ((kp, wk_sb, bkb), (vp, wv_sb, bvb)):
                for fh in range(2):
                    for c in range(NCK):
                        b.mm(dst_p[:, fh * 512:(fh + 1) * 512],
                             lx[:, c * XAW + toff:c * XAW + toff + 128],
                             wsb[:, c * DK + fh * 512:c * DK + fh * 512 + 512],
                             start=(c == 0), stop=False)
                    b.mm(dst_p[:, fh * 512:(fh + 1) * 512], ones_bfr[0:1, 0:128],
                         brow[0:1, fh * 512:fh * 512 + 512], start=False, stop=True,
                         inc=b.s_pe if (dst_p is vp and fh == 1) else None)
            pe_proj[q] = b.s_pe.n
            b.wait("v", b.s_pe, pe_proj[q])
            if q >= 2:
                b.wait("v", b.s_ac, ac_exp[q - 2])
                b.wait("v", b.s_pe, pe_m[q - 2])
            b.op("v", "tensor_scalar_min", ph1[bi][:, :], kp[:, :], 0.0, inc=b.s_dv)
            b.wait("s", b.s_dv, b.s_dv.n)
            b.op("s", "activation", ph2[bi][:, :], ph1[bi][:, :], AF.Exp, inc=b.s_ac)
            ac_exp[q] = b.s_ac.n
            b.wait("v", b.s_ac, ac_exp[q])
            b.op("v", "scalar_tensor_tensor", kt[bi][:, :], kp[:, :], 0.0,
                 ph2[bi][:, :], ALU.max, ALU.add, inc=b.s_dv)
            dv_k[q] = b.s_dv.n
            b.wait("s", b.s_pe, pe_proj[q])
            if q >= 2:
                b.wait("s", b.s_pe, pe_m[q - 2])
            for g in range(NG):
                b.op("s", "copy", vt[bi][:, g * 129:g * 129 + 128],
                     vp[:, g * 128:(g + 1) * 128],
                     inc=b.s_ac if g == NG - 1 else None)
            ac_v[q] = b.s_ac.n
            b.wait("t", b.s_dv, dv_k[q])
            b.wait("t", b.s_ac, ac_v[q])
            for g in range(NG):
                bank, half = divmod(g, 2)
                b.mm(mps[bank][:, half * 129:half * 129 + 129],
                     kt[bi][:, g * 128:(g + 1) * 128],
                     vt[bi][:, g * 129:g * 129 + 129],
                     start=(q == 0 and half == 0),
                     stop=(q == NTC - 1 and half == 1),
                     inc=b.s_pe if g == NG - 1 else None)
            pe_m[q] = b.s_pe.n
        nxt = tile + 1
        if nxt < NTA:
            if nxt >= 2:
                b.wait("g", b.s_pe, pe_m[(nxt - 1) * SUBS - 1])
            xa_cnt.append(load_xa(nxt, xa[nxt % 2]))

    b.wait("v", b.s_pe, pe_m[NTC - 1])
    for bk in range(4):
        b.op("v", "tensor_copy", msb[:, bk * 258:(bk + 1) * 258], mps[bk][:, :],
             inc=b.s_dv if bk == 3 else None)
    b.wait("g", b.s_dv, b.s_dv.n)
    n1 = b.dma("g", b.s_ccd, cc1_in, msb[:, :])
    b.wait("g", b.s_ccd, 16 * n1)
    b.cc(cc1_in, cc1_out, b.s_cc1)
    b.wait("g", b.s_cc1, b.s_cc1.n)
    n2 = b.dma("g", b.s_ccd, msb[:, :], cc1_out)
    b.wait("v", b.s_ccd, 16 * n2)
    ksum_view = sap(msb, 128, [[1032, 128], [129, NG]])
    b.op("v", "tensor_scalar_add", recip[:, :], ksum_view, 1e-10, inc=b.s_dv)
    b.wait("v", b.s_dv, b.s_dv.n)
    b.op("v", "reciprocal", recip[:, :], recip[:, :], inc=b.s_dv)
    for g in range(NG):
        src = sap(msb, 258 * (g // 2) + 129 * (g % 2), [[1032, 128], [1, 128]])
        b.op("v", "tensor_copy", mb_sb[:, g * 128:(g + 1) * 128], src,
             inc=b.s_dv if g == NG - 1 else None)
    dv_mready = b.s_dv.n

    b.barrier()
    pa.close()

    pb = ExitStack()
    pbb = ExitStack()
    stby = sb("stby", [1, TB], F32, pb)
    stbq = sb("stbq", [1, TB], F32, pb)
    strg = sb("strg", [1, T], F32, pb)
    vmrow = sb("vmrow", [1, T], F32, pb)
    rows = sb("rows", [1, 2 * TB], F32, pb)
    bqc = sb("bqc_s", [128, NCK], F32, pb)
    boc = sb("boc_s", [128, NCK], F32, pb)
    qp = [pb.enter_context(nc.psum_tensor(f"qp{i}", [128, 512], F32)) for i in range(2)]
    app = pb.enter_context(nc.psum_tensor("app", [128, 512], F32))
    opp = [pb.enter_context(nc.psum_tensor(f"opp{i}", [128, 512], F32)) for i in range(2)]
    sta = pb.enter_context(nc.psum_tensor("sta", [128, 512], F32))
    stb = pb.enter_context(nc.psum_tensor("stb", [128, 512], F32))
    xt = sb("xt", [128, NCK * TB], BF16, pbb)
    at = sb("at", [128, NH * TB], BF16, pbb)
    wcol = [sb(f"wcol{i}", [128, NCK * 128], BF16, pbb) for i in range(2)]
    qsb = sb("qsb", [128, 512], BF16, pbb)
    bh1 = sb("bh1", [128, 512], BF16, pbb)
    bh2 = sb("bh2", [128, 512], BF16, pbb)
    sqs = [sb(f"sqs{i}", [128, TB], BF16, pbb) for i in range(2)]

    b.dma("y", b.s_csy, bqc[:, :], bqc_in)
    b.dma("y", b.s_csy, boc[:, :], boc_in)
    b.dma("y", b.s_csy, vmrow[:, :], vism_in)
    NCSTY_B = b.s_csy.n // 16

    def load_wcol(par, dst, wsrc, col):
        sem = b.s_wq0 if par == 0 else b.s_wq1
        b.wait("g", sem, sem.n)
        for qq in range(4):
            src = fm3(wsrc[qq * 1024:(qq + 1) * 1024, col * 128:(col + 1) * 128])
            b.dma("g", sem, sb3(dst[:, qq * 8 * 128:(qq + 1) * 8 * 128], 128), src)
        return (sem, sem.n)

    def load_xtile(sup):
        b.wait("g", b.s_xt, b.s_xt.n)
        t0 = sup * TB
        for qq in range(4):
            src = fm3(x0T[qq * 1024:(qq + 1) * 1024, t0:t0 + TB])
            b.dma("g", b.s_xt, sb3(xt[:, qq * 8 * TB:(qq + 1) * 8 * TB], TB), src)
        return b.s_xt.n // 16

    n_x1w = 0
    xt_cnt = load_xtile(0)
    prev_sup_last_pe = 0
    dv_gate_done = 0
    for sup in range(NSUP):
        t0 = sup * TB
        if sup > 0:
            b.wait("g", b.s_pe, prev_sup_last_pe)
        wq_loaded = [load_wcol(0, wcol[0], wqT, 0), load_wcol(1, wcol[1], wqT, 1)]

        pe_head = [0] * NH
        dv_q = [[0, 0] for _ in range(NH)]
        ac_attn = [[0, 0] for _ in range(NH)]
        pe_oproj = [0] * NCK
        dv_y = [[0, 0] for _ in range(NCK)]
        ac_sq = [0] * NCK
        dv_stat = [0] * NCK
        pe_statmm = [0] * NCK

        dvq_prev = {0: 0, 1: 0}
        ac_prev_attn = 0
        pe_prev_attn = 0
        for h in range(NH):
            g = h // 4
            if h == 0:
                b.wait("t", b.s_xt, 16 * xt_cnt)
                b.wait("t", b.s_csy, 16 * NCSTY_B)
            b.wait("t", *wq_loaded[h])
            for i in range(2):
                if dvq_prev[i]:
                    b.wait("t", b.s_dv, dvq_prev[i])
                for c in range(NCK):
                    b.mm(qp[i][:, :],
                         wcol[h % 2][:, c * 128:(c + 1) * 128],
                         xt[:, c * TB + i * 512:c * TB + i * 512 + 512],
                         start=(c == 0), stop=(c == NCK - 1),
                         inc=b.s_pe if c == NCK - 1 else None)
                pe_q = b.s_pe.n
                b.wait("v", b.s_pe, pe_q)
                b.wait("v", b.s_ac, b.s_ac.n)
                b.op("v", "tensor_scalar", bh1[:, :], qp[i][:, :],
                     bqc[:, h:h + 1], 0.0, ALU.add, ALU.min, inc=b.s_dv)
                b.wait("s", b.s_dv, b.s_dv.n)
                b.op("s", "activation", bh2[:, :], bh1[:, :], AF.Exp, inc=b.s_ac)
                b.wait("v", b.s_ac, b.s_ac.n)
                b.op("v", "tensor_scalar", bh1[:, :], qp[i][:, :],
                     bqc[:, h:h + 1], 0.0, ALU.add, ALU.max, inc=b.s_dv)
                b.wait("v", b.s_dv, b.s_dv.n)
                if pe_prev_attn:
                    b.wait("v", b.s_pe, pe_prev_attn)
                b.op("v", "tensor_add", qsb[:, :], bh1[:, :], bh2[:, :], inc=b.s_dv)
                dv_q[h][i] = b.s_dv.n
                dvq_prev[i] = b.s_dv.n
                b.wait("t", b.s_dv, dv_q[h][i])
                if sup == 0 and h == 0 and i == 0:
                    b.wait("t", b.s_dv, dv_mready)
                if ac_prev_attn:
                    b.wait("t", b.s_ac, ac_prev_attn)
                b.mm(app[:, :], mb_sb[:, g * 128:(g + 1) * 128], qsb[:, :],
                     start=True, stop=True, inc=b.s_pe)
                pe_prev_attn = b.s_pe.n
                b.wait("s", b.s_pe, pe_prev_attn)
                b.op("s", "activation", at[:, h * TB + i * 512:h * TB + i * 512 + 512],
                     app[:, :], AF.Copy, scale=recip[:, g:g + 1], inc=b.s_ac)
                ac_attn[h][i] = b.s_ac.n
                ac_prev_attn = b.s_ac.n
            pe_head[h] = b.s_pe.n
            if h + 2 < NH:
                b.wait("g", b.s_pe, pe_head[h])
                wq_loaded.append(load_wcol(h % 2, wcol[h % 2], wqT, h + 2))
        b.wait("g", b.s_pe, pe_head[NH - 2])
        wq_loaded.append(load_wcol(0, wcol[0], woT, 0))
        b.wait("g", b.s_pe, pe_head[NH - 1])
        wq_loaded.append(load_wcol(1, wcol[1], woT, 1))

        def emit_stats(so):
            b.wait("t", b.s_dv, dv_y[so][1])
            b.wait("t", b.s_ac, ac_sq[so])
            if so >= 1:
                b.wait("t", b.s_dv, dv_stat[so - 1])
            if so >= 1:
                b.wait("v", b.s_dv, dv_stat[so - 1])
            for i in range(2):
                if i == 1:
                    b.wait("t", b.s_dv, b.s_dv.n)
                b.mm(sta[0:1, :], onesc_bf[:, 0:1],
                     xt[:, so * TB + i * 512:so * TB + i * 512 + 512],
                     start=True, stop=True)
                b.mm(stb[0:1, :], onesc_bf[:, 0:1],
                     sqs[so % 2][:, i * 512:(i + 1) * 512],
                     start=True, stop=True, inc=b.s_pe)
                pe_st = b.s_pe.n
                pe_statmm[so] = pe_st
                b.wait("v", b.s_pe, pe_st)
                if so == 0:
                    b.op("v", "tensor_copy", stby[0:1, i * 512:(i + 1) * 512],
                         sta[0:1, :])
                    b.op("v", "tensor_copy", stbq[0:1, i * 512:(i + 1) * 512],
                         stb[0:1, :], inc=b.s_dv)
                else:
                    b.op("v", "tensor_tensor", stby[0:1, i * 512:(i + 1) * 512],
                         sta[0:1, :], stby[0:1, i * 512:(i + 1) * 512], ALU.add)
                    b.op("v", "tensor_tensor", stbq[0:1, i * 512:(i + 1) * 512],
                         stb[0:1, :], stbq[0:1, i * 512:(i + 1) * 512], ALU.add,
                         inc=b.s_dv)
            dv_stat[so] = b.s_dv.n

        for oc in range(NCK):
            gi = 32 + oc
            b.wait("t", *wq_loaded[gi])
            if oc == 0:
                b.wait("t", b.s_ac, ac_attn[NH - 1][1])
            for i in range(2):
                if oc >= 1:
                    b.wait("t", b.s_dv, dv_y[oc - 1][i])
                for c in range(NCK):
                    b.mm(opp[i][:, :],
                         wcol[gi % 2][:, c * 128:(c + 1) * 128],
                         at[:, c * TB + i * 512:c * TB + i * 512 + 512],
                         start=(c == 0), stop=(c == NCK - 1),
                         inc=b.s_pe if c == NCK - 1 else None)
                pe_o = b.s_pe.n
                b.wait("v", b.s_pe, pe_o)
                b.op("v", "scalar_tensor_tensor",
                     xt[:, oc * TB + i * 512:oc * TB + i * 512 + 512],
                     opp[i][:, :], boc[:, oc:oc + 1],
                     xt[:, oc * TB + i * 512:oc * TB + i * 512 + 512],
                     ALU.add, ALU.add, inc=b.s_dv)
                dv_y[oc][i] = b.s_dv.n
            pe_oproj[oc] = b.s_pe.n
            b.wait("s", b.s_dv, dv_y[oc][1])
            if oc >= 2:
                b.wait("s", b.s_pe, pe_statmm[oc - 2])
            b.op("s", "activation", sqs[oc % 2][:, 0:512],
                 xt[:, oc * TB:oc * TB + 512], AF.Square)
            b.op("s", "activation", sqs[oc % 2][:, 512:1024],
                 xt[:, oc * TB + 512:oc * TB + 1024], AF.Square, inc=b.s_ac)
            ac_sq[oc] = b.s_ac.n
            if gi + 2 < 64:
                b.wait("g", b.s_pe, pe_oproj[oc])
                wq_loaded.append(load_wcol(gi % 2, wcol[gi % 2], woT, oc + 2))
            if oc >= 1:
                emit_stats(oc - 1)
        emit_stats(NCK - 1)

        b.wait("v", b.s_dv, dv_stat[NCK - 1])
        b.op("v", "tensor_scalar_mul", stby[0:1, :], stby[0:1, :], 1.0 / D,
             inc=b.s_dv)
        b.wait("s", b.s_dv, b.s_dv.n)
        b.op("s", "square", rows[0:1, 0:TB], stby[0:1, :], inc=b.s_ac)
        b.op("v", "tensor_scalar_mul", stbq[0:1, :], stbq[0:1, :], 1.0 / D,
             inc=b.s_dv)
        b.wait("v", b.s_ac, b.s_ac.n)
        b.wait("v", b.s_dv, b.s_dv.n)
        b.op("v", "tensor_sub", stbq[0:1, :], stbq[0:1, :], rows[0:1, 0:TB],
             inc=b.s_dv)
        b.wait("v", b.s_dv, b.s_dv.n)
        b.op("v", "tensor_scalar_add", stbq[0:1, :], stbq[0:1, :], EPS_LN,
             inc=b.s_dv)
        b.wait("s", b.s_dv, b.s_dv.n)
        b.op("s", "activation", stbq[0:1, :], stbq[0:1, :], AF.Ln, inc=b.s_ac)
        b.wait("s", b.s_ac, b.s_ac.n)
        b.op("s", "activation", rows[0:1, 0:TB], stbq[0:1, :], AF.Exp, scale=-0.5,
             inc=b.s_ac)
        b.wait("v", b.s_ac, b.s_ac.n)
        b.op("v", "tensor_mul", rows[0:1, TB:2 * TB], stby[0:1, :], rows[0:1, 0:TB],
             inc=b.s_dv)
        b.wait("v", b.s_dv, b.s_dv.n)
        b.op("v", "tensor_scalar_mul", rows[0:1, TB:2 * TB], rows[0:1, TB:2 * TB],
             -1.0, inc=b.s_dv)
        dv_rows = b.s_dv.n

        dv_p2 = [[0] * NCK, [0] * NCK]
        pe_strg = [[0] * NCK, [0] * NCK]
        dv_strg_prev = 0
        for i in range(2):
            b.wait("t", b.s_dv, dv_rows)
            if i == 1:
                b.wait("t", b.s_dv, dv_p2[0][NCK - 1])
            else:
                b.wait("t", b.s_dv, dvq_prev[0])
                b.wait("t", b.s_dv, dvq_prev[1])
            b.mm(qp[0][:, :], ones_f32[0:1, 0:128],
                 rows[0:1, i * 512:(i + 1) * 512], start=True, stop=True)
            b.mm(qp[1][:, :], ones_f32[0:1, 0:128],
                 rows[0:1, TB + i * 512:TB + (i + 1) * 512],
                 start=True, stop=True, inc=b.s_pe)
            pe_bc = b.s_pe.n
            b.wait("v", b.s_pe, pe_bc)
            for oc in range(NCK):
                seg = xt[:, oc * TB + i * 512:oc * TB + i * 512 + 512]
                b.op("v", "tensor_tensor", seg, qp[0][:, :], seg, ALU.mult,
                     inc=b.s_dv)
                b.wait("v", b.s_dv, b.s_dv.n)
                b.op("v", "tensor_tensor", seg, qp[1][:, :], seg, ALU.add, inc=b.s_dv)
                dv_p2[i][oc] = b.s_dv.n
                b.wait("y", b.s_dv, dv_p2[i][oc])
                n_x1w = b.dma("y", b.s_x1w,
                              x1b[oc * 128:(oc + 1) * 128,
                                  t0 + i * 512:t0 + i * 512 + 512], seg)
                b.wait("s", b.s_dv, dv_p2[i][oc])
                if oc >= 2:
                    b.wait("s", b.s_pe, pe_strg[i][oc - 2])
                b.op("s", "activation", sqs[oc % 2][:, 0:512], seg, AF.Abs, inc=b.s_ac)
                ac_ab = b.s_ac.n
                b.wait("t", b.s_ac, ac_ab)
                if dv_strg_prev:
                    b.wait("t", b.s_dv, dv_strg_prev)
                b.mm(sta[0:1, :], onesc_bf[:, 0:1], sqs[oc % 2][:, 0:512],
                     start=True, stop=True, inc=b.s_pe)
                pe_strg[i][oc] = b.s_pe.n
                b.wait("v", b.s_pe, pe_strg[i][oc])
                if dv_strg_prev:
                    b.wait("v", b.s_dv, dv_strg_prev)
                dst = strg[0:1, t0 + i * 512:t0 + i * 512 + 512]
                if oc == 0:
                    b.op("v", "tensor_copy", dst, sta[0:1, :], inc=b.s_dv)
                else:
                    b.op("v", "tensor_tensor", dst, sta[0:1, :], dst, ALU.add,
                         inc=b.s_dv)
                dv_strg_prev = b.s_dv.n

        b.wait("t", b.s_dv, dv_p2[1][NCK - 1])
        dv_g_prev = 0
        for tc in range(TB // 128):
            tg = sup * (TB // 128) + tc
            if dv_g_prev:
                b.wait("t", b.s_dv, dv_g_prev)
            for c in range(NCK):
                b.mm(app[:, 0:E], xt[:, c * TB + tc * 128:c * TB + tc * 128 + 128],
                     gwb[:, c * E:(c + 1) * E], start=(c == 0), stop=(c == NCK - 1),
                     inc=b.s_pe if c == NCK - 1 else None)
            pe_g = b.s_pe.n
            b.wait("v", b.s_pe, pe_g)
            b.op("v", "tensor_copy", g_pre[:, tg * E:(tg + 1) * E], app[:, 0:E],
                 inc=b.s_dv)
            dv_g_prev = b.s_dv.n
        dv_gate_done = b.s_dv.n
        prev_sup_last_pe = b.s_pe.n

        if sup + 1 < NSUP:
            b.wait("g", b.s_dv, dv_gate_done)
            b.wait("g", b.s_x1w, 16 * n_x1w)
            xt_cnt = load_xtile(sup + 1)

    N_X1W = b.s_x1w.n // 16
    b.wait("y", b.s_x1w, 16 * N_X1W)
    b.barrier()
    pbb.close()

    b.wait("v", b.s_csy, 16 * NCSTY_B)
    b.wait("v", b.s_dv, dv_strg_prev)
    b.op("v", "tensor_mul", vmrow[:, :], vmrow[:, :], strg[:, :], inc=b.s_dv)
    b.wait("v", b.s_dv, b.s_dv.n)
    b.op("v", "reduce_sum", sacc[0:1, 0:1], vmrow[0:1, :], axis=AX.X)
    b.op("v", "reduce_sum", sacc[0:1, 1:2], strg[0:1, :], axis=AX.X, inc=b.s_dv)
    b.op("v", "tensor_scalar_mul", sacc[0:1, 2:5], g_pre[0:1, 0:E],
         g0s[0:1, 0:1], inc=b.s_dv)
    b.wait("g", b.s_dv, b.s_dv.n)
    b.wait("g", b.s_gp, GP0)
    n3 = b.dma("g", b.s_ccd, cc2_in, sacc[:, :])
    b.wait("g", b.s_ccd, 16 * n3)
    b.cc(cc2_in, cc2_out, b.s_cc2)
    b.wait("g", b.s_cc2, b.s_cc2.n)
    n4 = b.dma("g", b.s_ccd, red2[:, :], cc2_out)
    b.wait("v", b.s_ccd, 16 * n4)
    b.op("v", "tensor_sub", small[0:1, 0:1], red2[0:1, 1:2], red2[0:1, 0:1])
    b.op("v", "tensor_scalar_mul", small[0:1, 1:2], red2[0:1, 0:1], 1.0 / N_VIS,
         inc=b.s_dv)
    b.wait("v", b.s_dv, b.s_dv.n)
    b.op("v", "tensor_scalar_mul", small[0:1, 0:1], small[0:1, 0:1], 1.0 / N_TXT)
    b.op("v", "tensor_scalar_add", small[0:1, 1:2], small[0:1, 1:2], 1e-8, inc=b.s_dv)
    b.wait("v", b.s_dv, b.s_dv.n)
    b.op("v", "reciprocal", small[0:1, 1:2], small[0:1, 1:2], inc=b.s_dv)
    b.wait("v", b.s_dv, b.s_dv.n)
    b.op("v", "tensor_mul", small[0:1, 2:3], small[0:1, 0:1], small[0:1, 1:2],
         inc=b.s_dv)
    b.wait("v", b.s_dv, b.s_dv.n)
    b.op("v", "tensor_scalar_mul", small[0:1, 4:7], gwl[0:1, :], small[0:1, 2:3],
         inc=b.s_dv)
    b.wait("v", b.s_dv, b.s_dv.n)
    b.op("v", "tensor_add", small[0:1, 4:7], small[0:1, 4:7], gbb[0:1, :], inc=b.s_dv)
    b.wait("v", b.s_dv, b.s_dv.n)
    b.op("v", "tensor_add", small[0:1, 8:11], red2[0:1, 2:5], small[0:1, 4:7],
         inc=b.s_dv)
    b.wait("v", b.s_dv, b.s_dv.n)
    a0, a1, a2 = small[0:1, 8:9], small[0:1, 9:10], small[0:1, 10:11]
    b.op("v", "tensor_tensor", small[0:1, 12:13], a0, a1, ALU.is_ge)
    b.op("v", "tensor_tensor", small[0:1, 13:14], a0, a2, ALU.is_ge)
    b.op("v", "tensor_tensor", small[0:1, 14:15], a1, a2, ALU.is_ge, inc=b.s_dv)
    b.wait("v", b.s_dv, b.s_dv.n)
    b.op("v", "tensor_add", small[0:1, 16:17], small[0:1, 12:13], small[0:1, 13:14])
    b.op("v", "tensor_sub", small[0:1, 17:18], small[0:1, 12:13], small[0:1, 14:15])
    b.op("v", "tensor_add", small[0:1, 18:19], small[0:1, 13:14], small[0:1, 14:15],
         inc=b.s_dv)
    b.wait("v", b.s_dv, b.s_dv.n)
    b.op("v", "tensor_scalar", small[0:1, 16:17], small[0:1, 16:17], -1.0, 2.0,
         ALU.mult, ALU.add)
    b.op("v", "tensor_scalar_add", small[0:1, 17:18], small[0:1, 17:18], 1.0,
         inc=b.s_dv)
    b.wait("v", b.s_dv, b.s_dv.n)
    b.op("v", "tensor_single_scalar", m01[0:1, 0:3], small[0:1, 16:19], 0.0,
         ALU.is_equal)
    b.op("v", "tensor_single_scalar", m01[0:1, 3:6], small[0:1, 16:19], 1.0,
         ALU.is_equal, inc=b.s_dv)

    gm = sb("gm", [128, 6 * 16], F32, pb)
    lg = sb("lg", [128, NTC * E], F32, pb)
    b.wait("t", b.s_dv, b.s_dv.n)
    b.mm(app[:, 0:E], ones_f32[0:1, 0:128], small[0:1, 4:7], start=True, stop=True,
         inc=b.s_pe)
    pe_add3 = b.s_pe.n
    b.wait("v", b.s_pe, pe_add3)
    for tc in range(NTC):
        b.op("v", "tensor_tensor", lg[:, tc * E:(tc + 1) * E],
             app[:, 0:E], g_pre[:, tc * E:(tc + 1) * E], ALU.add,
             inc=b.s_dv if tc == NTC - 1 else None)
    b.wait("s", b.s_dv, b.s_dv.n)
    b.op("s", "activation", lg[:, :], lg[:, :], AF.Exp, inc=b.s_ac)
    b.wait("v", b.s_ac, b.s_ac.n)
    lg3 = lg[:, :].rearrange("p (c e) -> p c e", e=E)
    b.op("v", "reduce_max", gm[:, 0:16], lg3, axis=AX.X)
    b.op("v", "tensor_reduce", gm[:, 16:32], lg3, axis=AX.X, op=ALU.min)
    b.op("v", "reduce_sum", gm[:, 32:48], lg3, axis=AX.X, inc=b.s_dv)
    b.wait("v", b.s_dv, b.s_dv.n)
    b.op("v", "tensor_sub", gm[:, 48:64], gm[:, 32:48], gm[:, 0:16], inc=b.s_dv)
    b.wait("v", b.s_dv, b.s_dv.n)
    b.op("v", "tensor_sub", gm[:, 48:64], gm[:, 48:64], gm[:, 16:32], inc=b.s_dv)
    b.wait("v", b.s_dv, b.s_dv.n)
    b.op("v", "tensor_scalar_mul", gm[:, 64:80], gm[:, 32:48], 1e-8, inc=b.s_dv)
    b.wait("v", b.s_dv, b.s_dv.n)
    b.op("v", "tensor_add", gm[:, 64:80], gm[:, 64:80], gm[:, 0:16], inc=b.s_dv)
    b.wait("v", b.s_dv, b.s_dv.n)
    b.op("v", "tensor_add", gm[:, 64:80], gm[:, 64:80], gm[:, 48:64], inc=b.s_dv)
    b.wait("v", b.s_dv, b.s_dv.n)
    b.op("v", "reciprocal", gm[:, 64:80], gm[:, 64:80], inc=b.s_dv)
    b.wait("v", b.s_dv, b.s_dv.n)
    b.op("v", "tensor_mul", gm[:, 0:16], gm[:, 0:16], gm[:, 64:80])
    b.op("v", "tensor_mul", gm[:, 48:64], gm[:, 48:64], gm[:, 64:80], inc=b.s_dv)
    dv_c = b.s_dv.n
    c0row = sb("c0row", [1, T], F32, pb)
    c1row = sb("c1row", [1, T], F32, pb)
    ctr = sb("ctr", [16, 128], F32, pb)
    b.wait("t", b.s_dv, dv_c)
    b.wait("t", b.s_csy, 16 * NCSTY_B)
    for idx, (c0_, dst_row) in enumerate(((0, c0row), (48, c1row))):
        if idx == 1:
            b.wait("t", b.s_ac, b.s_ac.n)
        b.transp(app[0:16, 0:128], gm[:, c0_:c0_ + 16], idf[:, :], inc=b.s_pe)
        b.wait("s", b.s_pe, b.s_pe.n)
        b.wait("s", b.s_ccd, 16 * (b.s_ccd.n // 16))
        b.op("s", "copy", ctr[:, :], app[0:16, 0:128], inc=b.s_ac)
        b.wait("g", b.s_ac, b.s_ac.n)
        nci = b.dma("g", b.s_ccd, scr16[idx], ctr[:, :])
        b.wait("g", b.s_ccd, 16 * nci)
        ncj = b.dma("g", b.s_ccd, dst_row[0:1, :],
                    scr16[idx].rearrange("a bb -> (a bb)")[None, :])
    b.wait("v", b.s_ccd, 16 * (b.s_ccd.n // 16))
    wbr = sb("wbr", [1, T], BF16, pb)
    dv_wrow = [0] * E
    w3cnt = [0] * E
    for e in range(E):
        if e > 0:
            b.wait("v", b.s_ccd, 16 * w3cnt[e - 1])
        for half in range(2):
            hs = slice(half * TB, (half + 1) * TB)
            if e > 0 or half > 0:
                b.wait("v", b.s_dv, b.s_dv.n)
            b.op("v", "tensor_scalar_mul", rows[0:1, 0:TB], c0row[0:1, hs],
                 m01[0:1, e:e + 1])
            b.op("v", "tensor_scalar_mul", rows[0:1, TB:2 * TB], c1row[0:1, hs],
                 m01[0:1, 3 + e:4 + e], inc=b.s_dv)
            b.wait("v", b.s_dv, b.s_dv.n)
            b.op("v", "tensor_add", rows[0:1, 0:TB], rows[0:1, 0:TB],
                 rows[0:1, TB:2 * TB], inc=b.s_dv)
            b.wait("v", b.s_dv, b.s_dv.n)
            b.op("v", "tensor_copy", wbr[0:1, hs], rows[0:1, 0:TB], inc=b.s_dv)
        dv_wrow[e] = b.s_dv.n
        b.wait("g", b.s_dv, dv_wrow[e])
        b.wait("g", b.s_ccd, b.s_ccd.n)
        b.dma("g", b.s_ccd, w3b[e:e + 1, :], wbr[0:1, :])
        w3cnt[e] = b.dma("g", b.s_ccd, w0d[0:1, e * T:(e + 1) * T], wbr[0:1, :])
    b.wait("y", b.s_ccd, 16 * w3cnt[E - 1])

    b.barrier()
    pb.close()

    f1 = ExitStack()
    w0b = sb("w0b", [1, E * T], BF16, f1)
    x1s = sb("x1s", [128, NCK * T], BF16, f1)
    wbc = sb("wbc", [128, E * T], BF16, f1)
    w1col = [sb(f"w1col{i}", [128, NCK * 128], BF16, f1) for i in range(2)]
    t1b = [sb(f"t1b{i}", [128, 512], BF16, f1) for i in range(2)]
    actb = [sb(f"actb{i}", [128, T], BF16, f1) for i in range(2)]
    h1p = [f1.enter_context(nc.psum_tensor(f"h1p{i}", [128, 512], F32))
           for i in range(2)]

    b.wait("g", b.s_x1w, 16 * N_X1W)
    b.wait("g", b.s_cst, b.s_cst.n)
    nw0 = b.dma("g", b.s_cst, w0b[:, :], w0d)
    b.wait("t", b.s_cst, 16 * nw0)
    dv_wbc = 0
    for e in range(E):
        for qq in range(T // 512):
            if dv_wbc:
                b.wait("t", b.s_dv, dv_wbc)
            b.mm(h1p[0][:, :], ones_bfr[0:1, 0:128],
                 w0b[0:1, e * T + qq * 512:e * T + (qq + 1) * 512],
                 start=True, stop=True, inc=b.s_pe)
            b.wait("v", b.s_pe, b.s_pe.n)
            b.op("v", "tensor_copy", wbc[:, e * T + qq * 512:e * T + (qq + 1) * 512],
                 h1p[0][:, :], inc=b.s_dv)
            dv_wbc = b.s_dv.n

    nx1 = 0
    for qq in range(8):
        nx1 = b.dma("y", b.s_x1r, sb3(x1s[:, qq * 4 * T:(qq + 1) * 4 * T], T),
                    fm3(x1b[qq * 512:(qq + 1) * 512, :]))

    def load_w1col(par, e, fc):
        sem = b.s_w10 if par == 0 else b.s_w11
        b.wait("g", sem, sem.n)
        for qq in range(4):
            src = fm3(w1[e, qq * 1024:(qq + 1) * 1024, fc * 128:(fc + 1) * 128])
            b.dma("g", sem, sb3(w1col[par][:, qq * 8 * 128:(qq + 1) * 8 * 128], 128), src)
        return (sem, sem.n)

    w1_loaded = [load_w1col(0, 0, 0), load_w1col(1, 0, 1)]
    b.wait("s", b.s_csy, 16 * NCSTY_B)
    ACT_FN = {0: AF.Tanh, 1: AF.Gelu, 2: AF.Silu}
    ac_f1 = {}
    dv_f1 = {}
    aw_cnt = {}
    pe_f1 = {}
    for e in range(E):
        for fc in range(NCK):
            ci = e * NCK + fc
            b.wait("t", *w1_loaded[ci])
            if ci == 0:
                b.wait("t", b.s_x1r, 16 * nx1)
                b.wait("t", b.s_dv, dv_wbc)
            for tt in range(4):
                idx = ci * 4 + tt
                pp = idx % 2
                if idx >= 2:
                    b.wait("t", b.s_ac, ac_f1[idx - 2])
                for c in range(NCK):
                    b.mm(h1p[pp][:, :], w1col[ci % 2][:, c * 128:(c + 1) * 128],
                         x1s[:, c * T + tt * 512:c * T + tt * 512 + 512],
                         start=(c == 0), stop=(c == NCK - 1),
                         inc=b.s_pe if c == NCK - 1 else None)
                pe_f1[idx] = b.s_pe.n
                b.wait("s", b.s_pe, pe_f1[idx])
                if idx >= 2:
                    b.wait("s", b.s_dv, dv_f1[idx - 2])
                b.op("s", "activation", t1b[pp][:, :], h1p[pp][:, :], ACT_FN[e],
                     bias=b1pp[:, ci:ci + 1], inc=b.s_ac)
                ac_f1[idx] = b.s_ac.n
                b.wait("v", b.s_ac, ac_f1[idx])
                if ci >= 2 and tt == 0:
                    b.wait("v", *aw_cnt[ci - 2])
                if ci == 0 and tt == 0:
                    b.wait("v", b.s_dv, dv_wbc)
                b.op("v", "tensor_mul", actb[ci % 2][:, tt * 512:(tt + 1) * 512],
                     t1b[pp][:, :], wbc[:, e * T + tt * 512:e * T + tt * 512 + 512],
                     inc=b.s_dv)
                dv_f1[idx] = b.s_dv.n
            b.wait("y", b.s_dv, dv_f1[ci * 4 + 3])
            aw_sem = b.s_aw0 if ci % 2 == 0 else b.s_aw1
            b.dma("y", aw_sem, actsb[ci * 128:(ci + 1) * 128, :],
                  actb[ci % 2][:, :])
            aw_cnt[ci] = (aw_sem, aw_sem.n)
            nci = ci + 2
            if nci < E * NCK:
                b.wait("g", b.s_pe, pe_f1[ci * 4 + 3])
                w1_loaded.append(load_w1col(nci % 2, nci // NCK, nci % NCK))
    N_AW0, N_AW1 = b.s_aw0.n, b.s_aw1.n

    b.barrier()
    f1.close()

    late = ExitStack()
    st2y = sb("st2y", [1, T], F32, late)
    st2q = sb("st2q", [1, T], F32, late)
    f2 = ExitStack()
    w2blk = [sb(f"w2blk{i}", [128, 96 * 384], BF16, f2) for i in range(2)]
    actt = [sb(f"actt{i}", [128, 4 * 512], BF16, f2) for i in range(3)]
    x1r = sb("x1r", [128, 3 * T], BF16, f2)
    y2t = [sb(f"y2t{i}", [128, 512], BF16, f2) for i in range(2)]
    sqf = [sb(f"sqf{i}", [128, 512], BF16, f2) for i in range(2)]
    b2b = sb("b2b", [E, D], BF16, f2)
    moe = [f2.enter_context(nc.psum_tensor(f"moe{i}", [128, 512], F32))
           for i in range(6)]
    stc = f2.enter_context(nc.psum_tensor("stc", [128, 512], F32))
    std_ = f2.enter_context(nc.psum_tensor("std_", [128, 512], F32))

    b.wait("g", b.s_pe, b.s_pe.n)
    b.wait("g", b.s_aw0, N_AW0)
    b.wait("g", b.s_aw1, N_AW1)
    b.wait("g", b.s_cst, b.s_cst.n)
    b.dma("g", b.s_cst, b2b[:, :], b2_)
    NCST_F2 = b.s_cst.n // 16

    blocks = []
    o0 = 0
    while o0 < NCK:
        ocb = min(3, NCK - o0)
        blocks.append((o0, ocb))
        o0 += ocb
    NBLK = len(blocks)

    def load_w2blk(par, o0, ocb):
        sem = b.s_w20 if par == 0 else b.s_w21
        b.wait("g", sem, sem.n)
        for e in range(E):
            if ocb == 3:
                for cg in range(8):
                    src = fm3(w2[e, cg * 512:(cg + 1) * 512,
                                 o0 * 128:(o0 + ocb) * 128])
                    b.dma("g", sem,
                          sb3(w2blk[par][:, (e * NCK + cg * 4) * 384:
                                         (e * NCK + cg * 4 + 4) * 384], 384), src)
            else:
                for c in range(NCK):
                    src = w2[e, c * 128:(c + 1) * 128, o0 * 128:(o0 + ocb) * 128]
                    b.dma("g", sem,
                          w2blk[par][:, (e * NCK + c) * 384:
                                     (e * NCK + c) * 384 + ocb * 128], src)
        return (sem, sem.n)

    w2_loaded = [load_w2blk(0, *blocks[0])]
    if NBLK > 1:
        w2_loaded.append(load_w2blk(1, *blocks[1]))

    pe_f2_grp = {}
    dv_y2 = {}
    pe_st2 = {}
    dv_st2 = 0
    yw_cnt = {}
    par_free = {0: 0, 1: 0}
    pe_blk = [0] * NBLK

    for bi_, (o0, ocb) in enumerate(blocks):
        par = bi_ % 2
        b.wait("t", *w2_loaded[bi_])
        if bi_ == 0:
            b.wait("s", b.s_x1w, 16 * N_X1W)
            b.wait("s", b.s_aw0, N_AW0)
            b.wait("s", b.s_aw1, N_AW1)
        else:
            b.wait("s", b.s_dv, dv_y2[(bi_ - 1, NTT2 - 1, blocks[bi_ - 1][1] - 1)])
        x1r_cnt = b.dma("s", b.s_x1r,
                        sb3(x1r[:, 0:ocb * T], T),
                        fm3(x1b[o0 * 128:(o0 + ocb) * 128, :]))
        if bi_ == 0:
            b.wait("t", b.s_cst, 16 * NCST_F2)
            b.wait("y", b.s_aw0, N_AW0)
            b.wait("y", b.s_aw1, N_AW1)
        for tt in range(NTT2):
            tpar = tt % 2
            tf0 = tt * 512
            for kg in range(24):
                ring = kg % 3
                prev_key = None
                if kg >= 3:
                    prev_key = (bi_, tt, kg - 3)
                elif (bi_, tt) != (0, 0):
                    ptt = (bi_, tt - 1) if tt > 0 else (bi_ - 1, NTT2 - 1)
                    prev_key = (ptt[0], ptt[1], 21 + ring)
                if prev_key is not None:
                    b.wait("y", b.s_pe, pe_f2_grp[prev_key])
                src = fm3(actsb[kg * 512:(kg + 1) * 512, tf0:tf0 + 512])
                ar_sem = (b.s_ar0, b.s_ar1, b.s_ar2)[ring]
                b.dma("y", ar_sem, sb3(actt[ring][:, :], 512), src)
                b.wait("t", ar_sem, ar_sem.n)
                if kg == 0 and par_free[tpar]:
                    b.wait("t", b.s_dv, par_free[tpar])
                for kc in range(4):
                    k = kg * 4 + kc
                    for oc in range(ocb):
                        b.mm(moe[tpar * 3 + oc][:, :],
                             w2blk[par][:, k * 384 + oc * 128:k * 384 + oc * 128 + 128],
                             actt[ring][:, kc * 512:(kc + 1) * 512],
                             start=(k == 0), stop=False,
                             inc=b.s_pe if (kc == 3 and oc == ocb - 1) else None)
                pe_f2_grp[(bi_, tt, kg)] = b.s_pe.n
            for oc in range(ocb):
                b.mm(moe[tpar * 3 + oc][:, :],
                     b2b[0:E, (o0 + oc) * 128:(o0 + oc + 1) * 128],
                     w3b[0:E, tf0:tf0 + 512], start=False, stop=True, inc=b.s_pe)
                pe_moe = b.s_pe.n
                b.wait("v", b.s_pe, pe_moe)
                b.wait("v", b.s_x1r, 16 * x1r_cnt)
                prev = (bi_, tt - 1, oc) if tt > 0 else (
                    (bi_ - 1, NTT2 - 1, oc) if bi_ > 0 else None)
                pkey = None
                if oc >= 2:
                    pkey = (bi_, tt, oc - 2)
                elif prev is not None:
                    pocb = blocks[prev[0]][1]
                    j = pocb - 1 if (pocb - 1) % 2 == oc % 2 else pocb - 2
                    if j >= 0:
                        pkey = (prev[0], prev[1], j)
                if pkey is not None and pkey in dv_y2:
                    b.wait("v", *yw_cnt[pkey])
                    b.wait("v", b.s_pe, pe_st2[pkey])
                b.op("v", "tensor_tensor", y2t[oc % 2][:, :],
                     moe[tpar * 3 + oc][:, :],
                     x1r[:, oc * T + tf0:oc * T + tf0 + 512], ALU.add, inc=b.s_dv)
                dv_y2[(bi_, tt, oc)] = b.s_dv.n
                b.wait("s", b.s_dv, dv_y2[(bi_, tt, oc)])
                yw_sem = b.s_y2w0 if oc % 2 == 0 else b.s_y2w1
                b.dma("s", yw_sem,
                      y2b[(o0 + oc) * 128:(o0 + oc + 1) * 128, tf0:tf0 + 512],
                      y2t[oc % 2][:, :])
                yw_cnt[(bi_, tt, oc)] = (yw_sem, yw_sem.n)
                if pkey is not None and pkey in pe_st2:
                    b.wait("s", b.s_pe, pe_st2[pkey])
                b.op("s", "activation", sqf[oc % 2][:, :], y2t[oc % 2][:, :],
                     AF.Square, inc=b.s_ac)
                ac_s2 = b.s_ac.n
                b.wait("t", b.s_ac, ac_s2)
                if dv_st2:
                    b.wait("t", b.s_dv, dv_st2)
                b.mm(stc[0:1, :], onesc_bf[:, 0:1], y2t[oc % 2][:, :],
                     start=True, stop=True)
                b.mm(std_[0:1, :], onesc_bf[:, 0:1], sqf[oc % 2][:, :],
                     start=True, stop=True, inc=b.s_pe)
                pe_st2[(bi_, tt, oc)] = b.s_pe.n
                b.wait("v", b.s_pe, pe_st2[(bi_, tt, oc)])
                if dv_st2:
                    b.wait("v", b.s_dv, dv_st2)
                first = (bi_ == 0 and oc == 0)
                if first:
                    b.op("v", "tensor_copy", st2y[0:1, tf0:tf0 + 512], stc[0:1, :])
                    b.op("v", "tensor_copy", st2q[0:1, tf0:tf0 + 512], std_[0:1, :],
                         inc=b.s_dv)
                else:
                    b.op("v", "tensor_tensor", st2y[0:1, tf0:tf0 + 512], stc[0:1, :],
                         st2y[0:1, tf0:tf0 + 512], ALU.add)
                    b.op("v", "tensor_tensor", st2q[0:1, tf0:tf0 + 512], std_[0:1, :],
                         st2q[0:1, tf0:tf0 + 512], ALU.add, inc=b.s_dv)
                dv_st2 = b.s_dv.n
            par_free[tpar] = dv_y2[(bi_, tt, ocb - 1)]
        pe_blk[bi_] = b.s_pe.n
        if bi_ + 2 < NBLK:
            b.wait("g", b.s_pe, pe_blk[bi_])
            w2_loaded.append(load_w2blk((bi_ + 2) % 2, *blocks[bi_ + 2]))
    N_Y2W0, N_Y2W1 = b.s_y2w0.n, b.s_y2w1.n

    b.barrier()
    f2.close()

    f3 = ExitStack()
    y2s = sb("y2s", [128, NCK * TB], BF16, f3)
    o32 = [sb(f"o32_{i}", [128, TB], F32, f3) for i in range(2)]
    a2r = sb("a2r", [1, T], F32, f3)
    b2r = sb("b2r", [1, T], F32, f3)
    tmpr = sb("tmpr", [1, T], F32, f3)
    bca = f3.enter_context(nc.psum_tensor("bca", [128, TB], F32))
    bcb = f3.enter_context(nc.psum_tensor("bcb", [128, TB], F32))

    b.wait("v", b.s_dv, dv_st2)
    b.op("v", "tensor_scalar_mul", tmpr[0:1, :], st2y[0:1, :], 1.0 / D, inc=b.s_dv)
    b.wait("s", b.s_dv, b.s_dv.n)
    b.op("s", "square", a2r[0:1, :], tmpr[0:1, :], inc=b.s_ac)
    b.op("v", "tensor_scalar_mul", b2r[0:1, :], st2q[0:1, :], 1.0 / D, inc=b.s_dv)
    b.wait("v", b.s_ac, b.s_ac.n)
    b.wait("v", b.s_dv, b.s_dv.n)
    b.op("v", "tensor_sub", b2r[0:1, :], b2r[0:1, :], a2r[0:1, :], inc=b.s_dv)
    b.wait("v", b.s_dv, b.s_dv.n)
    b.op("v", "tensor_scalar_add", b2r[0:1, :], b2r[0:1, :], EPS_LN, inc=b.s_dv)
    b.wait("s", b.s_dv, b.s_dv.n)
    b.op("s", "activation", b2r[0:1, :], b2r[0:1, :], AF.Ln, inc=b.s_ac)
    b.wait("s", b.s_ac, b.s_ac.n)
    b.op("s", "activation", a2r[0:1, :], b2r[0:1, :], AF.Exp, scale=-0.5, inc=b.s_ac)
    b.wait("v", b.s_ac, b.s_ac.n)
    b.op("v", "tensor_mul", b2r[0:1, :], tmpr[0:1, :], a2r[0:1, :], inc=b.s_dv)
    b.wait("v", b.s_dv, b.s_dv.n)
    b.op("v", "tensor_scalar_mul", b2r[0:1, :], b2r[0:1, :], -1.0, inc=b.s_dv)
    dv_rows2 = b.s_dv.n

    n_ow = 0
    ow_cnt = {}
    dv_o = {}
    for sup in range(NSUP):
        t0 = sup * TB
        for qq in range(4):
            if sup == 0 and qq == 0:
                b.wait("y", b.s_y2w0, N_Y2W0)
                b.wait("y", b.s_y2w1, N_Y2W1)
            if sup > 0 and qq == 0:
                b.wait("y", b.s_ow0, b.s_ow0.n)
                b.wait("y", b.s_ow1, b.s_ow1.n)
            src = fm3(y2b[qq * 1024:(qq + 1) * 1024, t0:t0 + TB])
            ny2r = b.dma("y", b.s_y2r, sb3(y2s[:, qq * 8 * TB:(qq + 1) * 8 * TB], TB), src)
        b.wait("t", b.s_y2r, 16 * ny2r)
        b.wait("t", b.s_dv, dv_rows2)
        if sup > 0:
            b.wait("t", b.s_dv, dv_o[(sup - 1, NCK - 1)])
        for i in range(2):
            b.mm(bca[:, i * 512:(i + 1) * 512], ones_f32[0:1, 0:128],
                 a2r[0:1, t0 + i * 512:t0 + i * 512 + 512],
                 start=True, stop=True)
            b.mm(bcb[:, i * 512:(i + 1) * 512], ones_f32[0:1, 0:128],
                 b2r[0:1, t0 + i * 512:t0 + i * 512 + 512],
                 start=True, stop=True, inc=b.s_pe if i == 1 else None)
        pe_bc2 = b.s_pe.n
        b.wait("v", b.s_pe, pe_bc2)
        for oc in range(NCK):
            if oc >= 2:
                b.wait("v", *ow_cnt[(sup, oc - 2)])
            elif sup > 0:
                b.wait("v", *ow_cnt[(sup - 1, NCK - 2 + oc)])
            for i in range(2):
                seg = y2s[:, oc * TB + i * 512:oc * TB + i * 512 + 512]
                dst = o32[oc % 2][:, i * 512:i * 512 + 512]
                b.op("v", "tensor_tensor", dst, bca[:, i * 512:(i + 1) * 512],
                     seg, ALU.mult, inc=b.s_dv)
                b.wait("v", b.s_dv, b.s_dv.n)
                b.op("v", "tensor_tensor", dst, bcb[:, i * 512:(i + 1) * 512],
                     dst, ALU.add, inc=b.s_dv if i == 1 else None)
            dv_o[(sup, oc)] = b.s_dv.n
            b.wait("s", b.s_dv, dv_o[(sup, oc)])
            ow_sem = b.s_ow0 if oc % 2 == 0 else b.s_ow1
            b.dma("s", ow_sem, outT[oc * 128:(oc + 1) * 128, t0:t0 + TB],
                  o32[oc % 2][:, :])
            ow_cnt[(sup, oc)] = (ow_sem, ow_sem.n)

    b.wait("y", b.s_ow0, b.s_ow0.n)
    b.wait("y", b.s_ow1, b.s_ow1.n)
    b.barrier()
    f3.close()
    late.close()

    blk = ctx.enter_context(nc.Block())

    def runner(ops_list):
        def run(eng):
            for f in ops_list:
                f(eng)
        return run

    blk.tensor(runner(b.ops["t"]))
    blk.vector(runner(b.ops["v"]))
    blk.scalar(runner(b.ops["s"]))
    blk.gpsimd(runner(b.ops["g"]))
    blk.sync(runner(b.ops["y"]))

    b.check_deadlock()
    ctx.close()
    return nc


_NC_CACHE = None


def _get_nc():
    global _NC_CACHE
    if _NC_CACHE is None:
        _NC_CACHE = build_program()
    return _NC_CACHE


def kernel(**inputs):
    x = np.asarray(inputs["L0_attn_hidden_states"], dtype=np.float32)
    Wq = np.asarray(inputs["L0_attn_Wq"], dtype=np.float32)
    Wk = np.asarray(inputs["L0_attn_Wk"], dtype=np.float32)
    Wv = np.asarray(inputs["L0_attn_Wv"], dtype=np.float32)
    Wo = np.asarray(inputs["L0_attn_Wo"], dtype=np.float32)
    bq = np.asarray(inputs["L0_attn_bq"], dtype=np.float32)
    bk = np.asarray(inputs["L0_attn_bk"], dtype=np.float32)
    bv = np.asarray(inputs["L0_attn_bv"], dtype=np.float32)
    bo = np.asarray(inputs["L0_attn_bo"], dtype=np.float32)
    gate_w = np.asarray(inputs["L0_ffn_gate_w"], dtype=np.float32)
    gate_b = np.asarray(inputs["L0_ffn_gate_b"], dtype=np.float32)
    w1_ = np.asarray(inputs["L0_ffn_w1"], dtype=np.float32)
    b1_ = np.asarray(inputs["L0_ffn_b1"], dtype=np.float32)
    w2_ = np.asarray(inputs["L0_ffn_w2"], dtype=np.float32)
    b2v = np.asarray(inputs["L0_ffn_b2"], dtype=np.float32)

    shared = dict(
        wqT=np.ascontiguousarray(Wq.T),
        wkT=np.ascontiguousarray(Wk.T),
        wvT=np.ascontiguousarray(Wv.T),
        woT=np.ascontiguousarray(Wo.T),
        w1=np.ascontiguousarray(w1_),
        w2=np.ascontiguousarray(w2_),
        bqc=np.ascontiguousarray(bq.reshape(NCK, 128).T),
        bk_r=bk.reshape(1, DK),
        bv_r=bv.reshape(1, DK),
        boc=np.ascontiguousarray(bo.reshape(NCK, 128).T),
        b1r=np.ascontiguousarray(
            b1_.reshape(E, NCK, 128).transpose(2, 0, 1).reshape(128, E * NCK)),
        b2_=np.ascontiguousarray(b2v),
        gwr=np.ascontiguousarray(
            gate_w[:D].reshape(NCK, 128, E).transpose(1, 0, 2).reshape(128, NCK * E)),
        gwlast=np.ascontiguousarray(gate_w[D:D + 1]),
        gateb=gate_b.reshape(1, E),
        ident=np.eye(128, dtype=np.float32),
    )
    in_maps = []
    for c in range(8):
        bidx, s = divmod(c, 4)
        sl = x[bidx, s * T:(s + 1) * T, :]
        vis = np.zeros((1, T), dtype=np.float32)
        lo = s * T
        n_vis = min(max(VIS - lo, 0), T)
        vis[0, :n_vis] = 1.0
        m = dict(shared)
        m["x0T"] = np.ascontiguousarray(sl.T)
        m["vism"] = vis
        m["g0s"] = np.array([[1.0 if s == 0 else 0.0]], dtype=np.float32)
        in_maps.append(m)

    nc = _get_nc()
    res = run_bass_kernel_spmd(nc, in_maps, core_ids=list(range(8)))
    out = np.empty((2, SEQ, D), dtype=np.float32)
    for c in range(8):
        bidx, s = divmod(c, 4)
        out[bidx, s * T:(s + 1) * T, :] = res.results[c]["outT"].T
    return out


# version 03
# speedup vs baseline: 1.0110x; 1.0110x over previous
import numpy as np
from contextlib import ExitStack

import concourse.bass as bass
import concourse.mybir as mybir
from concourse.bass_utils import run_bass_kernel_spmd

F32 = mybir.dt.float32
BF16 = mybir.dt.bfloat16
AF = mybir.ActivationFunctionType
ALU = mybir.AluOpType
AX = mybir.AxisListType

D = 4096
NCK = 32
DK = 1024
NG = 8
NH = 32
T = 2048
TB = 1024
NSUP = T // TB
NTC = T // 128
E = 3
VIS = 2457
SEQ = 8192
N_VIS = VIS * D
N_TXT = (SEQ - VIS) * D
EPS_LN = 1e-5
NTT2 = T // 512
ENGS = ("t", "v", "s", "g", "y")


class Sem:
    def __init__(self, nc, ctx, name):
        self.h = ctx.enter_context(nc.semaphore(name))
        self.n = 0

    def plan(self, k=1):
        self.n += k
        return self.n


class Bld:
    def __init__(self, nc, ctx):
        self.nc = nc
        self.ops = {e: [] for e in ENGS}
        self.trace = {e: [] for e in ENGS}
        for n in ("s_cst", "s_csy", "s_wkv", "s_xa0", "s_xa1", "s_xa2",
                  "s_wq0", "s_wq1", "s_wq2", "s_xt",
                  "s_w10", "s_w11", "s_w12", "s_w20", "s_w21",
                  "s_x1w", "s_x1r", "s_aw0", "s_aw1", "s_ar0", "s_ar1", "s_ar2",
                  "s_y2w0", "s_y2w1", "s_y2r", "s_ow0", "s_ow1",
                  "s_ccd", "s_cc1", "s_cc2",
                  "s_pe", "s_dv", "s_ac", "s_gp", "s_ph"):
            setattr(self, n, Sem(nc, ctx, n))

    def wait(self, eng, sem, val):
        if val <= 0:
            return
        assert val <= sem.n, f"wait {val} > planned {sem.n}"
        self.trace[eng].append(("w", id(sem), val))
        self.ops[eng].append(lambda e, s=sem.h, v=val: e.wait_ge(s, v))

    def _inc(self, eng, sem, k):
        sem.plan(k)
        self.trace[eng].append(("i", id(sem), k))

    def mm(self, out_ap, lhsT_ap, rhs_ap, start, stop, inc=None):
        if inc is not None:
            self._inc("t", inc, 1)
            self.ops["t"].append(
                lambda e, o=out_ap, l=lhsT_ap, r=rhs_ap, a=start, bb=stop, s=inc.h:
                e.matmul(o, l, r, start=a, stop=bb).then_inc(s, 1))
        else:
            self.ops["t"].append(
                lambda e, o=out_ap, l=lhsT_ap, r=rhs_ap, a=start, bb=stop:
                e.matmul(o, l, r, start=a, stop=bb))
        return self.s_pe.n

    def transp(self, out_ap, in_ap, id_ap, inc=None):
        if inc is not None:
            self._inc("t", inc, 1)
            self.ops["t"].append(lambda e, o=out_ap, i=in_ap, d=id_ap, s=inc.h:
                                 e.transpose(o, i, d).then_inc(s, 1))
        else:
            self.ops["t"].append(lambda e, o=out_ap, i=in_ap, d=id_ap:
                                 e.transpose(o, i, d))

    def op(self, eng, meth, *args, inc=None, **kw):
        if inc is not None:
            self._inc(eng, inc, 1)
            self.ops[eng].append(lambda e, m=meth, a=args, k=kw, s=inc.h:
                                 getattr(e, m)(*a, **k).then_inc(s, 1))
        else:
            self.ops[eng].append(lambda e, m=meth, a=args, k=kw:
                                 getattr(e, m)(*a, **k))

    def dma(self, qeng, sem, out_ap, in_ap, **kw):
        self._inc(qeng, sem, 16)
        self.ops[qeng].append(lambda e, o=out_ap, i=in_ap, s=sem.h, k=kw:
                              e.dma_start(out=o, in_=i, **k).then_inc(s, 16))
        return sem.n // 16

    def cc(self, in_ap, out_ap, sem):
        self._inc("g", sem, 1)
        self.ops["g"].append(lambda e, i=in_ap, o=out_ap, s=sem.h:
                             e.collective_compute(
                                 "AllReduce", ALU.add,
                                 replica_groups=[[0, 1, 2, 3], [4, 5, 6, 7]],
                                 ins=[i], outs=[o]).then_inc(s, 1))

    def barrier(self):
        bb = self.barr
        self.wait("v", self.s_gp, self.barr_ready)
        self.wait("s", self.s_gp, self.barr_ready)
        self._inc("v", self.s_ph, 1)
        self.ops["v"].append(lambda en, o=bb, s=self.s_ph.h:
                             en.tensor_copy(o[0:1, 0:1], o[0:1, 1:2]).then_inc(s, 1))
        self._inc("s", self.s_ph, 1)
        self.ops["s"].append(lambda en, o=bb, s=self.s_ph.h:
                             en.copy(o[0:1, 2:3], o[0:1, 3:4]).then_inc(s, 1))
        self._inc("g", self.s_ph, 1)
        self.ops["g"].append(lambda en, o=bb, s=self.s_ph.h:
                             en.memset(o[0:1, 4:5], 0.0).then_inc(s, 1))
        for e in ("t", "y"):
            self._inc(e, self.s_ph, 1)
            self.ops[e].append(lambda en, s=self.s_ph.h: en.sem_inc(s, 1))
        tgt = self.s_ph.n
        for e in ENGS:
            self.wait(e, self.s_ph, tgt)

    def check_deadlock(self):
        pcs = {e: 0 for e in ENGS}
        vals = {}
        progress = True
        while progress:
            progress = False
            for e in ENGS:
                tr = self.trace[e]
                while pcs[e] < len(tr):
                    kind, sem, v = tr[pcs[e]]
                    if kind == "i":
                        vals[sem] = vals.get(sem, 0) + v
                        pcs[e] += 1
                        progress = True
                    elif vals.get(sem, 0) >= v:
                        pcs[e] += 1
                        progress = True
                    else:
                        break
        stuck = {e: (pcs[e], len(self.trace[e])) for e in ENGS
                 if pcs[e] < len(self.trace[e])}
        if stuck:
            raise RuntimeError(f"schedule deadlock: {stuck}")



def fm3(dram_slice, p=128):
    return dram_slice.rearrange("(c p) t -> p c t", p=p)


def sb3(sbuf_slice, w):
    return sbuf_slice.rearrange("p (c t) -> p c t", t=w)

def sap(handle, offset, pattern):
    return bass.AP(handle[:, :].tensor, offset, pattern)


def build_program():
    nc = bass.Bass("TRN2", target_bir_lowering=False)
    ctx = ExitStack()
    b = Bld(nc, ctx)

    def din(name, shape):
        return nc.dram_tensor(name, shape, F32, kind="ExternalInput").ap()

    x0T = din("x0T", [D, T])
    wqT = din("wqT", [D, D])
    wkT = din("wkT", [D, DK])
    wvT = din("wvT", [D, DK])
    woT = din("woT", [D, D])
    w1 = din("w1", [E, D, D])
    w2 = din("w2", [E, D, D])
    bqc_in = din("bqc", [128, NCK])
    bk_r = din("bk_r", [1, DK])
    bv_r = din("bv_r", [1, DK])
    boc_in = din("boc", [128, NCK])
    b1r = din("b1r", [128, E * NCK])
    b2_ = din("b2_", [E, D])
    gwr = din("gwr", [128, E * NCK])
    gwlast = din("gwlast", [1, E])
    gateb = din("gateb", [1, E])
    vism_in = din("vism", [1, T])
    g0s_in = din("g0s", [1, 1])
    ident_in = din("ident", [128, 128])

    outT = nc.dram_tensor("outT", [D, T], F32, kind="ExternalOutput").ap()

    x1b = nc.dram_tensor("x1b", [D, T], BF16, kind="Internal").ap()
    actsb = nc.dram_tensor("actsb", [E * D, T], BF16, kind="Internal").ap()
    y2b = nc.dram_tensor("y2b", [D, T], BF16, kind="Internal").ap()
    cc1_in = nc.dram_tensor("cc1_in", [128, 1032], F32, kind="Internal").ap()
    cc1_out = nc.dram_tensor("cc1_out", [128, 1032], F32, kind="Internal").ap()
    cc2_in = nc.dram_tensor("cc2_in", [1, 8], F32, kind="Internal").ap()
    cc2_out = nc.dram_tensor("cc2_out", [1, 8], F32, kind="Internal").ap()
    scr16 = nc.dram_tensor("scr16", [2, 16, 128], F32, kind="Internal").ap()
    w0d = nc.dram_tensor("w0d", [1, E * T], BF16, kind="Internal").ap()

    def sb(name, shape, dt, st=ctx):
        return st.enter_context(nc.sbuf_tensor(name, shape, dt))

    barr = sb("barr", [1, 8], F32)
    b.barr = barr
    b.op("g", "memset", barr[:, :], 0.0, inc=b.s_gp)
    b.barr_ready = b.s_gp.n
    ones_bfr = sb("ones_bfr", [1, 512], BF16)
    ones_f32 = sb("ones_f32", [1, 512], F32)
    onesc_bf = sb("onesc_bf", [128, 1], BF16)
    b1pp = sb("b1pp", [128, E * NCK], F32)
    gwb = sb("gwb", [128, E * NCK], BF16)
    gwl = sb("gwl", [1, E], F32)
    gbb = sb("gbb", [1, E], F32)
    g0s = sb("g0s_s", [1, 1], F32)
    idf = sb("idf", [128, 128], F32)
    mb_sb = sb("mb_sb", [128, NG * 128], BF16)
    recip = sb("recip", [128, NG], F32)
    g_pre = sb("g_pre", [128, NTC * E], F32)
    sacc = sb("sacc", [1, 8], F32)
    red2 = sb("red2", [1, 8], F32)
    small = sb("small", [1, 32], F32)
    m01 = sb("m01", [1, 8], F32)
    w3b = sb("w3b", [E, T], BF16)

    b.op("g", "memset", ones_bfr[:, :], 1.0)
    b.op("g", "memset", onesc_bf[:, :], 1.0)
    b.op("g", "memset", ones_f32[:, :], 1.0)
    b.op("g", "memset", sacc[:, :], 0.0, inc=b.s_gp)
    GP0 = b.s_gp.n
    b.dma("g", b.s_cst, gwb[:, :], gwr)
    for dst, src in ((b1pp, b1r), (gwl, gwlast), (gbb, gateb), (g0s, g0s_in),
                     (idf, ident_in)):
        b.dma("y", b.s_csy, dst[:, :], src)
    NCSTY = b.s_csy.n // 16

    pa = ExitStack()
    wk_sb = sb("wk_sb", [128, NCK * DK], BF16, pa)
    wv_sb = sb("wv_sb", [128, NCK * DK], BF16, pa)
    XAW = 256
    xa = [sb(f"xa{i}", [128, NCK * XAW], BF16, pa) for i in range(2)]
    kt = [sb(f"kt{i}", [128, DK], BF16, pa) for i in range(2)]
    vt = [sb(f"vt{i}", [128, NG * 129], BF16, pa) for i in range(2)]
    ph1 = [sb(f"ph1_{i}", [128, DK], BF16, pa) for i in range(2)]
    ph2 = [sb(f"ph2_{i}", [128, DK], BF16, pa) for i in range(2)]
    msb = sb("msb", [128, 1032], F32, pa)
    bkb = sb("bkb", [1, DK], BF16, pa)
    bvb = sb("bvb", [1, DK], BF16, pa)
    kp = pa.enter_context(nc.psum_tensor("kp", [128, DK], F32))
    vp = pa.enter_context(nc.psum_tensor("vp", [128, DK], F32))
    mps = [pa.enter_context(nc.psum_tensor(f"mps{i}", [128, 258], F32)) for i in range(4)]

    b.wait("g", b.s_cst, b.s_cst.n)
    b.dma("g", b.s_cst, bkb[:, :], bk_r)
    b.dma("g", b.s_cst, bvb[:, :], bv_r)
    NCST_A = b.s_cst.n // 16
    for i in range(2):
        b.op("g", "memset", vt[i][:, :], 1.0, inc=b.s_gp if i == 1 else None)
    GPV = b.s_gp.n

    for c in range(NCK):
        b.dma("g", b.s_wkv, wk_sb[:, c * DK:(c + 1) * DK], wkT[c * 128:(c + 1) * 128, :])
        b.dma("g", b.s_wkv, wv_sb[:, c * DK:(c + 1) * DK], wvT[c * 128:(c + 1) * 128, :])
    NWKV = b.s_wkv.n // 16

    def load_xa(tile, buf):
        sem = (b.s_xa0, b.s_xa1)[tile % 2]
        b.wait("g", sem, sem.n)
        t0 = tile * XAW
        for q in range(4):
            src = fm3(x0T[q * 1024:(q + 1) * 1024, t0:t0 + XAW])
            b.dma("g", sem, sb3(buf[:, q * 8 * XAW:(q + 1) * 8 * XAW], XAW), src)
        return (sem, sem.n)

    NTA = T // XAW
    SUBS = XAW // 128
    pe_proj = [0] * NTC
    dv_k = [0] * NTC
    ac_v = [0] * NTC
    ac_exp = [0] * NTC
    pe_m = [0] * NTC

    xa_cnt = [load_xa(0, xa[0])]
    for tile in range(NTA):
        for j in range(SUBS):
            q = tile * SUBS + j
            bi = q % 2
            if q == 0:
                b.wait("t", b.s_wkv, 16 * NWKV)
                b.wait("t", b.s_cst, 16 * NCST_A)
                b.wait("t", b.s_gp, GPV)
            if j == 0:
                b.wait("t", *xa_cnt[tile])
            if q >= 1:
                b.wait("t", b.s_dv, dv_k[q - 1])
                b.wait("t", b.s_ac, ac_v[q - 1])
            lx = xa[tile % 2]
            toff = j * 128
            for dst_p, wsb, brow in ((kp, wk_sb, bkb), (vp, wv_sb, bvb)):
                for fh in range(2):
                    for c in range(NCK):
                        b.mm(dst_p[:, fh * 512:(fh + 1) * 512],
                             lx[:, c * XAW + toff:c * XAW + toff + 128],
                             wsb[:, c * DK + fh * 512:c * DK + fh * 512 + 512],
                             start=(c == 0), stop=False)
                    b.mm(dst_p[:, fh * 512:(fh + 1) * 512], ones_bfr[0:1, 0:128],
                         brow[0:1, fh * 512:fh * 512 + 512], start=False, stop=True,
                         inc=b.s_pe if (dst_p is vp and fh == 1) else None)
            pe_proj[q] = b.s_pe.n
            b.wait("v", b.s_pe, pe_proj[q])
            if q >= 2:
                b.wait("v", b.s_ac, ac_exp[q - 2])
                b.wait("v", b.s_pe, pe_m[q - 2])
            b.op("v", "tensor_scalar_min", ph1[bi][:, :], kp[:, :], 0.0, inc=b.s_dv)
            b.wait("s", b.s_dv, b.s_dv.n)
            b.op("s", "activation", ph2[bi][:, :], ph1[bi][:, :], AF.Exp, inc=b.s_ac)
            ac_exp[q] = b.s_ac.n
            b.wait("v", b.s_ac, ac_exp[q])
            b.op("v", "scalar_tensor_tensor", kt[bi][:, :], kp[:, :], 0.0,
                 ph2[bi][:, :], ALU.max, ALU.add, inc=b.s_dv)
            dv_k[q] = b.s_dv.n
            b.wait("s", b.s_pe, pe_proj[q])
            if q >= 2:
                b.wait("s", b.s_pe, pe_m[q - 2])
            for g in range(NG):
                b.op("s", "copy", vt[bi][:, g * 129:g * 129 + 128],
                     vp[:, g * 128:(g + 1) * 128],
                     inc=b.s_ac if g == NG - 1 else None)
            ac_v[q] = b.s_ac.n
            b.wait("t", b.s_dv, dv_k[q])
            b.wait("t", b.s_ac, ac_v[q])
            for g in range(NG):
                bank, half = divmod(g, 2)
                b.mm(mps[bank][:, half * 129:half * 129 + 129],
                     kt[bi][:, g * 128:(g + 1) * 128],
                     vt[bi][:, g * 129:g * 129 + 129],
                     start=(q == 0 and half == 0),
                     stop=(q == NTC - 1 and half == 1),
                     inc=b.s_pe if g == NG - 1 else None)
            pe_m[q] = b.s_pe.n
        nxt = tile + 1
        if nxt < NTA:
            if nxt >= 2:
                b.wait("g", b.s_pe, pe_m[(nxt - 1) * SUBS - 1])
            xa_cnt.append(load_xa(nxt, xa[nxt % 2]))

    b.wait("v", b.s_pe, pe_m[NTC - 1])
    for bk in range(4):
        b.op("v", "tensor_copy", msb[:, bk * 258:(bk + 1) * 258], mps[bk][:, :],
             inc=b.s_dv if bk == 3 else None)
    b.wait("g", b.s_dv, b.s_dv.n)
    n1 = b.dma("g", b.s_ccd, cc1_in, msb[:, :])
    b.wait("g", b.s_ccd, 16 * n1)
    b.cc(cc1_in, cc1_out, b.s_cc1)
    b.wait("g", b.s_cc1, b.s_cc1.n)
    n2 = b.dma("g", b.s_ccd, msb[:, :], cc1_out)
    b.wait("v", b.s_ccd, 16 * n2)
    ksum_view = sap(msb, 128, [[1032, 128], [129, NG]])
    b.op("v", "tensor_scalar_add", recip[:, :], ksum_view, 1e-10, inc=b.s_dv)
    b.wait("v", b.s_dv, b.s_dv.n)
    b.op("v", "reciprocal", recip[:, :], recip[:, :], inc=b.s_dv)
    for g in range(NG):
        src = sap(msb, 258 * (g // 2) + 129 * (g % 2), [[1032, 128], [1, 128]])
        b.op("v", "tensor_copy", mb_sb[:, g * 128:(g + 1) * 128], src,
             inc=b.s_dv if g == NG - 1 else None)
    dv_mready = b.s_dv.n

    b.barrier()
    pa.close()

    pb = ExitStack()
    pbb = ExitStack()
    stby = sb("stby", [1, TB], F32, pb)
    stbq = sb("stbq", [1, TB], F32, pb)
    strg = sb("strg", [1, T], F32, pb)
    vmrow = sb("vmrow", [1, T], F32, pb)
    rows = sb("rows", [1, 2 * TB], F32, pb)
    bqc = sb("bqc_s", [128, NCK], F32, pb)
    boc = sb("boc_s", [128, NCK], F32, pb)
    qp = [pb.enter_context(nc.psum_tensor(f"qp{i}", [128, 512], F32)) for i in range(2)]
    app = pb.enter_context(nc.psum_tensor("app", [128, 512], F32))
    opp = [pb.enter_context(nc.psum_tensor(f"opp{i}", [128, 512], F32)) for i in range(2)]
    sta = pb.enter_context(nc.psum_tensor("sta", [128, 512], F32))
    stb = pb.enter_context(nc.psum_tensor("stb", [128, 512], F32))
    xt = sb("xt", [128, NCK * TB], BF16, pbb)
    at = sb("at", [128, NH * TB], BF16, pbb)
    wcol = [sb(f"wcol{i}", [128, NCK * 128], BF16, pbb) for i in range(3)]
    qsb = sb("qsb", [128, 512], BF16, pbb)
    bh1 = sb("bh1", [128, 512], BF16, pbb)
    bh2 = sb("bh2", [128, 512], BF16, pbb)
    sqs = [sb(f"sqs{i}", [128, TB], BF16, pbb) for i in range(2)]

    b.dma("y", b.s_csy, bqc[:, :], bqc_in)
    b.dma("y", b.s_csy, boc[:, :], boc_in)
    b.dma("y", b.s_csy, vmrow[:, :], vism_in)
    NCSTY_B = b.s_csy.n // 16

    def load_wcol(par, dst, wsrc, col):
        sem = (b.s_wq0, b.s_wq1, b.s_wq2)[par]
        b.wait("g", sem, sem.n)
        for qq in range(4):
            src = fm3(wsrc[qq * 1024:(qq + 1) * 1024, col * 128:(col + 1) * 128])
            b.dma("g", sem, sb3(dst[:, qq * 8 * 128:(qq + 1) * 8 * 128], 128), src)
        return (sem, sem.n)

    def load_xtile(sup):
        b.wait("g", b.s_xt, b.s_xt.n)
        t0 = sup * TB
        for qq in range(4):
            src = fm3(x0T[qq * 1024:(qq + 1) * 1024, t0:t0 + TB])
            b.dma("g", b.s_xt, sb3(xt[:, qq * 8 * TB:(qq + 1) * 8 * TB], TB), src)
        return b.s_xt.n // 16

    n_x1w = 0
    xt_cnt = load_xtile(0)
    prev_sup_last_pe = 0
    dv_gate_done = 0
    for sup in range(NSUP):
        t0 = sup * TB
        if sup > 0:
            b.wait("g", b.s_pe, prev_sup_last_pe)
        wsrcs = [(wqT, c) for c in range(NCK)] + [(woT, c) for c in range(NCK)]
        wq_loaded = [load_wcol(gi % 3, wcol[gi % 3], *wsrcs[gi]) for gi in range(3)]

        pe_head = [0] * NH
        dv_q = [[0, 0] for _ in range(NH)]
        ac_attn = [[0, 0] for _ in range(NH)]
        pe_oproj = [0] * NCK
        dv_y = [[0, 0] for _ in range(NCK)]
        ac_sq = [0] * NCK
        dv_stat = [0] * NCK
        pe_statmm = [0] * NCK

        dvq_prev = {0: 0, 1: 0}
        ac_prev_attn = 0
        pe_prev_attn = 0
        for h in range(NH):
            g = h // 4
            if h == 0:
                b.wait("t", b.s_xt, 16 * xt_cnt)
                b.wait("t", b.s_csy, 16 * NCSTY_B)
            b.wait("t", *wq_loaded[h])
            wch = wcol[h % 3]
            for i in range(2):
                if dvq_prev[i]:
                    b.wait("t", b.s_dv, dvq_prev[i])
                for c in range(NCK):
                    b.mm(qp[i][:, :],
                         wch[:, c * 128:(c + 1) * 128],
                         xt[:, c * TB + i * 512:c * TB + i * 512 + 512],
                         start=(c == 0), stop=(c == NCK - 1),
                         inc=b.s_pe if c == NCK - 1 else None)
                pe_q = b.s_pe.n
                b.wait("v", b.s_pe, pe_q)
                b.wait("v", b.s_ac, b.s_ac.n)
                b.op("v", "tensor_scalar", bh1[:, :], qp[i][:, :],
                     bqc[:, h:h + 1], 0.0, ALU.add, ALU.min, inc=b.s_dv)
                b.wait("s", b.s_dv, b.s_dv.n)
                b.op("s", "activation", bh2[:, :], bh1[:, :], AF.Exp, inc=b.s_ac)
                b.wait("v", b.s_ac, b.s_ac.n)
                b.op("v", "tensor_scalar", bh1[:, :], qp[i][:, :],
                     bqc[:, h:h + 1], 0.0, ALU.add, ALU.max, inc=b.s_dv)
                b.wait("v", b.s_dv, b.s_dv.n)
                if pe_prev_attn:
                    b.wait("v", b.s_pe, pe_prev_attn)
                b.op("v", "tensor_add", qsb[:, :], bh1[:, :], bh2[:, :], inc=b.s_dv)
                dv_q[h][i] = b.s_dv.n
                dvq_prev[i] = b.s_dv.n
                b.wait("t", b.s_dv, dv_q[h][i])
                if sup == 0 and h == 0 and i == 0:
                    b.wait("t", b.s_dv, dv_mready)
                if ac_prev_attn:
                    b.wait("t", b.s_ac, ac_prev_attn)
                b.mm(app[:, :], mb_sb[:, g * 128:(g + 1) * 128], qsb[:, :],
                     start=True, stop=True, inc=b.s_pe)
                pe_prev_attn = b.s_pe.n
                b.wait("s", b.s_pe, pe_prev_attn)
                b.op("s", "activation", at[:, h * TB + i * 512:h * TB + i * 512 + 512],
                     app[:, :], AF.Copy, scale=recip[:, g:g + 1], inc=b.s_ac)
                ac_attn[h][i] = b.s_ac.n
                ac_prev_attn = b.s_ac.n
            pe_head[h] = b.s_pe.n
            gi_next = h + 3
            if gi_next < 64:
                b.wait("g", b.s_pe, pe_head[h])
                wq_loaded.append(load_wcol(gi_next % 3, wcol[gi_next % 3],
                                           *wsrcs[gi_next]))

        def emit_stats(so):
            b.wait("t", b.s_dv, dv_y[so][1])
            b.wait("t", b.s_ac, ac_sq[so])
            if so >= 1:
                b.wait("t", b.s_dv, dv_stat[so - 1])
            if so >= 1:
                b.wait("v", b.s_dv, dv_stat[so - 1])
            for i in range(2):
                if i == 1:
                    b.wait("t", b.s_dv, b.s_dv.n)
                b.mm(sta[0:1, :], onesc_bf[:, 0:1],
                     xt[:, so * TB + i * 512:so * TB + i * 512 + 512],
                     start=True, stop=True)
                b.mm(stb[0:1, :], onesc_bf[:, 0:1],
                     sqs[so % 2][:, i * 512:(i + 1) * 512],
                     start=True, stop=True, inc=b.s_pe)
                pe_st = b.s_pe.n
                pe_statmm[so] = pe_st
                b.wait("v", b.s_pe, pe_st)
                if so == 0:
                    b.op("v", "tensor_copy", stby[0:1, i * 512:(i + 1) * 512],
                         sta[0:1, :])
                    b.op("v", "tensor_copy", stbq[0:1, i * 512:(i + 1) * 512],
                         stb[0:1, :], inc=b.s_dv)
                else:
                    b.op("v", "tensor_tensor", stby[0:1, i * 512:(i + 1) * 512],
                         sta[0:1, :], stby[0:1, i * 512:(i + 1) * 512], ALU.add)
                    b.op("v", "tensor_tensor", stbq[0:1, i * 512:(i + 1) * 512],
                         stb[0:1, :], stbq[0:1, i * 512:(i + 1) * 512], ALU.add,
                         inc=b.s_dv)
            dv_stat[so] = b.s_dv.n

        for oc in range(NCK):
            gi = 32 + oc
            b.wait("t", *wq_loaded[gi])
            if oc == 0:
                b.wait("t", b.s_ac, ac_attn[NH - 1][1])
            wcg = wcol[gi % 3]
            for i in range(2):
                if oc >= 1:
                    b.wait("t", b.s_dv, dv_y[oc - 1][i])
                for c in range(NCK):
                    b.mm(opp[i][:, :],
                         wcg[:, c * 128:(c + 1) * 128],
                         at[:, c * TB + i * 512:c * TB + i * 512 + 512],
                         start=(c == 0), stop=(c == NCK - 1),
                         inc=b.s_pe if c == NCK - 1 else None)
                pe_o = b.s_pe.n
                b.wait("v", b.s_pe, pe_o)
                b.op("v", "scalar_tensor_tensor",
                     xt[:, oc * TB + i * 512:oc * TB + i * 512 + 512],
                     opp[i][:, :], boc[:, oc:oc + 1],
                     xt[:, oc * TB + i * 512:oc * TB + i * 512 + 512],
                     ALU.add, ALU.add, inc=b.s_dv)
                dv_y[oc][i] = b.s_dv.n
            pe_oproj[oc] = b.s_pe.n
            b.wait("s", b.s_dv, dv_y[oc][1])
            if oc >= 2:
                b.wait("s", b.s_pe, pe_statmm[oc - 2])
            b.op("s", "activation", sqs[oc % 2][:, 0:512],
                 xt[:, oc * TB:oc * TB + 512], AF.Square)
            b.op("s", "activation", sqs[oc % 2][:, 512:1024],
                 xt[:, oc * TB + 512:oc * TB + 1024], AF.Square, inc=b.s_ac)
            ac_sq[oc] = b.s_ac.n
            if gi + 3 < 64:
                b.wait("g", b.s_pe, pe_oproj[oc])
                wq_loaded.append(load_wcol((gi + 3) % 3, wcol[(gi + 3) % 3],
                                           *wsrcs[gi + 3]))
            if oc >= 1:
                emit_stats(oc - 1)
        emit_stats(NCK - 1)

        b.wait("v", b.s_dv, dv_stat[NCK - 1])
        b.op("v", "tensor_scalar_mul", stby[0:1, :], stby[0:1, :], 1.0 / D,
             inc=b.s_dv)
        b.wait("s", b.s_dv, b.s_dv.n)
        b.op("s", "square", rows[0:1, 0:TB], stby[0:1, :], inc=b.s_ac)
        b.op("v", "tensor_scalar_mul", stbq[0:1, :], stbq[0:1, :], 1.0 / D,
             inc=b.s_dv)
        b.wait("v", b.s_ac, b.s_ac.n)
        b.wait("v", b.s_dv, b.s_dv.n)
        b.op("v", "tensor_sub", stbq[0:1, :], stbq[0:1, :], rows[0:1, 0:TB],
             inc=b.s_dv)
        b.wait("v", b.s_dv, b.s_dv.n)
        b.op("v", "tensor_scalar_add", stbq[0:1, :], stbq[0:1, :], EPS_LN,
             inc=b.s_dv)
        b.wait("s", b.s_dv, b.s_dv.n)
        b.op("s", "activation", stbq[0:1, :], stbq[0:1, :], AF.Ln, inc=b.s_ac)
        b.wait("s", b.s_ac, b.s_ac.n)
        b.op("s", "activation", rows[0:1, 0:TB], stbq[0:1, :], AF.Exp, scale=-0.5,
             inc=b.s_ac)
        b.wait("v", b.s_ac, b.s_ac.n)
        b.op("v", "tensor_mul", rows[0:1, TB:2 * TB], stby[0:1, :], rows[0:1, 0:TB],
             inc=b.s_dv)
        b.wait("v", b.s_dv, b.s_dv.n)
        b.op("v", "tensor_scalar_mul", rows[0:1, TB:2 * TB], rows[0:1, TB:2 * TB],
             -1.0, inc=b.s_dv)
        dv_rows = b.s_dv.n

        dv_p2 = [[0] * NCK, [0] * NCK]
        pe_strg = [[0] * NCK, [0] * NCK]
        dv_strg_prev = 0
        for i in range(2):
            b.wait("t", b.s_dv, dv_rows)
            if i == 1:
                b.wait("t", b.s_dv, dv_p2[0][NCK - 1])
            else:
                b.wait("t", b.s_dv, dvq_prev[0])
                b.wait("t", b.s_dv, dvq_prev[1])
            b.mm(qp[0][:, :], ones_f32[0:1, 0:128],
                 rows[0:1, i * 512:(i + 1) * 512], start=True, stop=True)
            b.mm(qp[1][:, :], ones_f32[0:1, 0:128],
                 rows[0:1, TB + i * 512:TB + (i + 1) * 512],
                 start=True, stop=True, inc=b.s_pe)
            pe_bc = b.s_pe.n
            b.wait("v", b.s_pe, pe_bc)
            for oc in range(NCK):
                seg = xt[:, oc * TB + i * 512:oc * TB + i * 512 + 512]
                b.op("v", "tensor_tensor", seg, qp[0][:, :], seg, ALU.mult,
                     inc=b.s_dv)
                b.wait("v", b.s_dv, b.s_dv.n)
                b.op("v", "tensor_tensor", seg, qp[1][:, :], seg, ALU.add, inc=b.s_dv)
                dv_p2[i][oc] = b.s_dv.n
                b.wait("y", b.s_dv, dv_p2[i][oc])
                n_x1w = b.dma("y", b.s_x1w,
                              x1b[oc * 128:(oc + 1) * 128,
                                  t0 + i * 512:t0 + i * 512 + 512], seg)
                b.wait("s", b.s_dv, dv_p2[i][oc])
                if oc >= 2:
                    b.wait("s", b.s_pe, pe_strg[i][oc - 2])
                b.op("s", "activation", sqs[oc % 2][:, 0:512], seg, AF.Abs, inc=b.s_ac)
                ac_ab = b.s_ac.n
                b.wait("t", b.s_ac, ac_ab)
                if dv_strg_prev:
                    b.wait("t", b.s_dv, dv_strg_prev)
                b.mm(sta[0:1, :], onesc_bf[:, 0:1], sqs[oc % 2][:, 0:512],
                     start=True, stop=True, inc=b.s_pe)
                pe_strg[i][oc] = b.s_pe.n
                b.wait("v", b.s_pe, pe_strg[i][oc])
                if dv_strg_prev:
                    b.wait("v", b.s_dv, dv_strg_prev)
                dst = strg[0:1, t0 + i * 512:t0 + i * 512 + 512]
                if oc == 0:
                    b.op("v", "tensor_copy", dst, sta[0:1, :], inc=b.s_dv)
                else:
                    b.op("v", "tensor_tensor", dst, sta[0:1, :], dst, ALU.add,
                         inc=b.s_dv)
                dv_strg_prev = b.s_dv.n

        b.wait("t", b.s_dv, dv_p2[1][NCK - 1])
        dv_g_prev = 0
        for tc in range(TB // 128):
            tg = sup * (TB // 128) + tc
            if dv_g_prev:
                b.wait("t", b.s_dv, dv_g_prev)
            for c in range(NCK):
                b.mm(app[:, 0:E], xt[:, c * TB + tc * 128:c * TB + tc * 128 + 128],
                     gwb[:, c * E:(c + 1) * E], start=(c == 0), stop=(c == NCK - 1),
                     inc=b.s_pe if c == NCK - 1 else None)
            pe_g = b.s_pe.n
            b.wait("v", b.s_pe, pe_g)
            b.op("v", "tensor_copy", g_pre[:, tg * E:(tg + 1) * E], app[:, 0:E],
                 inc=b.s_dv)
            dv_g_prev = b.s_dv.n
        dv_gate_done = b.s_dv.n
        prev_sup_last_pe = b.s_pe.n

        if sup + 1 < NSUP:
            b.wait("g", b.s_dv, dv_gate_done)
            b.wait("g", b.s_x1w, 16 * n_x1w)
            xt_cnt = load_xtile(sup + 1)

    N_X1W = b.s_x1w.n // 16
    b.wait("y", b.s_x1w, 16 * N_X1W)
    b.barrier()
    pbb.close()

    b.wait("v", b.s_csy, 16 * NCSTY_B)
    b.wait("v", b.s_dv, dv_strg_prev)
    b.op("v", "tensor_mul", vmrow[:, :], vmrow[:, :], strg[:, :], inc=b.s_dv)
    b.wait("v", b.s_dv, b.s_dv.n)
    b.op("v", "reduce_sum", sacc[0:1, 0:1], vmrow[0:1, :], axis=AX.X)
    b.op("v", "reduce_sum", sacc[0:1, 1:2], strg[0:1, :], axis=AX.X, inc=b.s_dv)
    b.op("v", "tensor_scalar_mul", sacc[0:1, 2:5], g_pre[0:1, 0:E],
         g0s[0:1, 0:1], inc=b.s_dv)
    b.wait("g", b.s_dv, b.s_dv.n)
    b.wait("g", b.s_gp, GP0)
    n3 = b.dma("g", b.s_ccd, cc2_in, sacc[:, :])
    b.wait("g", b.s_ccd, 16 * n3)
    b.cc(cc2_in, cc2_out, b.s_cc2)
    b.wait("g", b.s_cc2, b.s_cc2.n)
    n4 = b.dma("g", b.s_ccd, red2[:, :], cc2_out)
    b.wait("v", b.s_ccd, 16 * n4)
    b.op("v", "tensor_sub", small[0:1, 0:1], red2[0:1, 1:2], red2[0:1, 0:1])
    b.op("v", "tensor_scalar_mul", small[0:1, 1:2], red2[0:1, 0:1], 1.0 / N_VIS,
         inc=b.s_dv)
    b.wait("v", b.s_dv, b.s_dv.n)
    b.op("v", "tensor_scalar_mul", small[0:1, 0:1], small[0:1, 0:1], 1.0 / N_TXT)
    b.op("v", "tensor_scalar_add", small[0:1, 1:2], small[0:1, 1:2], 1e-8, inc=b.s_dv)
    b.wait("v", b.s_dv, b.s_dv.n)
    b.op("v", "reciprocal", small[0:1, 1:2], small[0:1, 1:2], inc=b.s_dv)
    b.wait("v", b.s_dv, b.s_dv.n)
    b.op("v", "tensor_mul", small[0:1, 2:3], small[0:1, 0:1], small[0:1, 1:2],
         inc=b.s_dv)
    b.wait("v", b.s_dv, b.s_dv.n)
    b.op("v", "tensor_scalar_mul", small[0:1, 4:7], gwl[0:1, :], small[0:1, 2:3],
         inc=b.s_dv)
    b.wait("v", b.s_dv, b.s_dv.n)
    b.op("v", "tensor_add", small[0:1, 4:7], small[0:1, 4:7], gbb[0:1, :], inc=b.s_dv)
    b.wait("v", b.s_dv, b.s_dv.n)
    b.op("v", "tensor_add", small[0:1, 8:11], red2[0:1, 2:5], small[0:1, 4:7],
         inc=b.s_dv)
    b.wait("v", b.s_dv, b.s_dv.n)
    a0, a1, a2 = small[0:1, 8:9], small[0:1, 9:10], small[0:1, 10:11]
    b.op("v", "tensor_tensor", small[0:1, 12:13], a0, a1, ALU.is_ge)
    b.op("v", "tensor_tensor", small[0:1, 13:14], a0, a2, ALU.is_ge)
    b.op("v", "tensor_tensor", small[0:1, 14:15], a1, a2, ALU.is_ge, inc=b.s_dv)
    b.wait("v", b.s_dv, b.s_dv.n)
    b.op("v", "tensor_add", small[0:1, 16:17], small[0:1, 12:13], small[0:1, 13:14])
    b.op("v", "tensor_sub", small[0:1, 17:18], small[0:1, 12:13], small[0:1, 14:15])
    b.op("v", "tensor_add", small[0:1, 18:19], small[0:1, 13:14], small[0:1, 14:15],
         inc=b.s_dv)
    b.wait("v", b.s_dv, b.s_dv.n)
    b.op("v", "tensor_scalar", small[0:1, 16:17], small[0:1, 16:17], -1.0, 2.0,
         ALU.mult, ALU.add)
    b.op("v", "tensor_scalar_add", small[0:1, 17:18], small[0:1, 17:18], 1.0,
         inc=b.s_dv)
    b.wait("v", b.s_dv, b.s_dv.n)
    b.op("v", "tensor_single_scalar", m01[0:1, 0:3], small[0:1, 16:19], 0.0,
         ALU.is_equal)
    b.op("v", "tensor_single_scalar", m01[0:1, 3:6], small[0:1, 16:19], 1.0,
         ALU.is_equal, inc=b.s_dv)

    gm = sb("gm", [128, 6 * 16], F32, pb)
    lg = sb("lg", [128, NTC * E], F32, pb)
    b.wait("t", b.s_dv, b.s_dv.n)
    b.mm(app[:, 0:E], ones_f32[0:1, 0:128], small[0:1, 4:7], start=True, stop=True,
         inc=b.s_pe)
    pe_add3 = b.s_pe.n
    b.wait("v", b.s_pe, pe_add3)
    for tc in range(NTC):
        b.op("v", "tensor_tensor", lg[:, tc * E:(tc + 1) * E],
             app[:, 0:E], g_pre[:, tc * E:(tc + 1) * E], ALU.add,
             inc=b.s_dv if tc == NTC - 1 else None)
    b.wait("s", b.s_dv, b.s_dv.n)
    b.op("s", "activation", lg[:, :], lg[:, :], AF.Exp, inc=b.s_ac)
    b.wait("v", b.s_ac, b.s_ac.n)
    lg3 = lg[:, :].rearrange("p (c e) -> p c e", e=E)
    b.op("v", "reduce_max", gm[:, 0:16], lg3, axis=AX.X)
    b.op("v", "tensor_reduce", gm[:, 16:32], lg3, axis=AX.X, op=ALU.min)
    b.op("v", "reduce_sum", gm[:, 32:48], lg3, axis=AX.X, inc=b.s_dv)
    b.wait("v", b.s_dv, b.s_dv.n)
    b.op("v", "tensor_sub", gm[:, 48:64], gm[:, 32:48], gm[:, 0:16], inc=b.s_dv)
    b.wait("v", b.s_dv, b.s_dv.n)
    b.op("v", "tensor_sub", gm[:, 48:64], gm[:, 48:64], gm[:, 16:32], inc=b.s_dv)
    b.wait("v", b.s_dv, b.s_dv.n)
    b.op("v", "tensor_scalar_mul", gm[:, 64:80], gm[:, 32:48], 1e-8, inc=b.s_dv)
    b.wait("v", b.s_dv, b.s_dv.n)
    b.op("v", "tensor_add", gm[:, 64:80], gm[:, 64:80], gm[:, 0:16], inc=b.s_dv)
    b.wait("v", b.s_dv, b.s_dv.n)
    b.op("v", "tensor_add", gm[:, 64:80], gm[:, 64:80], gm[:, 48:64], inc=b.s_dv)
    b.wait("v", b.s_dv, b.s_dv.n)
    b.op("v", "reciprocal", gm[:, 64:80], gm[:, 64:80], inc=b.s_dv)
    b.wait("v", b.s_dv, b.s_dv.n)
    b.op("v", "tensor_mul", gm[:, 0:16], gm[:, 0:16], gm[:, 64:80])
    b.op("v", "tensor_mul", gm[:, 48:64], gm[:, 48:64], gm[:, 64:80], inc=b.s_dv)
    dv_c = b.s_dv.n
    c0row = sb("c0row", [1, T], F32, pb)
    c1row = sb("c1row", [1, T], F32, pb)
    ctr = sb("ctr", [16, 128], F32, pb)
    b.wait("t", b.s_dv, dv_c)
    b.wait("t", b.s_csy, 16 * NCSTY_B)
    for idx, (c0_, dst_row) in enumerate(((0, c0row), (48, c1row))):
        if idx == 1:
            b.wait("t", b.s_ac, b.s_ac.n)
        b.transp(app[0:16, 0:128], gm[:, c0_:c0_ + 16], idf[:, :], inc=b.s_pe)
        b.wait("s", b.s_pe, b.s_pe.n)
        b.wait("s", b.s_ccd, 16 * (b.s_ccd.n // 16))
        b.op("s", "copy", ctr[:, :], app[0:16, 0:128], inc=b.s_ac)
        b.wait("g", b.s_ac, b.s_ac.n)
        nci = b.dma("g", b.s_ccd, scr16[idx], ctr[:, :])
        b.wait("g", b.s_ccd, 16 * nci)
        ncj = b.dma("g", b.s_ccd, dst_row[0:1, :],
                    scr16[idx].rearrange("a bb -> (a bb)")[None, :])
    b.wait("v", b.s_ccd, 16 * (b.s_ccd.n // 16))
    wbr = sb("wbr", [1, T], BF16, pb)
    dv_wrow = [0] * E
    w3cnt = [0] * E
    for e in range(E):
        if e > 0:
            b.wait("v", b.s_ccd, 16 * w3cnt[e - 1])
        for half in range(2):
            hs = slice(half * TB, (half + 1) * TB)
            if e > 0 or half > 0:
                b.wait("v", b.s_dv, b.s_dv.n)
            b.op("v", "tensor_scalar_mul", rows[0:1, 0:TB], c0row[0:1, hs],
                 m01[0:1, e:e + 1])
            b.op("v", "tensor_scalar_mul", rows[0:1, TB:2 * TB], c1row[0:1, hs],
                 m01[0:1, 3 + e:4 + e], inc=b.s_dv)
            b.wait("v", b.s_dv, b.s_dv.n)
            b.op("v", "tensor_add", rows[0:1, 0:TB], rows[0:1, 0:TB],
                 rows[0:1, TB:2 * TB], inc=b.s_dv)
            b.wait("v", b.s_dv, b.s_dv.n)
            b.op("v", "tensor_copy", wbr[0:1, hs], rows[0:1, 0:TB], inc=b.s_dv)
        dv_wrow[e] = b.s_dv.n
        b.wait("g", b.s_dv, dv_wrow[e])
        b.wait("g", b.s_ccd, b.s_ccd.n)
        b.dma("g", b.s_ccd, w3b[e:e + 1, :], wbr[0:1, :])
        w3cnt[e] = b.dma("g", b.s_ccd, w0d[0:1, e * T:(e + 1) * T], wbr[0:1, :])
    b.wait("y", b.s_ccd, 16 * w3cnt[E - 1])

    b.barrier()
    pb.close()

    f1 = ExitStack()
    w0b = sb("w0b", [1, E * T], BF16, f1)
    x1s = sb("x1s", [128, NCK * T], BF16, f1)
    wbc = sb("wbc", [128, E * T], BF16, f1)
    w1col = [sb(f"w1col{i}", [128, NCK * 128], BF16, f1) for i in range(3)]
    t1b = [sb(f"t1b{i}", [128, 512], BF16, f1) for i in range(2)]
    actb = [sb(f"actb{i}", [128, T], BF16, f1) for i in range(2)]
    h1p = [f1.enter_context(nc.psum_tensor(f"h1p{i}", [128, 512], F32))
           for i in range(2)]

    b.wait("g", b.s_x1w, 16 * N_X1W)
    b.wait("g", b.s_cst, b.s_cst.n)
    nw0 = b.dma("g", b.s_cst, w0b[:, :], w0d)
    b.wait("t", b.s_cst, 16 * nw0)
    dv_wbc = 0
    for e in range(E):
        for qq in range(T // 512):
            if dv_wbc:
                b.wait("t", b.s_dv, dv_wbc)
            b.mm(h1p[0][:, :], ones_bfr[0:1, 0:128],
                 w0b[0:1, e * T + qq * 512:e * T + (qq + 1) * 512],
                 start=True, stop=True, inc=b.s_pe)
            b.wait("v", b.s_pe, b.s_pe.n)
            b.op("v", "tensor_copy", wbc[:, e * T + qq * 512:e * T + (qq + 1) * 512],
                 h1p[0][:, :], inc=b.s_dv)
            dv_wbc = b.s_dv.n

    nx1 = 0
    for qq in range(8):
        nx1 = b.dma("y", b.s_x1r, sb3(x1s[:, qq * 4 * T:(qq + 1) * 4 * T], T),
                    fm3(x1b[qq * 512:(qq + 1) * 512, :]))

    def load_w1col(par, e, fc):
        sem = (b.s_w10, b.s_w11, b.s_w12)[par]
        b.wait("g", sem, sem.n)
        for qq in range(4):
            src = fm3(w1[e, qq * 1024:(qq + 1) * 1024, fc * 128:(fc + 1) * 128])
            b.dma("g", sem, sb3(w1col[par][:, qq * 8 * 128:(qq + 1) * 8 * 128], 128), src)
        return (sem, sem.n)

    w1_loaded = [load_w1col(0, 0, 0), load_w1col(1, 0, 1), load_w1col(2, 0, 2)]
    b.wait("s", b.s_csy, 16 * NCSTY_B)
    ACT_FN = {0: AF.Tanh, 1: AF.Gelu, 2: AF.Silu}
    ac_f1 = {}
    dv_f1 = {}
    aw_cnt = {}
    pe_f1 = {}
    for e in range(E):
        for fc in range(NCK):
            ci = e * NCK + fc
            b.wait("t", *w1_loaded[ci])
            if ci == 0:
                b.wait("t", b.s_x1r, 16 * nx1)
                b.wait("t", b.s_dv, dv_wbc)
            for tt in range(4):
                idx = ci * 4 + tt
                pp = idx % 2
                if idx >= 2:
                    b.wait("t", b.s_ac, ac_f1[idx - 2])
                for c in range(NCK):
                    b.mm(h1p[pp][:, :], w1col[ci % 3][:, c * 128:(c + 1) * 128],
                         x1s[:, c * T + tt * 512:c * T + tt * 512 + 512],
                         start=(c == 0), stop=(c == NCK - 1),
                         inc=b.s_pe if c == NCK - 1 else None)
                pe_f1[idx] = b.s_pe.n
                b.wait("s", b.s_pe, pe_f1[idx])
                if idx >= 2:
                    b.wait("s", b.s_dv, dv_f1[idx - 2])
                b.op("s", "activation", t1b[pp][:, :], h1p[pp][:, :], ACT_FN[e],
                     bias=b1pp[:, ci:ci + 1], inc=b.s_ac)
                ac_f1[idx] = b.s_ac.n
                b.wait("v", b.s_ac, ac_f1[idx])
                if ci >= 2 and tt == 0:
                    b.wait("v", *aw_cnt[ci - 2])
                if ci == 0 and tt == 0:
                    b.wait("v", b.s_dv, dv_wbc)
                b.op("v", "tensor_mul", actb[ci % 2][:, tt * 512:(tt + 1) * 512],
                     t1b[pp][:, :], wbc[:, e * T + tt * 512:e * T + tt * 512 + 512],
                     inc=b.s_dv)
                dv_f1[idx] = b.s_dv.n
            b.wait("y", b.s_dv, dv_f1[ci * 4 + 3])
            aw_sem = b.s_aw0 if ci % 2 == 0 else b.s_aw1
            b.dma("y", aw_sem, actsb[ci * 128:(ci + 1) * 128, :],
                  actb[ci % 2][:, :])
            aw_cnt[ci] = (aw_sem, aw_sem.n)
            nci = ci + 3
            if nci < E * NCK:
                b.wait("g", b.s_pe, pe_f1[ci * 4 + 3])
                w1_loaded.append(load_w1col(nci % 3, nci // NCK, nci % NCK))
    N_AW0, N_AW1 = b.s_aw0.n, b.s_aw1.n

    b.barrier()
    f1.close()

    late = ExitStack()
    st2y = sb("st2y", [1, T], F32, late)
    st2q = sb("st2q", [1, T], F32, late)
    f2 = ExitStack()
    w2blk = [sb(f"w2blk{i}", [128, 96 * 384], BF16, f2) for i in range(2)]
    actt = [sb(f"actt{i}", [128, 4 * 512], BF16, f2) for i in range(3)]
    x1r = sb("x1r", [128, 3 * T], BF16, f2)
    y2t = [sb(f"y2t{i}", [128, 512], BF16, f2) for i in range(2)]
    sqf = [sb(f"sqf{i}", [128, 512], BF16, f2) for i in range(2)]
    b2b = sb("b2b", [E, D], BF16, f2)
    moe = [f2.enter_context(nc.psum_tensor(f"moe{i}", [128, 512], F32))
           for i in range(6)]
    stc = f2.enter_context(nc.psum_tensor("stc", [128, 512], F32))
    std_ = f2.enter_context(nc.psum_tensor("std_", [128, 512], F32))

    b.wait("g", b.s_pe, b.s_pe.n)
    b.wait("g", b.s_aw0, N_AW0)
    b.wait("g", b.s_aw1, N_AW1)
    b.wait("g", b.s_cst, b.s_cst.n)
    b.dma("g", b.s_cst, b2b[:, :], b2_)
    NCST_F2 = b.s_cst.n // 16

    blocks = []
    o0 = 0
    while o0 < NCK:
        ocb = min(3, NCK - o0)
        blocks.append((o0, ocb))
        o0 += ocb
    NBLK = len(blocks)

    def load_w2blk(par, o0, ocb):
        sem = b.s_w20 if par == 0 else b.s_w21
        b.wait("g", sem, sem.n)
        for e in range(E):
            if ocb == 3:
                for cg in range(8):
                    src = fm3(w2[e, cg * 512:(cg + 1) * 512,
                                 o0 * 128:(o0 + ocb) * 128])
                    b.dma("g", sem,
                          sb3(w2blk[par][:, (e * NCK + cg * 4) * 384:
                                         (e * NCK + cg * 4 + 4) * 384], 384), src)
            else:
                for c in range(NCK):
                    src = w2[e, c * 128:(c + 1) * 128, o0 * 128:(o0 + ocb) * 128]
                    b.dma("g", sem,
                          w2blk[par][:, (e * NCK + c) * 384:
                                     (e * NCK + c) * 384 + ocb * 128], src)
        return (sem, sem.n)

    w2_loaded = [load_w2blk(0, *blocks[0])]
    if NBLK > 1:
        w2_loaded.append(load_w2blk(1, *blocks[1]))

    pe_f2_grp = {}
    dv_y2 = {}
    pe_st2 = {}
    dv_st2 = 0
    yw_cnt = {}
    par_free = {0: 0, 1: 0}
    pe_blk = [0] * NBLK

    for bi_, (o0, ocb) in enumerate(blocks):
        par = bi_ % 2
        b.wait("t", *w2_loaded[bi_])
        if bi_ == 0:
            b.wait("s", b.s_x1w, 16 * N_X1W)
            b.wait("s", b.s_aw0, N_AW0)
            b.wait("s", b.s_aw1, N_AW1)
        else:
            b.wait("s", b.s_dv, dv_y2[(bi_ - 1, NTT2 - 1, blocks[bi_ - 1][1] - 1)])
        x1r_cnt = b.dma("s", b.s_x1r,
                        sb3(x1r[:, 0:ocb * T], T),
                        fm3(x1b[o0 * 128:(o0 + ocb) * 128, :]))
        if bi_ == 0:
            b.wait("t", b.s_cst, 16 * NCST_F2)
            b.wait("y", b.s_aw0, N_AW0)
            b.wait("y", b.s_aw1, N_AW1)
        for tt in range(NTT2):
            tpar = tt % 2
            tf0 = tt * 512
            for kg in range(24):
                ring = kg % 3
                prev_key = None
                if kg >= 3:
                    prev_key = (bi_, tt, kg - 3)
                elif (bi_, tt) != (0, 0):
                    ptt = (bi_, tt - 1) if tt > 0 else (bi_ - 1, NTT2 - 1)
                    prev_key = (ptt[0], ptt[1], 21 + ring)
                if prev_key is not None:
                    b.wait("y", b.s_pe, pe_f2_grp[prev_key])
                src = fm3(actsb[kg * 512:(kg + 1) * 512, tf0:tf0 + 512])
                ar_sem = (b.s_ar0, b.s_ar1, b.s_ar2)[ring]
                b.dma("y", ar_sem, sb3(actt[ring][:, :], 512), src)
                b.wait("t", ar_sem, ar_sem.n)
                if kg == 0 and par_free[tpar]:
                    b.wait("t", b.s_dv, par_free[tpar])
                for kc in range(4):
                    k = kg * 4 + kc
                    for oc in range(ocb):
                        b.mm(moe[tpar * 3 + oc][:, :],
                             w2blk[par][:, k * 384 + oc * 128:k * 384 + oc * 128 + 128],
                             actt[ring][:, kc * 512:(kc + 1) * 512],
                             start=(k == 0), stop=False,
                             inc=b.s_pe if (kc == 3 and oc == ocb - 1) else None)
                pe_f2_grp[(bi_, tt, kg)] = b.s_pe.n
            for oc in range(ocb):
                b.mm(moe[tpar * 3 + oc][:, :],
                     b2b[0:E, (o0 + oc) * 128:(o0 + oc + 1) * 128],
                     w3b[0:E, tf0:tf0 + 512], start=False, stop=True, inc=b.s_pe)
                pe_moe = b.s_pe.n
                b.wait("v", b.s_pe, pe_moe)
                b.wait("v", b.s_x1r, 16 * x1r_cnt)
                prev = (bi_, tt - 1, oc) if tt > 0 else (
                    (bi_ - 1, NTT2 - 1, oc) if bi_ > 0 else None)
                pkey = None
                if oc >= 2:
                    pkey = (bi_, tt, oc - 2)
                elif prev is not None:
                    pocb = blocks[prev[0]][1]
                    j = pocb - 1 if (pocb - 1) % 2 == oc % 2 else pocb - 2
                    if j >= 0:
                        pkey = (prev[0], prev[1], j)
                if pkey is not None and pkey in dv_y2:
                    b.wait("v", *yw_cnt[pkey])
                    b.wait("v", b.s_pe, pe_st2[pkey])
                b.op("v", "tensor_tensor", y2t[oc % 2][:, :],
                     moe[tpar * 3 + oc][:, :],
                     x1r[:, oc * T + tf0:oc * T + tf0 + 512], ALU.add, inc=b.s_dv)
                dv_y2[(bi_, tt, oc)] = b.s_dv.n
                b.wait("s", b.s_dv, dv_y2[(bi_, tt, oc)])
                yw_sem = b.s_y2w0 if oc % 2 == 0 else b.s_y2w1
                b.dma("s", yw_sem,
                      y2b[(o0 + oc) * 128:(o0 + oc + 1) * 128, tf0:tf0 + 512],
                      y2t[oc % 2][:, :])
                yw_cnt[(bi_, tt, oc)] = (yw_sem, yw_sem.n)
                if pkey is not None and pkey in pe_st2:
                    b.wait("s", b.s_pe, pe_st2[pkey])
                b.op("s", "activation", sqf[oc % 2][:, :], y2t[oc % 2][:, :],
                     AF.Square, inc=b.s_ac)
                ac_s2 = b.s_ac.n
                b.wait("t", b.s_ac, ac_s2)
                if dv_st2:
                    b.wait("t", b.s_dv, dv_st2)
                b.mm(stc[0:1, :], onesc_bf[:, 0:1], y2t[oc % 2][:, :],
                     start=True, stop=True)
                b.mm(std_[0:1, :], onesc_bf[:, 0:1], sqf[oc % 2][:, :],
                     start=True, stop=True, inc=b.s_pe)
                pe_st2[(bi_, tt, oc)] = b.s_pe.n
                b.wait("v", b.s_pe, pe_st2[(bi_, tt, oc)])
                if dv_st2:
                    b.wait("v", b.s_dv, dv_st2)
                first = (bi_ == 0 and oc == 0)
                if first:
                    b.op("v", "tensor_copy", st2y[0:1, tf0:tf0 + 512], stc[0:1, :])
                    b.op("v", "tensor_copy", st2q[0:1, tf0:tf0 + 512], std_[0:1, :],
                         inc=b.s_dv)
                else:
                    b.op("v", "tensor_tensor", st2y[0:1, tf0:tf0 + 512], stc[0:1, :],
                         st2y[0:1, tf0:tf0 + 512], ALU.add)
                    b.op("v", "tensor_tensor", st2q[0:1, tf0:tf0 + 512], std_[0:1, :],
                         st2q[0:1, tf0:tf0 + 512], ALU.add, inc=b.s_dv)
                dv_st2 = b.s_dv.n
            par_free[tpar] = dv_y2[(bi_, tt, ocb - 1)]
        pe_blk[bi_] = b.s_pe.n
        if bi_ + 2 < NBLK:
            b.wait("g", b.s_pe, pe_blk[bi_])
            w2_loaded.append(load_w2blk((bi_ + 2) % 2, *blocks[bi_ + 2]))
    N_Y2W0, N_Y2W1 = b.s_y2w0.n, b.s_y2w1.n

    b.barrier()
    f2.close()

    f3 = ExitStack()
    y2s = sb("y2s", [128, NCK * TB], BF16, f3)
    o32 = [sb(f"o32_{i}", [128, TB], F32, f3) for i in range(2)]
    a2r = sb("a2r", [1, T], F32, f3)
    b2r = sb("b2r", [1, T], F32, f3)
    tmpr = sb("tmpr", [1, T], F32, f3)
    bca = f3.enter_context(nc.psum_tensor("bca", [128, TB], F32))
    bcb = f3.enter_context(nc.psum_tensor("bcb", [128, TB], F32))

    b.wait("v", b.s_dv, dv_st2)
    b.op("v", "tensor_scalar_mul", tmpr[0:1, :], st2y[0:1, :], 1.0 / D, inc=b.s_dv)
    b.wait("s", b.s_dv, b.s_dv.n)
    b.op("s", "square", a2r[0:1, :], tmpr[0:1, :], inc=b.s_ac)
    b.op("v", "tensor_scalar_mul", b2r[0:1, :], st2q[0:1, :], 1.0 / D, inc=b.s_dv)
    b.wait("v", b.s_ac, b.s_ac.n)
    b.wait("v", b.s_dv, b.s_dv.n)
    b.op("v", "tensor_sub", b2r[0:1, :], b2r[0:1, :], a2r[0:1, :], inc=b.s_dv)
    b.wait("v", b.s_dv, b.s_dv.n)
    b.op("v", "tensor_scalar_add", b2r[0:1, :], b2r[0:1, :], EPS_LN, inc=b.s_dv)
    b.wait("s", b.s_dv, b.s_dv.n)
    b.op("s", "activation", b2r[0:1, :], b2r[0:1, :], AF.Ln, inc=b.s_ac)
    b.wait("s", b.s_ac, b.s_ac.n)
    b.op("s", "activation", a2r[0:1, :], b2r[0:1, :], AF.Exp, scale=-0.5, inc=b.s_ac)
    b.wait("v", b.s_ac, b.s_ac.n)
    b.op("v", "tensor_mul", b2r[0:1, :], tmpr[0:1, :], a2r[0:1, :], inc=b.s_dv)
    b.wait("v", b.s_dv, b.s_dv.n)
    b.op("v", "tensor_scalar_mul", b2r[0:1, :], b2r[0:1, :], -1.0, inc=b.s_dv)
    dv_rows2 = b.s_dv.n

    n_ow = 0
    ow_cnt = {}
    dv_o = {}
    for sup in range(NSUP):
        t0 = sup * TB
        for qq in range(4):
            if sup == 0 and qq == 0:
                b.wait("y", b.s_y2w0, N_Y2W0)
                b.wait("y", b.s_y2w1, N_Y2W1)
            if sup > 0 and qq == 0:
                b.wait("y", b.s_ow0, b.s_ow0.n)
                b.wait("y", b.s_ow1, b.s_ow1.n)
            src = fm3(y2b[qq * 1024:(qq + 1) * 1024, t0:t0 + TB])
            ny2r = b.dma("y", b.s_y2r, sb3(y2s[:, qq * 8 * TB:(qq + 1) * 8 * TB], TB), src)
        b.wait("t", b.s_y2r, 16 * ny2r)
        b.wait("t", b.s_dv, dv_rows2)
        if sup > 0:
            b.wait("t", b.s_dv, dv_o[(sup - 1, NCK - 1)])
        for i in range(2):
            b.mm(bca[:, i * 512:(i + 1) * 512], ones_f32[0:1, 0:128],
                 a2r[0:1, t0 + i * 512:t0 + i * 512 + 512],
                 start=True, stop=True)
            b.mm(bcb[:, i * 512:(i + 1) * 512], ones_f32[0:1, 0:128],
                 b2r[0:1, t0 + i * 512:t0 + i * 512 + 512],
                 start=True, stop=True, inc=b.s_pe if i == 1 else None)
        pe_bc2 = b.s_pe.n
        b.wait("v", b.s_pe, pe_bc2)
        for oc in range(NCK):
            if oc >= 2:
                b.wait("v", *ow_cnt[(sup, oc - 2)])
            elif sup > 0:
                b.wait("v", *ow_cnt[(sup - 1, NCK - 2 + oc)])
            for i in range(2):
                seg = y2s[:, oc * TB + i * 512:oc * TB + i * 512 + 512]
                dst = o32[oc % 2][:, i * 512:i * 512 + 512]
                b.op("v", "tensor_tensor", dst, bca[:, i * 512:(i + 1) * 512],
                     seg, ALU.mult, inc=b.s_dv)
                b.wait("v", b.s_dv, b.s_dv.n)
                b.op("v", "tensor_tensor", dst, bcb[:, i * 512:(i + 1) * 512],
                     dst, ALU.add, inc=b.s_dv if i == 1 else None)
            dv_o[(sup, oc)] = b.s_dv.n
            b.wait("s", b.s_dv, dv_o[(sup, oc)])
            ow_sem = b.s_ow0 if oc % 2 == 0 else b.s_ow1
            b.dma("s", ow_sem, outT[oc * 128:(oc + 1) * 128, t0:t0 + TB],
                  o32[oc % 2][:, :])
            ow_cnt[(sup, oc)] = (ow_sem, ow_sem.n)

    b.wait("y", b.s_ow0, b.s_ow0.n)
    b.wait("y", b.s_ow1, b.s_ow1.n)
    b.barrier()
    f3.close()
    late.close()

    blk = ctx.enter_context(nc.Block())

    def runner(ops_list):
        def run(eng):
            for f in ops_list:
                f(eng)
        return run

    blk.tensor(runner(b.ops["t"]))
    blk.vector(runner(b.ops["v"]))
    blk.scalar(runner(b.ops["s"]))
    blk.gpsimd(runner(b.ops["g"]))
    blk.sync(runner(b.ops["y"]))

    b.check_deadlock()
    ctx.close()
    return nc


_NC_CACHE = None


def _get_nc():
    global _NC_CACHE
    if _NC_CACHE is None:
        _NC_CACHE = build_program()
    return _NC_CACHE


def kernel(**inputs):
    x = np.asarray(inputs["L0_attn_hidden_states"], dtype=np.float32)
    Wq = np.asarray(inputs["L0_attn_Wq"], dtype=np.float32)
    Wk = np.asarray(inputs["L0_attn_Wk"], dtype=np.float32)
    Wv = np.asarray(inputs["L0_attn_Wv"], dtype=np.float32)
    Wo = np.asarray(inputs["L0_attn_Wo"], dtype=np.float32)
    bq = np.asarray(inputs["L0_attn_bq"], dtype=np.float32)
    bk = np.asarray(inputs["L0_attn_bk"], dtype=np.float32)
    bv = np.asarray(inputs["L0_attn_bv"], dtype=np.float32)
    bo = np.asarray(inputs["L0_attn_bo"], dtype=np.float32)
    gate_w = np.asarray(inputs["L0_ffn_gate_w"], dtype=np.float32)
    gate_b = np.asarray(inputs["L0_ffn_gate_b"], dtype=np.float32)
    w1_ = np.asarray(inputs["L0_ffn_w1"], dtype=np.float32)
    b1_ = np.asarray(inputs["L0_ffn_b1"], dtype=np.float32)
    w2_ = np.asarray(inputs["L0_ffn_w2"], dtype=np.float32)
    b2v = np.asarray(inputs["L0_ffn_b2"], dtype=np.float32)

    shared = dict(
        wqT=np.ascontiguousarray(Wq.T),
        wkT=np.ascontiguousarray(Wk.T),
        wvT=np.ascontiguousarray(Wv.T),
        woT=np.ascontiguousarray(Wo.T),
        w1=np.ascontiguousarray(w1_),
        w2=np.ascontiguousarray(w2_),
        bqc=np.ascontiguousarray(bq.reshape(NCK, 128).T),
        bk_r=bk.reshape(1, DK),
        bv_r=bv.reshape(1, DK),
        boc=np.ascontiguousarray(bo.reshape(NCK, 128).T),
        b1r=np.ascontiguousarray(
            b1_.reshape(E, NCK, 128).transpose(2, 0, 1).reshape(128, E * NCK)),
        b2_=np.ascontiguousarray(b2v),
        gwr=np.ascontiguousarray(
            gate_w[:D].reshape(NCK, 128, E).transpose(1, 0, 2).reshape(128, NCK * E)),
        gwlast=np.ascontiguousarray(gate_w[D:D + 1]),
        gateb=gate_b.reshape(1, E),
        ident=np.eye(128, dtype=np.float32),
    )
    in_maps = []
    for c in range(8):
        bidx, s = divmod(c, 4)
        sl = x[bidx, s * T:(s + 1) * T, :]
        vis = np.zeros((1, T), dtype=np.float32)
        lo = s * T
        n_vis = min(max(VIS - lo, 0), T)
        vis[0, :n_vis] = 1.0
        m = dict(shared)
        m["x0T"] = np.ascontiguousarray(sl.T)
        m["vism"] = vis
        m["g0s"] = np.array([[1.0 if s == 0 else 0.0]], dtype=np.float32)
        in_maps.append(m)

    nc = _get_nc()
    res = run_bass_kernel_spmd(nc, in_maps, core_ids=list(range(8)))
    out = np.empty((2, SEQ, D), dtype=np.float32)
    for c in range(8):
        bidx, s = divmod(c, 4)
        out[bidx, s * T:(s + 1) * T, :] = res.results[c]["outT"].T
    return out


# version 04 (final)
# speedup vs baseline: 1.0247x; 1.0136x over previous
import numpy as np
from contextlib import ExitStack

import concourse.bass as bass
import concourse.mybir as mybir
from concourse.bass_utils import run_bass_kernel_spmd

F32 = mybir.dt.float32
BF16 = mybir.dt.bfloat16
AF = mybir.ActivationFunctionType
ALU = mybir.AluOpType
AX = mybir.AxisListType

D = 4096
NCK = 32
DK = 1024
NG = 8
NH = 32
T = 2048
TB = 1024
NSUP = T // TB
NTC = T // 128
E = 3
VIS = 2457
SEQ = 8192
N_VIS = VIS * D
N_TXT = (SEQ - VIS) * D
EPS_LN = 1e-5
NTT2 = T // 512
ENGS = ("t", "v", "s", "g", "y")


class Sem:
    def __init__(self, nc, ctx, name):
        self.h = ctx.enter_context(nc.semaphore(name))
        self.n = 0

    def plan(self, k=1):
        self.n += k
        return self.n


class Bld:
    def __init__(self, nc, ctx):
        self.nc = nc
        self.ops = {e: [] for e in ENGS}
        self.trace = {e: [] for e in ENGS}
        for n in ("s_cst", "s_csy", "s_wkv", "s_xa0", "s_xa1", "s_xa2",
                  "s_wq0", "s_wq1", "s_wq2", "s_xt",
                  "s_w10", "s_w11", "s_w12", "s_w20", "s_w21",
                  "s_x1w", "s_x1r", "s_aw0", "s_aw1", "s_ar0", "s_ar1", "s_ar2",
                  "s_y2w0", "s_y2w1", "s_y2r", "s_ow0", "s_ow1",
                  "s_ccd", "s_cc1", "s_cc2",
                  "s_pe", "s_dv", "s_ac", "s_gp", "s_ph"):
            setattr(self, n, Sem(nc, ctx, n))

    def wait(self, eng, sem, val):
        if val <= 0:
            return
        assert val <= sem.n, f"wait {val} > planned {sem.n}"
        self.trace[eng].append(("w", id(sem), val))
        self.ops[eng].append(lambda e, s=sem.h, v=val: e.wait_ge(s, v))

    def _inc(self, eng, sem, k):
        sem.plan(k)
        self.trace[eng].append(("i", id(sem), k))

    def mm(self, out_ap, lhsT_ap, rhs_ap, start, stop, inc=None):
        if inc is not None:
            self._inc("t", inc, 1)
            self.ops["t"].append(
                lambda e, o=out_ap, l=lhsT_ap, r=rhs_ap, a=start, bb=stop, s=inc.h:
                e.matmul(o, l, r, start=a, stop=bb).then_inc(s, 1))
        else:
            self.ops["t"].append(
                lambda e, o=out_ap, l=lhsT_ap, r=rhs_ap, a=start, bb=stop:
                e.matmul(o, l, r, start=a, stop=bb))
        return self.s_pe.n

    def transp(self, out_ap, in_ap, id_ap, inc=None):
        if inc is not None:
            self._inc("t", inc, 1)
            self.ops["t"].append(lambda e, o=out_ap, i=in_ap, d=id_ap, s=inc.h:
                                 e.transpose(o, i, d).then_inc(s, 1))
        else:
            self.ops["t"].append(lambda e, o=out_ap, i=in_ap, d=id_ap:
                                 e.transpose(o, i, d))

    def op(self, eng, meth, *args, inc=None, **kw):
        if inc is not None:
            self._inc(eng, inc, 1)
            self.ops[eng].append(lambda e, m=meth, a=args, k=kw, s=inc.h:
                                 getattr(e, m)(*a, **k).then_inc(s, 1))
        else:
            self.ops[eng].append(lambda e, m=meth, a=args, k=kw:
                                 getattr(e, m)(*a, **k))

    def dma(self, qeng, sem, out_ap, in_ap, **kw):
        self._inc(qeng, sem, 16)
        self.ops[qeng].append(lambda e, o=out_ap, i=in_ap, s=sem.h, k=kw:
                              e.dma_start(out=o, in_=i, **k).then_inc(s, 16))
        return sem.n // 16

    def cc(self, in_ap, out_ap, sem):
        self._inc("g", sem, 1)
        self.ops["g"].append(lambda e, i=in_ap, o=out_ap, s=sem.h:
                             e.collective_compute(
                                 "AllReduce", ALU.add,
                                 replica_groups=[[0, 1, 2, 3], [4, 5, 6, 7]],
                                 ins=[i], outs=[o]).then_inc(s, 1))

    def barrier(self):
        bb = self.barr
        self.wait("v", self.s_gp, self.barr_ready)
        self.wait("s", self.s_gp, self.barr_ready)
        self._inc("v", self.s_ph, 1)
        self.ops["v"].append(lambda en, o=bb, s=self.s_ph.h:
                             en.tensor_copy(o[0:1, 0:1], o[0:1, 1:2]).then_inc(s, 1))
        self._inc("s", self.s_ph, 1)
        self.ops["s"].append(lambda en, o=bb, s=self.s_ph.h:
                             en.copy(o[0:1, 2:3], o[0:1, 3:4]).then_inc(s, 1))
        self._inc("g", self.s_ph, 1)
        self.ops["g"].append(lambda en, o=bb, s=self.s_ph.h:
                             en.memset(o[0:1, 4:5], 0.0).then_inc(s, 1))
        for e in ("t", "y"):
            self._inc(e, self.s_ph, 1)
            self.ops[e].append(lambda en, s=self.s_ph.h: en.sem_inc(s, 1))
        tgt = self.s_ph.n
        for e in ENGS:
            self.wait(e, self.s_ph, tgt)

    def check_deadlock(self):
        pcs = {e: 0 for e in ENGS}
        vals = {}
        progress = True
        while progress:
            progress = False
            for e in ENGS:
                tr = self.trace[e]
                while pcs[e] < len(tr):
                    kind, sem, v = tr[pcs[e]]
                    if kind == "i":
                        vals[sem] = vals.get(sem, 0) + v
                        pcs[e] += 1
                        progress = True
                    elif vals.get(sem, 0) >= v:
                        pcs[e] += 1
                        progress = True
                    else:
                        break
        stuck = {e: (pcs[e], len(self.trace[e])) for e in ENGS
                 if pcs[e] < len(self.trace[e])}
        if stuck:
            raise RuntimeError(f"schedule deadlock: {stuck}")



def fm3(dram_slice, p=128):
    return dram_slice.rearrange("(c p) t -> p c t", p=p)


def sb3(sbuf_slice, w):
    return sbuf_slice.rearrange("p (c t) -> p c t", t=w)

def sap(handle, offset, pattern):
    return bass.AP(handle[:, :].tensor, offset, pattern)


def build_program():
    nc = bass.Bass("TRN2", target_bir_lowering=False)
    ctx = ExitStack()
    b = Bld(nc, ctx)

    def din(name, shape):
        return nc.dram_tensor(name, shape, F32, kind="ExternalInput").ap()

    x0T = din("x0T", [D, T])
    wqT = din("wqT", [D, D])
    wkT = din("wkT", [D, DK])
    wvT = din("wvT", [D, DK])
    woT = din("woT", [D, D])
    w1 = din("w1", [E, D, D])
    w2 = din("w2", [E, D, D])
    bqc_in = din("bqc", [128, NCK])
    bk_r = din("bk_r", [1, DK])
    bv_r = din("bv_r", [1, DK])
    boc_in = din("boc", [128, NCK])
    b1r = din("b1r", [128, E * NCK])
    b2_ = din("b2_", [E, D])
    gwr = din("gwr", [128, E * NCK])
    gwlast = din("gwlast", [1, E])
    gateb = din("gateb", [1, E])
    vism_in = din("vism", [1, T])
    g0s_in = din("g0s", [1, 1])
    ident_in = din("ident", [128, 128])

    outT = nc.dram_tensor("outT", [D, T], F32, kind="ExternalOutput").ap()

    x1b = nc.dram_tensor("x1b", [D, T], BF16, kind="Internal").ap()
    actsb = nc.dram_tensor("actsb", [E * D, T], BF16, kind="Internal").ap()
    y2b = nc.dram_tensor("y2b", [D, T], BF16, kind="Internal").ap()
    cc1_in = nc.dram_tensor("cc1_in", [128, 1032], F32, kind="Internal").ap()
    cc1_out = nc.dram_tensor("cc1_out", [128, 1032], F32, kind="Internal").ap()
    cc2_in = nc.dram_tensor("cc2_in", [1, 8], F32, kind="Internal").ap()
    cc2_out = nc.dram_tensor("cc2_out", [1, 8], F32, kind="Internal").ap()
    scr16 = nc.dram_tensor("scr16", [2, 16, 128], F32, kind="Internal").ap()
    w0d = nc.dram_tensor("w0d", [1, E * T], BF16, kind="Internal").ap()

    def sb(name, shape, dt, st=ctx):
        return st.enter_context(nc.sbuf_tensor(name, shape, dt))

    barr = sb("barr", [1, 8], F32)
    b.barr = barr
    b.op("g", "memset", barr[:, :], 0.0, inc=b.s_gp)
    b.barr_ready = b.s_gp.n
    ones_bfr = sb("ones_bfr", [1, 512], BF16)
    ones_f32 = sb("ones_f32", [1, 512], F32)
    onesc_bf = sb("onesc_bf", [128, 1], BF16)
    b1pp = sb("b1pp", [128, E * NCK], F32)
    gwb = sb("gwb", [128, E * NCK], BF16)
    gwl = sb("gwl", [1, E], F32)
    gbb = sb("gbb", [1, E], F32)
    g0s = sb("g0s_s", [1, 1], F32)
    idf = sb("idf", [128, 128], F32)
    mb_sb = sb("mb_sb", [128, NG * 128], BF16)
    recip = sb("recip", [128, NG], F32)
    g_pre = sb("g_pre", [128, NTC * E], F32)
    sacc = sb("sacc", [1, 8], F32)
    red2 = sb("red2", [1, 8], F32)
    small = sb("small", [1, 32], F32)
    m01 = sb("m01", [1, 8], F32)
    w3b = sb("w3b", [E, T], BF16)

    b.op("g", "memset", ones_bfr[:, :], 1.0)
    b.op("g", "memset", onesc_bf[:, :], 1.0)
    b.op("g", "memset", ones_f32[:, :], 1.0)
    b.op("g", "memset", sacc[:, :], 0.0, inc=b.s_gp)
    GP0 = b.s_gp.n
    b.dma("g", b.s_cst, gwb[:, :], gwr)
    for dst, src in ((b1pp, b1r), (gwl, gwlast), (gbb, gateb), (g0s, g0s_in),
                     (idf, ident_in)):
        b.dma("y", b.s_csy, dst[:, :], src)
    NCSTY = b.s_csy.n // 16

    pa = ExitStack()
    wk_sb = sb("wk_sb", [128, NCK * DK], BF16, pa)
    wv_sb = sb("wv_sb", [128, NCK * DK], BF16, pa)
    XAW = 256
    xa = [sb(f"xa{i}", [128, NCK * XAW], BF16, pa) for i in range(2)]
    kt = [sb(f"kt{i}", [128, DK], BF16, pa) for i in range(2)]
    vt = [sb(f"vt{i}", [128, NG * 129], BF16, pa) for i in range(2)]
    ph1 = [sb(f"ph1_{i}", [128, DK], BF16, pa) for i in range(2)]
    ph2 = [sb(f"ph2_{i}", [128, DK], BF16, pa) for i in range(2)]
    msb = sb("msb", [128, 1032], F32, pa)
    bkb = sb("bkb", [1, DK], BF16, pa)
    bvb = sb("bvb", [1, DK], BF16, pa)
    kp = pa.enter_context(nc.psum_tensor("kp", [128, DK], F32))
    vp = pa.enter_context(nc.psum_tensor("vp", [128, DK], F32))
    mps = [pa.enter_context(nc.psum_tensor(f"mps{i}", [128, 258], F32)) for i in range(4)]

    b.wait("g", b.s_cst, b.s_cst.n)
    b.dma("g", b.s_cst, bkb[:, :], bk_r)
    b.dma("g", b.s_cst, bvb[:, :], bv_r)
    NCST_A = b.s_cst.n // 16
    for i in range(2):
        b.op("g", "memset", vt[i][:, :], 1.0, inc=b.s_gp if i == 1 else None)
    GPV = b.s_gp.n

    for c in range(NCK):
        b.dma("g", b.s_wkv, wk_sb[:, c * DK:(c + 1) * DK], wkT[c * 128:(c + 1) * 128, :])
        b.dma("g", b.s_wkv, wv_sb[:, c * DK:(c + 1) * DK], wvT[c * 128:(c + 1) * 128, :])
    NWKV = b.s_wkv.n // 16

    def load_xa(tile, buf):
        sem = (b.s_xa0, b.s_xa1)[tile % 2]
        b.wait("g", sem, sem.n)
        t0 = tile * XAW
        for q in range(4):
            src = fm3(x0T[q * 1024:(q + 1) * 1024, t0:t0 + XAW])
            b.dma("g", sem, sb3(buf[:, q * 8 * XAW:(q + 1) * 8 * XAW], XAW), src)
        return (sem, sem.n)

    NTA = T // XAW
    SUBS = XAW // 128
    pe_proj = [0] * NTC
    dv_k = [0] * NTC
    ac_v = [0] * NTC
    ac_exp = [0] * NTC
    pe_m = [0] * NTC

    xa_cnt = [load_xa(0, xa[0])]
    for tile in range(NTA):
        for j in range(SUBS):
            q = tile * SUBS + j
            bi = q % 2
            if q == 0:
                b.wait("t", b.s_wkv, 16 * NWKV)
                b.wait("t", b.s_cst, 16 * NCST_A)
                b.wait("t", b.s_gp, GPV)
            if j == 0:
                b.wait("t", *xa_cnt[tile])
            if q >= 1:
                b.wait("t", b.s_dv, dv_k[q - 1])
                b.wait("t", b.s_ac, ac_v[q - 1])
            lx = xa[tile % 2]
            toff = j * 128
            for dst_p, wsb, brow in ((kp, wk_sb, bkb), (vp, wv_sb, bvb)):
                for fh in range(2):
                    for c in range(NCK):
                        b.mm(dst_p[:, fh * 512:(fh + 1) * 512],
                             lx[:, c * XAW + toff:c * XAW + toff + 128],
                             wsb[:, c * DK + fh * 512:c * DK + fh * 512 + 512],
                             start=(c == 0), stop=False)
                    b.mm(dst_p[:, fh * 512:(fh + 1) * 512], ones_bfr[0:1, 0:128],
                         brow[0:1, fh * 512:fh * 512 + 512], start=False, stop=True,
                         inc=b.s_pe if (dst_p is vp and fh == 1) else None)
            pe_proj[q] = b.s_pe.n
            b.wait("v", b.s_pe, pe_proj[q])
            if q >= 2:
                b.wait("v", b.s_ac, ac_exp[q - 2])
                b.wait("v", b.s_pe, pe_m[q - 2])
            b.op("v", "tensor_scalar_min", ph1[bi][:, :], kp[:, :], 0.0, inc=b.s_dv)
            b.wait("s", b.s_dv, b.s_dv.n)
            b.op("s", "activation", ph2[bi][:, :], ph1[bi][:, :], AF.Exp, inc=b.s_ac)
            ac_exp[q] = b.s_ac.n
            b.wait("v", b.s_ac, ac_exp[q])
            b.op("v", "scalar_tensor_tensor", kt[bi][:, :], kp[:, :], 0.0,
                 ph2[bi][:, :], ALU.max, ALU.add, inc=b.s_dv)
            dv_k[q] = b.s_dv.n
            b.wait("s", b.s_pe, pe_proj[q])
            if q >= 2:
                b.wait("s", b.s_pe, pe_m[q - 2])
            for g in range(NG):
                b.op("s", "copy", vt[bi][:, g * 129:g * 129 + 128],
                     vp[:, g * 128:(g + 1) * 128],
                     inc=b.s_ac if g == NG - 1 else None)
            ac_v[q] = b.s_ac.n
            b.wait("t", b.s_dv, dv_k[q])
            b.wait("t", b.s_ac, ac_v[q])
            for g in range(NG):
                bank, half = divmod(g, 2)
                b.mm(mps[bank][:, half * 129:half * 129 + 129],
                     kt[bi][:, g * 128:(g + 1) * 128],
                     vt[bi][:, g * 129:g * 129 + 129],
                     start=(q == 0 and half == 0),
                     stop=(q == NTC - 1 and half == 1),
                     inc=b.s_pe if g == NG - 1 else None)
            pe_m[q] = b.s_pe.n
        nxt = tile + 1
        if nxt < NTA:
            if nxt >= 2:
                b.wait("g", b.s_pe, pe_m[(nxt - 1) * SUBS - 1])
            xa_cnt.append(load_xa(nxt, xa[nxt % 2]))

    b.wait("v", b.s_pe, pe_m[NTC - 1])
    for bk in range(4):
        b.op("v", "tensor_copy", msb[:, bk * 258:(bk + 1) * 258], mps[bk][:, :],
             inc=b.s_dv if bk == 3 else None)
    b.wait("g", b.s_dv, b.s_dv.n)
    n1 = b.dma("g", b.s_ccd, cc1_in, msb[:, :])
    b.wait("g", b.s_ccd, 16 * n1)
    b.cc(cc1_in, cc1_out, b.s_cc1)
    b.wait("g", b.s_cc1, b.s_cc1.n)
    n2 = b.dma("g", b.s_ccd, msb[:, :], cc1_out)
    b.wait("v", b.s_ccd, 16 * n2)
    ksum_view = sap(msb, 128, [[1032, 128], [129, NG]])
    b.op("v", "tensor_scalar_add", recip[:, :], ksum_view, 1e-10, inc=b.s_dv)
    b.wait("v", b.s_dv, b.s_dv.n)
    b.op("v", "reciprocal", recip[:, :], recip[:, :], inc=b.s_dv)
    for g in range(NG):
        src = sap(msb, 258 * (g // 2) + 129 * (g % 2), [[1032, 128], [1, 128]])
        b.op("v", "tensor_copy", mb_sb[:, g * 128:(g + 1) * 128], src,
             inc=b.s_dv if g == NG - 1 else None)
    dv_mready = b.s_dv.n

    b.barrier()
    pa.close()

    pb = ExitStack()
    pbb = ExitStack()
    stby = sb("stby", [1, TB], F32, pb)
    stbq = sb("stbq", [1, TB], F32, pb)
    strg = sb("strg", [1, T], F32, pb)
    vmrow = sb("vmrow", [1, T], F32, pb)
    rows = sb("rows", [1, 2 * TB], F32, pb)
    bqc = sb("bqc_s", [128, NCK], F32, pb)
    boc = sb("boc_s", [128, NCK], F32, pb)
    qp = [pb.enter_context(nc.psum_tensor(f"qp{i}", [128, 512], F32)) for i in range(2)]
    app = pb.enter_context(nc.psum_tensor("app", [128, 512], F32))
    opp = [pb.enter_context(nc.psum_tensor(f"opp{i}", [128, 512], F32)) for i in range(2)]
    sta = pb.enter_context(nc.psum_tensor("sta", [128, 512], F32))
    stb = pb.enter_context(nc.psum_tensor("stb", [128, 512], F32))
    xt = sb("xt", [128, NCK * TB], BF16, pbb)
    at = sb("at", [128, NH * TB], BF16, pbb)
    wcol = [sb(f"wcol{i}", [128, NCK * 128], BF16, pbb) for i in range(3)]
    qsb2 = [sb(f"qsb{i}", [128, 512], BF16, pbb) for i in range(2)]
    bh1 = sb("bh1", [128, 512], BF16, pbb)
    bh2 = sb("bh2", [128, 512], BF16, pbb)
    sqs = [sb(f"sqs{i}", [128, TB], BF16, pbb) for i in range(2)]

    b.dma("y", b.s_csy, bqc[:, :], bqc_in)
    b.dma("y", b.s_csy, boc[:, :], boc_in)
    b.dma("y", b.s_csy, vmrow[:, :], vism_in)
    NCSTY_B = b.s_csy.n // 16

    def load_wcol(par, dst, wsrc, col):
        sem = (b.s_wq0, b.s_wq1, b.s_wq2)[par]
        b.wait("g", sem, sem.n)
        for qq in range(4):
            src = fm3(wsrc[qq * 1024:(qq + 1) * 1024, col * 128:(col + 1) * 128])
            b.dma("g", sem, sb3(dst[:, qq * 8 * 128:(qq + 1) * 8 * 128], 128), src)
        return (sem, sem.n)

    def load_xtile(sup):
        b.wait("g", b.s_xt, b.s_xt.n)
        t0 = sup * TB
        for qq in range(4):
            src = fm3(x0T[qq * 1024:(qq + 1) * 1024, t0:t0 + TB])
            b.dma("g", b.s_xt, sb3(xt[:, qq * 8 * TB:(qq + 1) * 8 * TB], TB), src)
        return b.s_xt.n // 16

    n_x1w = 0
    xt_cnt = load_xtile(0)
    prev_sup_last_pe = 0
    dv_gate_done = 0
    for sup in range(NSUP):
        t0 = sup * TB
        if sup > 0:
            b.wait("g", b.s_pe, prev_sup_last_pe)
        wsrcs = [(wqT, c) for c in range(NCK)] + [(woT, c) for c in range(NCK)]
        wq_loaded = [load_wcol(gi % 3, wcol[gi % 3], *wsrcs[gi]) for gi in range(3)]

        pe_head = [0] * NH
        dv_q = [[0, 0] for _ in range(NH)]
        ac_attn = [[0, 0] for _ in range(NH)]
        pe_oproj = [0] * NCK
        dv_y = [[0, 0] for _ in range(NCK)]
        ac_sq = [0] * NCK
        dv_stat = [0] * NCK
        pe_statmm = [0] * NCK

        dvq_prev = {0: 0, 1: 0}
        ac_prev_attn = 0
        pe_attn_prev = {0: 0, 1: 0}
        attn_first = [sup == 0]

        def emit_attn(hh, ii):
            nonlocal ac_prev_attn
            gg = hh // 4
            b.wait("t", b.s_dv, dv_q[hh][ii])
            if attn_first[0]:
                b.wait("t", b.s_dv, dv_mready)
                attn_first[0] = False
            if ac_prev_attn:
                b.wait("t", b.s_ac, ac_prev_attn)
            b.mm(app[:, :], mb_sb[:, gg * 128:(gg + 1) * 128], qsb2[ii][:, :],
                 start=True, stop=True, inc=b.s_pe)
            pe_attn_prev[ii] = b.s_pe.n
            b.wait("s", b.s_pe, b.s_pe.n)
            b.op("s", "activation",
                 at[:, hh * TB + ii * 512:hh * TB + ii * 512 + 512],
                 app[:, :], AF.Copy, scale=recip[:, gg:gg + 1], inc=b.s_ac)
            ac_attn[hh][ii] = b.s_ac.n
            ac_prev_attn = b.s_ac.n

        for h in range(NH):
            g = h // 4
            if h == 0:
                b.wait("t", b.s_xt, 16 * xt_cnt)
                b.wait("t", b.s_csy, 16 * NCSTY_B)
            b.wait("t", *wq_loaded[h])
            wch = wcol[h % 3]
            for i in range(2):
                if dvq_prev[i]:
                    b.wait("t", b.s_dv, dvq_prev[i])
                for c in range(NCK):
                    b.mm(qp[i][:, :],
                         wch[:, c * 128:(c + 1) * 128],
                         xt[:, c * TB + i * 512:c * TB + i * 512 + 512],
                         start=(c == 0), stop=(c == NCK - 1),
                         inc=b.s_pe if c == NCK - 1 else None)
                pe_q = b.s_pe.n
                b.wait("v", b.s_pe, pe_q)
                b.wait("v", b.s_ac, b.s_ac.n)
                b.op("v", "tensor_scalar", bh1[:, :], qp[i][:, :],
                     bqc[:, h:h + 1], 0.0, ALU.add, ALU.min, inc=b.s_dv)
                b.wait("s", b.s_dv, b.s_dv.n)
                b.op("s", "activation", bh2[:, :], bh1[:, :], AF.Exp, inc=b.s_ac)
                b.wait("v", b.s_ac, b.s_ac.n)
                b.op("v", "tensor_scalar", bh1[:, :], qp[i][:, :],
                     bqc[:, h:h + 1], 0.0, ALU.add, ALU.max, inc=b.s_dv)
                b.wait("v", b.s_dv, b.s_dv.n)
                if pe_attn_prev[i]:
                    b.wait("v", b.s_pe, pe_attn_prev[i])
                b.op("v", "tensor_add", qsb2[i][:, :], bh1[:, :], bh2[:, :],
                     inc=b.s_dv)
                dv_q[h][i] = b.s_dv.n
                dvq_prev[i] = b.s_dv.n
                if i == 0 and h > 0:
                    emit_attn(h - 1, 1)
            emit_attn(h, 0)
            pe_head[h] = b.s_pe.n
            gi_next = h + 3
            if gi_next < 64:
                b.wait("g", b.s_pe, pe_head[h])
                wq_loaded.append(load_wcol(gi_next % 3, wcol[gi_next % 3],
                                           *wsrcs[gi_next]))
        emit_attn(NH - 1, 1)

        def emit_stats(so):
            b.wait("t", b.s_dv, dv_y[so][1])
            b.wait("t", b.s_ac, ac_sq[so])
            if so >= 1:
                b.wait("t", b.s_dv, dv_stat[so - 1])
            if so >= 1:
                b.wait("v", b.s_dv, dv_stat[so - 1])
            for i in range(2):
                if i == 1:
                    b.wait("t", b.s_dv, b.s_dv.n)
                b.mm(sta[0:1, :], onesc_bf[:, 0:1],
                     xt[:, so * TB + i * 512:so * TB + i * 512 + 512],
                     start=True, stop=True)
                b.mm(stb[0:1, :], onesc_bf[:, 0:1],
                     sqs[so % 2][:, i * 512:(i + 1) * 512],
                     start=True, stop=True, inc=b.s_pe)
                pe_st = b.s_pe.n
                pe_statmm[so] = pe_st
                b.wait("v", b.s_pe, pe_st)
                if so == 0:
                    b.op("v", "tensor_copy", stby[0:1, i * 512:(i + 1) * 512],
                         sta[0:1, :])
                    b.op("v", "tensor_copy", stbq[0:1, i * 512:(i + 1) * 512],
                         stb[0:1, :], inc=b.s_dv)
                else:
                    b.op("v", "tensor_tensor", stby[0:1, i * 512:(i + 1) * 512],
                         sta[0:1, :], stby[0:1, i * 512:(i + 1) * 512], ALU.add)
                    b.op("v", "tensor_tensor", stbq[0:1, i * 512:(i + 1) * 512],
                         stb[0:1, :], stbq[0:1, i * 512:(i + 1) * 512], ALU.add,
                         inc=b.s_dv)
            dv_stat[so] = b.s_dv.n

        for oc in range(NCK):
            gi = 32 + oc
            b.wait("t", *wq_loaded[gi])
            if oc == 0:
                b.wait("t", b.s_ac, ac_attn[NH - 1][1])
            wcg = wcol[gi % 3]
            for i in range(2):
                if oc >= 1:
                    b.wait("t", b.s_dv, dv_y[oc - 1][i])
                for c in range(NCK):
                    b.mm(opp[i][:, :],
                         wcg[:, c * 128:(c + 1) * 128],
                         at[:, c * TB + i * 512:c * TB + i * 512 + 512],
                         start=(c == 0), stop=(c == NCK - 1),
                         inc=b.s_pe if c == NCK - 1 else None)
                pe_o = b.s_pe.n
                b.wait("v", b.s_pe, pe_o)
                b.op("v", "scalar_tensor_tensor",
                     xt[:, oc * TB + i * 512:oc * TB + i * 512 + 512],
                     opp[i][:, :], boc[:, oc:oc + 1],
                     xt[:, oc * TB + i * 512:oc * TB + i * 512 + 512],
                     ALU.add, ALU.add, inc=b.s_dv)
                dv_y[oc][i] = b.s_dv.n
            pe_oproj[oc] = b.s_pe.n
            b.wait("s", b.s_dv, dv_y[oc][1])
            if oc >= 2:
                b.wait("s", b.s_pe, pe_statmm[oc - 2])
            b.op("s", "activation", sqs[oc % 2][:, 0:512],
                 xt[:, oc * TB:oc * TB + 512], AF.Square)
            b.op("s", "activation", sqs[oc % 2][:, 512:1024],
                 xt[:, oc * TB + 512:oc * TB + 1024], AF.Square, inc=b.s_ac)
            ac_sq[oc] = b.s_ac.n
            if gi + 3 < 64:
                b.wait("g", b.s_pe, pe_oproj[oc])
                wq_loaded.append(load_wcol((gi + 3) % 3, wcol[(gi + 3) % 3],
                                           *wsrcs[gi + 3]))
            if oc >= 1:
                emit_stats(oc - 1)
        emit_stats(NCK - 1)

        b.wait("v", b.s_dv, dv_stat[NCK - 1])
        b.op("v", "tensor_scalar_mul", stby[0:1, :], stby[0:1, :], 1.0 / D,
             inc=b.s_dv)
        b.wait("s", b.s_dv, b.s_dv.n)
        b.op("s", "square", rows[0:1, 0:TB], stby[0:1, :], inc=b.s_ac)
        b.op("v", "tensor_scalar_mul", stbq[0:1, :], stbq[0:1, :], 1.0 / D,
             inc=b.s_dv)
        b.wait("v", b.s_ac, b.s_ac.n)
        b.wait("v", b.s_dv, b.s_dv.n)
        b.op("v", "tensor_sub", stbq[0:1, :], stbq[0:1, :], rows[0:1, 0:TB],
             inc=b.s_dv)
        b.wait("v", b.s_dv, b.s_dv.n)
        b.op("v", "tensor_scalar_add", stbq[0:1, :], stbq[0:1, :], EPS_LN,
             inc=b.s_dv)
        b.wait("s", b.s_dv, b.s_dv.n)
        b.op("s", "activation", stbq[0:1, :], stbq[0:1, :], AF.Ln, inc=b.s_ac)
        b.wait("s", b.s_ac, b.s_ac.n)
        b.op("s", "activation", rows[0:1, 0:TB], stbq[0:1, :], AF.Exp, scale=-0.5,
             inc=b.s_ac)
        b.wait("v", b.s_ac, b.s_ac.n)
        b.op("v", "tensor_mul", rows[0:1, TB:2 * TB], stby[0:1, :], rows[0:1, 0:TB],
             inc=b.s_dv)
        b.wait("v", b.s_dv, b.s_dv.n)
        b.op("v", "tensor_scalar_mul", rows[0:1, TB:2 * TB], rows[0:1, TB:2 * TB],
             -1.0, inc=b.s_dv)
        dv_rows = b.s_dv.n

        dv_p2 = [[0] * NCK, [0] * NCK]
        pe_strg = [[0] * NCK, [0] * NCK]
        dv_strg_prev = 0
        for i in range(2):
            b.wait("t", b.s_dv, dv_rows)
            if i == 1:
                b.wait("t", b.s_dv, dv_p2[0][NCK - 1])
            else:
                b.wait("t", b.s_dv, dvq_prev[0])
                b.wait("t", b.s_dv, dvq_prev[1])
            b.mm(qp[0][:, :], ones_f32[0:1, 0:128],
                 rows[0:1, i * 512:(i + 1) * 512], start=True, stop=True)
            b.mm(qp[1][:, :], ones_f32[0:1, 0:128],
                 rows[0:1, TB + i * 512:TB + (i + 1) * 512],
                 start=True, stop=True, inc=b.s_pe)
            pe_bc = b.s_pe.n
            b.wait("v", b.s_pe, pe_bc)
            for oc in range(NCK):
                seg = xt[:, oc * TB + i * 512:oc * TB + i * 512 + 512]
                b.op("v", "tensor_tensor", seg, qp[0][:, :], seg, ALU.mult,
                     inc=b.s_dv)
                b.wait("v", b.s_dv, b.s_dv.n)
                b.op("v", "tensor_tensor", seg, qp[1][:, :], seg, ALU.add, inc=b.s_dv)
                dv_p2[i][oc] = b.s_dv.n
                b.wait("y", b.s_dv, dv_p2[i][oc])
                n_x1w = b.dma("y", b.s_x1w,
                              x1b[oc * 128:(oc + 1) * 128,
                                  t0 + i * 512:t0 + i * 512 + 512], seg)
                b.wait("s", b.s_dv, dv_p2[i][oc])
                if oc >= 2:
                    b.wait("s", b.s_pe, pe_strg[i][oc - 2])
                b.op("s", "activation", sqs[oc % 2][:, 0:512], seg, AF.Abs, inc=b.s_ac)
                ac_ab = b.s_ac.n
                b.wait("t", b.s_ac, ac_ab)
                if dv_strg_prev:
                    b.wait("t", b.s_dv, dv_strg_prev)
                b.mm(sta[0:1, :], onesc_bf[:, 0:1], sqs[oc % 2][:, 0:512],
                     start=True, stop=True, inc=b.s_pe)
                pe_strg[i][oc] = b.s_pe.n
                b.wait("v", b.s_pe, pe_strg[i][oc])
                if dv_strg_prev:
                    b.wait("v", b.s_dv, dv_strg_prev)
                dst = strg[0:1, t0 + i * 512:t0 + i * 512 + 512]
                if oc == 0:
                    b.op("v", "tensor_copy", dst, sta[0:1, :], inc=b.s_dv)
                else:
                    b.op("v", "tensor_tensor", dst, sta[0:1, :], dst, ALU.add,
                         inc=b.s_dv)
                dv_strg_prev = b.s_dv.n

        b.wait("t", b.s_dv, dv_p2[1][NCK - 1])
        dv_g_prev = 0
        for tc in range(TB // 128):
            tg = sup * (TB // 128) + tc
            if dv_g_prev:
                b.wait("t", b.s_dv, dv_g_prev)
            for c in range(NCK):
                b.mm(app[:, 0:E], xt[:, c * TB + tc * 128:c * TB + tc * 128 + 128],
                     gwb[:, c * E:(c + 1) * E], start=(c == 0), stop=(c == NCK - 1),
                     inc=b.s_pe if c == NCK - 1 else None)
            pe_g = b.s_pe.n
            b.wait("v", b.s_pe, pe_g)
            b.op("v", "tensor_copy", g_pre[:, tg * E:(tg + 1) * E], app[:, 0:E],
                 inc=b.s_dv)
            dv_g_prev = b.s_dv.n
        dv_gate_done = b.s_dv.n
        prev_sup_last_pe = b.s_pe.n

        if sup + 1 < NSUP:
            b.wait("g", b.s_dv, dv_gate_done)
            b.wait("g", b.s_x1w, 16 * n_x1w)
            xt_cnt = load_xtile(sup + 1)

    N_X1W = b.s_x1w.n // 16
    b.wait("y", b.s_x1w, 16 * N_X1W)
    b.barrier()
    pbb.close()

    b.wait("v", b.s_csy, 16 * NCSTY_B)
    b.wait("v", b.s_dv, dv_strg_prev)
    b.op("v", "tensor_mul", vmrow[:, :], vmrow[:, :], strg[:, :], inc=b.s_dv)
    b.wait("v", b.s_dv, b.s_dv.n)
    b.op("v", "reduce_sum", sacc[0:1, 0:1], vmrow[0:1, :], axis=AX.X)
    b.op("v", "reduce_sum", sacc[0:1, 1:2], strg[0:1, :], axis=AX.X, inc=b.s_dv)
    b.op("v", "tensor_scalar_mul", sacc[0:1, 2:5], g_pre[0:1, 0:E],
         g0s[0:1, 0:1], inc=b.s_dv)
    b.wait("g", b.s_dv, b.s_dv.n)
    b.wait("g", b.s_gp, GP0)
    n3 = b.dma("g", b.s_ccd, cc2_in, sacc[:, :])
    b.wait("g", b.s_ccd, 16 * n3)
    b.cc(cc2_in, cc2_out, b.s_cc2)
    b.wait("g", b.s_cc2, b.s_cc2.n)
    n4 = b.dma("g", b.s_ccd, red2[:, :], cc2_out)
    b.wait("v", b.s_ccd, 16 * n4)
    b.op("v", "tensor_sub", small[0:1, 0:1], red2[0:1, 1:2], red2[0:1, 0:1])
    b.op("v", "tensor_scalar_mul", small[0:1, 1:2], red2[0:1, 0:1], 1.0 / N_VIS,
         inc=b.s_dv)
    b.wait("v", b.s_dv, b.s_dv.n)
    b.op("v", "tensor_scalar_mul", small[0:1, 0:1], small[0:1, 0:1], 1.0 / N_TXT)
    b.op("v", "tensor_scalar_add", small[0:1, 1:2], small[0:1, 1:2], 1e-8, inc=b.s_dv)
    b.wait("v", b.s_dv, b.s_dv.n)
    b.op("v", "reciprocal", small[0:1, 1:2], small[0:1, 1:2], inc=b.s_dv)
    b.wait("v", b.s_dv, b.s_dv.n)
    b.op("v", "tensor_mul", small[0:1, 2:3], small[0:1, 0:1], small[0:1, 1:2],
         inc=b.s_dv)
    b.wait("v", b.s_dv, b.s_dv.n)
    b.op("v", "tensor_scalar_mul", small[0:1, 4:7], gwl[0:1, :], small[0:1, 2:3],
         inc=b.s_dv)
    b.wait("v", b.s_dv, b.s_dv.n)
    b.op("v", "tensor_add", small[0:1, 4:7], small[0:1, 4:7], gbb[0:1, :], inc=b.s_dv)
    b.wait("v", b.s_dv, b.s_dv.n)
    b.op("v", "tensor_add", small[0:1, 8:11], red2[0:1, 2:5], small[0:1, 4:7],
         inc=b.s_dv)
    b.wait("v", b.s_dv, b.s_dv.n)
    a0, a1, a2 = small[0:1, 8:9], small[0:1, 9:10], small[0:1, 10:11]
    b.op("v", "tensor_tensor", small[0:1, 12:13], a0, a1, ALU.is_ge)
    b.op("v", "tensor_tensor", small[0:1, 13:14], a0, a2, ALU.is_ge)
    b.op("v", "tensor_tensor", small[0:1, 14:15], a1, a2, ALU.is_ge, inc=b.s_dv)
    b.wait("v", b.s_dv, b.s_dv.n)
    b.op("v", "tensor_add", small[0:1, 16:17], small[0:1, 12:13], small[0:1, 13:14])
    b.op("v", "tensor_sub", small[0:1, 17:18], small[0:1, 12:13], small[0:1, 14:15])
    b.op("v", "tensor_add", small[0:1, 18:19], small[0:1, 13:14], small[0:1, 14:15],
         inc=b.s_dv)
    b.wait("v", b.s_dv, b.s_dv.n)
    b.op("v", "tensor_scalar", small[0:1, 16:17], small[0:1, 16:17], -1.0, 2.0,
         ALU.mult, ALU.add)
    b.op("v", "tensor_scalar_add", small[0:1, 17:18], small[0:1, 17:18], 1.0,
         inc=b.s_dv)
    b.wait("v", b.s_dv, b.s_dv.n)
    b.op("v", "tensor_single_scalar", m01[0:1, 0:3], small[0:1, 16:19], 0.0,
         ALU.is_equal)
    b.op("v", "tensor_single_scalar", m01[0:1, 3:6], small[0:1, 16:19], 1.0,
         ALU.is_equal, inc=b.s_dv)

    gm = sb("gm", [128, 6 * 16], F32, pb)
    lg = sb("lg", [128, NTC * E], F32, pb)
    b.wait("t", b.s_dv, b.s_dv.n)
    b.mm(app[:, 0:E], ones_f32[0:1, 0:128], small[0:1, 4:7], start=True, stop=True,
         inc=b.s_pe)
    pe_add3 = b.s_pe.n
    b.wait("v", b.s_pe, pe_add3)
    for tc in range(NTC):
        b.op("v", "tensor_tensor", lg[:, tc * E:(tc + 1) * E],
             app[:, 0:E], g_pre[:, tc * E:(tc + 1) * E], ALU.add,
             inc=b.s_dv if tc == NTC - 1 else None)
    b.wait("s", b.s_dv, b.s_dv.n)
    b.op("s", "activation", lg[:, :], lg[:, :], AF.Exp, inc=b.s_ac)
    b.wait("v", b.s_ac, b.s_ac.n)
    lg3 = lg[:, :].rearrange("p (c e) -> p c e", e=E)
    b.op("v", "reduce_max", gm[:, 0:16], lg3, axis=AX.X)
    b.op("v", "tensor_reduce", gm[:, 16:32], lg3, axis=AX.X, op=ALU.min)
    b.op("v", "reduce_sum", gm[:, 32:48], lg3, axis=AX.X, inc=b.s_dv)
    b.wait("v", b.s_dv, b.s_dv.n)
    b.op("v", "tensor_sub", gm[:, 48:64], gm[:, 32:48], gm[:, 0:16], inc=b.s_dv)
    b.wait("v", b.s_dv, b.s_dv.n)
    b.op("v", "tensor_sub", gm[:, 48:64], gm[:, 48:64], gm[:, 16:32], inc=b.s_dv)
    b.wait("v", b.s_dv, b.s_dv.n)
    b.op("v", "tensor_scalar_mul", gm[:, 64:80], gm[:, 32:48], 1e-8, inc=b.s_dv)
    b.wait("v", b.s_dv, b.s_dv.n)
    b.op("v", "tensor_add", gm[:, 64:80], gm[:, 64:80], gm[:, 0:16], inc=b.s_dv)
    b.wait("v", b.s_dv, b.s_dv.n)
    b.op("v", "tensor_add", gm[:, 64:80], gm[:, 64:80], gm[:, 48:64], inc=b.s_dv)
    b.wait("v", b.s_dv, b.s_dv.n)
    b.op("v", "reciprocal", gm[:, 64:80], gm[:, 64:80], inc=b.s_dv)
    b.wait("v", b.s_dv, b.s_dv.n)
    b.op("v", "tensor_mul", gm[:, 0:16], gm[:, 0:16], gm[:, 64:80])
    b.op("v", "tensor_mul", gm[:, 48:64], gm[:, 48:64], gm[:, 64:80], inc=b.s_dv)
    dv_c = b.s_dv.n
    c0row = sb("c0row", [1, T], F32, pb)
    c1row = sb("c1row", [1, T], F32, pb)
    ctr = sb("ctr", [16, 128], F32, pb)
    b.wait("t", b.s_dv, dv_c)
    b.wait("t", b.s_csy, 16 * NCSTY_B)
    for idx, (c0_, dst_row) in enumerate(((0, c0row), (48, c1row))):
        if idx == 1:
            b.wait("t", b.s_ac, b.s_ac.n)
        b.transp(app[0:16, 0:128], gm[:, c0_:c0_ + 16], idf[:, :], inc=b.s_pe)
        b.wait("s", b.s_pe, b.s_pe.n)
        b.wait("s", b.s_ccd, 16 * (b.s_ccd.n // 16))
        b.op("s", "copy", ctr[:, :], app[0:16, 0:128], inc=b.s_ac)
        b.wait("g", b.s_ac, b.s_ac.n)
        nci = b.dma("g", b.s_ccd, scr16[idx], ctr[:, :])
        b.wait("g", b.s_ccd, 16 * nci)
        ncj = b.dma("g", b.s_ccd, dst_row[0:1, :],
                    scr16[idx].rearrange("a bb -> (a bb)")[None, :])
    b.wait("v", b.s_ccd, 16 * (b.s_ccd.n // 16))
    wbr = sb("wbr", [1, T], BF16, pb)
    dv_wrow = [0] * E
    w3cnt = [0] * E
    for e in range(E):
        if e > 0:
            b.wait("v", b.s_ccd, 16 * w3cnt[e - 1])
        for half in range(2):
            hs = slice(half * TB, (half + 1) * TB)
            if e > 0 or half > 0:
                b.wait("v", b.s_dv, b.s_dv.n)
            b.op("v", "tensor_scalar_mul", rows[0:1, 0:TB], c0row[0:1, hs],
                 m01[0:1, e:e + 1])
            b.op("v", "tensor_scalar_mul", rows[0:1, TB:2 * TB], c1row[0:1, hs],
                 m01[0:1, 3 + e:4 + e], inc=b.s_dv)
            b.wait("v", b.s_dv, b.s_dv.n)
            b.op("v", "tensor_add", rows[0:1, 0:TB], rows[0:1, 0:TB],
                 rows[0:1, TB:2 * TB], inc=b.s_dv)
            b.wait("v", b.s_dv, b.s_dv.n)
            b.op("v", "tensor_copy", wbr[0:1, hs], rows[0:1, 0:TB], inc=b.s_dv)
        dv_wrow[e] = b.s_dv.n
        b.wait("g", b.s_dv, dv_wrow[e])
        b.wait("g", b.s_ccd, b.s_ccd.n)
        b.dma("g", b.s_ccd, w3b[e:e + 1, :], wbr[0:1, :])
        w3cnt[e] = b.dma("g", b.s_ccd, w0d[0:1, e * T:(e + 1) * T], wbr[0:1, :])
    b.wait("y", b.s_ccd, 16 * w3cnt[E - 1])

    b.barrier()
    pb.close()

    f1 = ExitStack()
    w0b = sb("w0b", [1, E * T], BF16, f1)
    x1s = sb("x1s", [128, NCK * T], BF16, f1)
    wbc = sb("wbc", [128, E * T], BF16, f1)
    w1col = [sb(f"w1col{i}", [128, NCK * 128], BF16, f1) for i in range(3)]
    t1b = [sb(f"t1b{i}", [128, 512], BF16, f1) for i in range(2)]
    actb = [sb(f"actb{i}", [128, T], BF16, f1) for i in range(2)]
    h1p = [f1.enter_context(nc.psum_tensor(f"h1p{i}", [128, 512], F32))
           for i in range(2)]

    b.wait("g", b.s_x1w, 16 * N_X1W)
    b.wait("g", b.s_cst, b.s_cst.n)
    nw0 = b.dma("g", b.s_cst, w0b[:, :], w0d)
    b.wait("t", b.s_cst, 16 * nw0)
    dv_wbc = 0
    for e in range(E):
        for qq in range(T // 512):
            if dv_wbc:
                b.wait("t", b.s_dv, dv_wbc)
            b.mm(h1p[0][:, :], ones_bfr[0:1, 0:128],
                 w0b[0:1, e * T + qq * 512:e * T + (qq + 1) * 512],
                 start=True, stop=True, inc=b.s_pe)
            b.wait("v", b.s_pe, b.s_pe.n)
            b.op("v", "tensor_copy", wbc[:, e * T + qq * 512:e * T + (qq + 1) * 512],
                 h1p[0][:, :], inc=b.s_dv)
            dv_wbc = b.s_dv.n

    nx1 = 0
    for qq in range(8):
        nx1 = b.dma("y", b.s_x1r, sb3(x1s[:, qq * 4 * T:(qq + 1) * 4 * T], T),
                    fm3(x1b[qq * 512:(qq + 1) * 512, :]))

    def load_w1col(par, e, fc):
        sem = (b.s_w10, b.s_w11, b.s_w12)[par]
        b.wait("g", sem, sem.n)
        for qq in range(4):
            src = fm3(w1[e, qq * 1024:(qq + 1) * 1024, fc * 128:(fc + 1) * 128])
            b.dma("g", sem, sb3(w1col[par][:, qq * 8 * 128:(qq + 1) * 8 * 128], 128), src)
        return (sem, sem.n)

    w1_loaded = [load_w1col(0, 0, 0), load_w1col(1, 0, 1), load_w1col(2, 0, 2)]
    b.wait("s", b.s_csy, 16 * NCSTY_B)
    ACT_FN = {0: AF.Tanh, 1: AF.Gelu, 2: AF.Silu}
    ac_f1 = {}
    dv_f1 = {}
    aw_cnt = {}
    pe_f1 = {}
    for e in range(E):
        for fc in range(NCK):
            ci = e * NCK + fc
            b.wait("t", *w1_loaded[ci])
            if ci == 0:
                b.wait("t", b.s_x1r, 16 * nx1)
                b.wait("t", b.s_dv, dv_wbc)
            for tt in range(4):
                idx = ci * 4 + tt
                pp = idx % 2
                if idx >= 2:
                    b.wait("t", b.s_ac, ac_f1[idx - 2])
                for c in range(NCK):
                    b.mm(h1p[pp][:, :], w1col[ci % 3][:, c * 128:(c + 1) * 128],
                         x1s[:, c * T + tt * 512:c * T + tt * 512 + 512],
                         start=(c == 0), stop=(c == NCK - 1),
                         inc=b.s_pe if c == NCK - 1 else None)
                pe_f1[idx] = b.s_pe.n
                b.wait("s", b.s_pe, pe_f1[idx])
                if idx >= 2:
                    b.wait("s", b.s_dv, dv_f1[idx - 2])
                b.op("s", "activation", t1b[pp][:, :], h1p[pp][:, :], ACT_FN[e],
                     bias=b1pp[:, ci:ci + 1], inc=b.s_ac)
                ac_f1[idx] = b.s_ac.n
                b.wait("v", b.s_ac, ac_f1[idx])
                if ci >= 2 and tt == 0:
                    b.wait("v", *aw_cnt[ci - 2])
                if ci == 0 and tt == 0:
                    b.wait("v", b.s_dv, dv_wbc)
                b.op("v", "tensor_mul", actb[ci % 2][:, tt * 512:(tt + 1) * 512],
                     t1b[pp][:, :], wbc[:, e * T + tt * 512:e * T + tt * 512 + 512],
                     inc=b.s_dv)
                dv_f1[idx] = b.s_dv.n
            b.wait("y", b.s_dv, dv_f1[ci * 4 + 3])
            aw_sem = b.s_aw0 if ci % 2 == 0 else b.s_aw1
            b.dma("y", aw_sem, actsb[ci * 128:(ci + 1) * 128, :],
                  actb[ci % 2][:, :])
            aw_cnt[ci] = (aw_sem, aw_sem.n)
            nci = ci + 3
            if nci < E * NCK:
                b.wait("g", b.s_pe, pe_f1[ci * 4 + 3])
                w1_loaded.append(load_w1col(nci % 3, nci // NCK, nci % NCK))
    N_AW0, N_AW1 = b.s_aw0.n, b.s_aw1.n

    b.barrier()
    f1.close()

    late = ExitStack()
    st2y = sb("st2y", [1, T], F32, late)
    st2q = sb("st2q", [1, T], F32, late)
    f2 = ExitStack()
    w2blk = [sb(f"w2blk{i}", [128, 96 * 384], BF16, f2) for i in range(2)]
    actt = [sb(f"actt{i}", [128, 4 * 512], BF16, f2) for i in range(3)]
    x1r = sb("x1r", [128, 3 * T], BF16, f2)
    y2t = [sb(f"y2t{i}", [128, 512], BF16, f2) for i in range(2)]
    sqf = [sb(f"sqf{i}", [128, 512], BF16, f2) for i in range(2)]
    b2b = sb("b2b", [E, D], BF16, f2)
    moe = [f2.enter_context(nc.psum_tensor(f"moe{i}", [128, 512], F32))
           for i in range(6)]
    stc = f2.enter_context(nc.psum_tensor("stc", [128, 512], F32))
    std_ = f2.enter_context(nc.psum_tensor("std_", [128, 512], F32))

    b.wait("g", b.s_pe, b.s_pe.n)
    b.wait("g", b.s_aw0, N_AW0)
    b.wait("g", b.s_aw1, N_AW1)
    b.wait("g", b.s_cst, b.s_cst.n)
    b.dma("g", b.s_cst, b2b[:, :], b2_)
    NCST_F2 = b.s_cst.n // 16

    blocks = []
    o0 = 0
    while o0 < NCK:
        ocb = min(3, NCK - o0)
        blocks.append((o0, ocb))
        o0 += ocb
    NBLK = len(blocks)

    def load_w2blk(par, o0, ocb):
        sem = b.s_w20 if par == 0 else b.s_w21
        b.wait("g", sem, sem.n)
        for e in range(E):
            if ocb == 3:
                for cg in range(8):
                    src = fm3(w2[e, cg * 512:(cg + 1) * 512,
                                 o0 * 128:(o0 + ocb) * 128])
                    b.dma("g", sem,
                          sb3(w2blk[par][:, (e * NCK + cg * 4) * 384:
                                         (e * NCK + cg * 4 + 4) * 384], 384), src)
            else:
                for c in range(NCK):
                    src = w2[e, c * 128:(c + 1) * 128, o0 * 128:(o0 + ocb) * 128]
                    b.dma("g", sem,
                          w2blk[par][:, (e * NCK + c) * 384:
                                     (e * NCK + c) * 384 + ocb * 128], src)
        return (sem, sem.n)

    w2_loaded = [load_w2blk(0, *blocks[0])]
    if NBLK > 1:
        w2_loaded.append(load_w2blk(1, *blocks[1]))

    pe_f2_grp = {}
    dv_y2 = {}
    pe_st2 = {}
    dv_st2 = 0
    yw_cnt = {}
    par_free = {0: 0, 1: 0}
    pe_blk = [0] * NBLK

    for bi_, (o0, ocb) in enumerate(blocks):
        par = bi_ % 2
        b.wait("t", *w2_loaded[bi_])
        if bi_ == 0:
            b.wait("s", b.s_x1w, 16 * N_X1W)
            b.wait("s", b.s_aw0, N_AW0)
            b.wait("s", b.s_aw1, N_AW1)
        else:
            b.wait("s", b.s_dv, dv_y2[(bi_ - 1, NTT2 - 1, blocks[bi_ - 1][1] - 1)])
        x1r_cnt = b.dma("s", b.s_x1r,
                        sb3(x1r[:, 0:ocb * T], T),
                        fm3(x1b[o0 * 128:(o0 + ocb) * 128, :]))
        if bi_ == 0:
            b.wait("t", b.s_cst, 16 * NCST_F2)
            b.wait("y", b.s_aw0, N_AW0)
            b.wait("y", b.s_aw1, N_AW1)
        for tt in range(NTT2):
            tpar = tt % 2
            tf0 = tt * 512
            for kg in range(24):
                ring = kg % 3
                prev_key = None
                if kg >= 3:
                    prev_key = (bi_, tt, kg - 3)
                elif (bi_, tt) != (0, 0):
                    ptt = (bi_, tt - 1) if tt > 0 else (bi_ - 1, NTT2 - 1)
                    prev_key = (ptt[0], ptt[1], 21 + ring)
                if prev_key is not None:
                    b.wait("y", b.s_pe, pe_f2_grp[prev_key])
                src = fm3(actsb[kg * 512:(kg + 1) * 512, tf0:tf0 + 512])
                ar_sem = (b.s_ar0, b.s_ar1, b.s_ar2)[ring]
                b.dma("y", ar_sem, sb3(actt[ring][:, :], 512), src)
                b.wait("t", ar_sem, ar_sem.n)
                if kg == 0 and par_free[tpar]:
                    b.wait("t", b.s_dv, par_free[tpar])
                for kc in range(4):
                    k = kg * 4 + kc
                    for oc in range(ocb):
                        b.mm(moe[tpar * 3 + oc][:, :],
                             w2blk[par][:, k * 384 + oc * 128:k * 384 + oc * 128 + 128],
                             actt[ring][:, kc * 512:(kc + 1) * 512],
                             start=(k == 0), stop=False,
                             inc=b.s_pe if (kc == 3 and oc == ocb - 1) else None)
                pe_f2_grp[(bi_, tt, kg)] = b.s_pe.n
            for oc in range(ocb):
                b.mm(moe[tpar * 3 + oc][:, :],
                     b2b[0:E, (o0 + oc) * 128:(o0 + oc + 1) * 128],
                     w3b[0:E, tf0:tf0 + 512], start=False, stop=True, inc=b.s_pe)
                pe_moe = b.s_pe.n
                b.wait("v", b.s_pe, pe_moe)
                b.wait("v", b.s_x1r, 16 * x1r_cnt)
                prev = (bi_, tt - 1, oc) if tt > 0 else (
                    (bi_ - 1, NTT2 - 1, oc) if bi_ > 0 else None)
                pkey = None
                if oc >= 2:
                    pkey = (bi_, tt, oc - 2)
                elif prev is not None:
                    pocb = blocks[prev[0]][1]
                    j = pocb - 1 if (pocb - 1) % 2 == oc % 2 else pocb - 2
                    if j >= 0:
                        pkey = (prev[0], prev[1], j)
                if pkey is not None and pkey in dv_y2:
                    b.wait("v", *yw_cnt[pkey])
                    b.wait("v", b.s_pe, pe_st2[pkey])
                b.op("v", "tensor_tensor", y2t[oc % 2][:, :],
                     moe[tpar * 3 + oc][:, :],
                     x1r[:, oc * T + tf0:oc * T + tf0 + 512], ALU.add, inc=b.s_dv)
                dv_y2[(bi_, tt, oc)] = b.s_dv.n
                b.wait("s", b.s_dv, dv_y2[(bi_, tt, oc)])
                yw_sem = b.s_y2w0 if oc % 2 == 0 else b.s_y2w1
                b.dma("s", yw_sem,
                      y2b[(o0 + oc) * 128:(o0 + oc + 1) * 128, tf0:tf0 + 512],
                      y2t[oc % 2][:, :])
                yw_cnt[(bi_, tt, oc)] = (yw_sem, yw_sem.n)
                if pkey is not None and pkey in pe_st2:
                    b.wait("s", b.s_pe, pe_st2[pkey])
                b.op("s", "activation", sqf[oc % 2][:, :], y2t[oc % 2][:, :],
                     AF.Square, inc=b.s_ac)
                ac_s2 = b.s_ac.n
                b.wait("t", b.s_ac, ac_s2)
                if dv_st2:
                    b.wait("t", b.s_dv, dv_st2)
                b.mm(stc[0:1, :], onesc_bf[:, 0:1], y2t[oc % 2][:, :],
                     start=True, stop=True)
                b.mm(std_[0:1, :], onesc_bf[:, 0:1], sqf[oc % 2][:, :],
                     start=True, stop=True, inc=b.s_pe)
                pe_st2[(bi_, tt, oc)] = b.s_pe.n
                b.wait("v", b.s_pe, pe_st2[(bi_, tt, oc)])
                if dv_st2:
                    b.wait("v", b.s_dv, dv_st2)
                first = (bi_ == 0 and oc == 0)
                if first:
                    b.op("v", "tensor_copy", st2y[0:1, tf0:tf0 + 512], stc[0:1, :])
                    b.op("v", "tensor_copy", st2q[0:1, tf0:tf0 + 512], std_[0:1, :],
                         inc=b.s_dv)
                else:
                    b.op("v", "tensor_tensor", st2y[0:1, tf0:tf0 + 512], stc[0:1, :],
                         st2y[0:1, tf0:tf0 + 512], ALU.add)
                    b.op("v", "tensor_tensor", st2q[0:1, tf0:tf0 + 512], std_[0:1, :],
                         st2q[0:1, tf0:tf0 + 512], ALU.add, inc=b.s_dv)
                dv_st2 = b.s_dv.n
            par_free[tpar] = dv_y2[(bi_, tt, ocb - 1)]
        pe_blk[bi_] = b.s_pe.n
        if bi_ + 2 < NBLK:
            b.wait("g", b.s_pe, pe_blk[bi_])
            w2_loaded.append(load_w2blk((bi_ + 2) % 2, *blocks[bi_ + 2]))
    N_Y2W0, N_Y2W1 = b.s_y2w0.n, b.s_y2w1.n

    b.barrier()
    f2.close()

    f3 = ExitStack()
    y2s = sb("y2s", [128, NCK * TB], BF16, f3)
    o32 = [sb(f"o32_{i}", [128, TB], F32, f3) for i in range(2)]
    a2r = sb("a2r", [1, T], F32, f3)
    b2r = sb("b2r", [1, T], F32, f3)
    tmpr = sb("tmpr", [1, T], F32, f3)
    bca = f3.enter_context(nc.psum_tensor("bca", [128, TB], F32))
    bcb = f3.enter_context(nc.psum_tensor("bcb", [128, TB], F32))

    b.wait("v", b.s_dv, dv_st2)
    b.op("v", "tensor_scalar_mul", tmpr[0:1, :], st2y[0:1, :], 1.0 / D, inc=b.s_dv)
    b.wait("s", b.s_dv, b.s_dv.n)
    b.op("s", "square", a2r[0:1, :], tmpr[0:1, :], inc=b.s_ac)
    b.op("v", "tensor_scalar_mul", b2r[0:1, :], st2q[0:1, :], 1.0 / D, inc=b.s_dv)
    b.wait("v", b.s_ac, b.s_ac.n)
    b.wait("v", b.s_dv, b.s_dv.n)
    b.op("v", "tensor_sub", b2r[0:1, :], b2r[0:1, :], a2r[0:1, :], inc=b.s_dv)
    b.wait("v", b.s_dv, b.s_dv.n)
    b.op("v", "tensor_scalar_add", b2r[0:1, :], b2r[0:1, :], EPS_LN, inc=b.s_dv)
    b.wait("s", b.s_dv, b.s_dv.n)
    b.op("s", "activation", b2r[0:1, :], b2r[0:1, :], AF.Ln, inc=b.s_ac)
    b.wait("s", b.s_ac, b.s_ac.n)
    b.op("s", "activation", a2r[0:1, :], b2r[0:1, :], AF.Exp, scale=-0.5, inc=b.s_ac)
    b.wait("v", b.s_ac, b.s_ac.n)
    b.op("v", "tensor_mul", b2r[0:1, :], tmpr[0:1, :], a2r[0:1, :], inc=b.s_dv)
    b.wait("v", b.s_dv, b.s_dv.n)
    b.op("v", "tensor_scalar_mul", b2r[0:1, :], b2r[0:1, :], -1.0, inc=b.s_dv)
    dv_rows2 = b.s_dv.n

    n_ow = 0
    ow_cnt = {}
    dv_o = {}
    for sup in range(NSUP):
        t0 = sup * TB
        for qq in range(4):
            if sup == 0 and qq == 0:
                b.wait("y", b.s_y2w0, N_Y2W0)
                b.wait("y", b.s_y2w1, N_Y2W1)
            if sup > 0 and qq == 0:
                b.wait("y", b.s_ow0, b.s_ow0.n)
                b.wait("y", b.s_ow1, b.s_ow1.n)
            src = fm3(y2b[qq * 1024:(qq + 1) * 1024, t0:t0 + TB])
            ny2r = b.dma("y", b.s_y2r, sb3(y2s[:, qq * 8 * TB:(qq + 1) * 8 * TB], TB), src)
        b.wait("t", b.s_y2r, 16 * ny2r)
        b.wait("t", b.s_dv, dv_rows2)
        if sup > 0:
            b.wait("t", b.s_dv, dv_o[(sup - 1, NCK - 1)])
        for i in range(2):
            b.mm(bca[:, i * 512:(i + 1) * 512], ones_f32[0:1, 0:128],
                 a2r[0:1, t0 + i * 512:t0 + i * 512 + 512],
                 start=True, stop=True)
            b.mm(bcb[:, i * 512:(i + 1) * 512], ones_f32[0:1, 0:128],
                 b2r[0:1, t0 + i * 512:t0 + i * 512 + 512],
                 start=True, stop=True, inc=b.s_pe if i == 1 else None)
        pe_bc2 = b.s_pe.n
        b.wait("v", b.s_pe, pe_bc2)
        for oc in range(NCK):
            if oc >= 2:
                b.wait("v", *ow_cnt[(sup, oc - 2)])
            elif sup > 0:
                b.wait("v", *ow_cnt[(sup - 1, NCK - 2 + oc)])
            for i in range(2):
                seg = y2s[:, oc * TB + i * 512:oc * TB + i * 512 + 512]
                dst = o32[oc % 2][:, i * 512:i * 512 + 512]
                b.op("v", "tensor_tensor", dst, bca[:, i * 512:(i + 1) * 512],
                     seg, ALU.mult, inc=b.s_dv)
                b.wait("v", b.s_dv, b.s_dv.n)
                b.op("v", "tensor_tensor", dst, bcb[:, i * 512:(i + 1) * 512],
                     dst, ALU.add, inc=b.s_dv if i == 1 else None)
            dv_o[(sup, oc)] = b.s_dv.n
            b.wait("s", b.s_dv, dv_o[(sup, oc)])
            ow_sem = b.s_ow0 if oc % 2 == 0 else b.s_ow1
            b.dma("s", ow_sem, outT[oc * 128:(oc + 1) * 128, t0:t0 + TB],
                  o32[oc % 2][:, :])
            ow_cnt[(sup, oc)] = (ow_sem, ow_sem.n)

    b.wait("y", b.s_ow0, b.s_ow0.n)
    b.wait("y", b.s_ow1, b.s_ow1.n)
    b.barrier()
    f3.close()
    late.close()

    blk = ctx.enter_context(nc.Block())

    def runner(ops_list):
        def run(eng):
            for f in ops_list:
                f(eng)
        return run

    blk.tensor(runner(b.ops["t"]))
    blk.vector(runner(b.ops["v"]))
    blk.scalar(runner(b.ops["s"]))
    blk.gpsimd(runner(b.ops["g"]))
    blk.sync(runner(b.ops["y"]))

    b.check_deadlock()
    ctx.close()
    return nc


_NC_CACHE = None


def _get_nc():
    global _NC_CACHE
    if _NC_CACHE is None:
        _NC_CACHE = build_program()
    return _NC_CACHE


def kernel(**inputs):
    x = np.asarray(inputs["L0_attn_hidden_states"], dtype=np.float32)
    Wq = np.asarray(inputs["L0_attn_Wq"], dtype=np.float32)
    Wk = np.asarray(inputs["L0_attn_Wk"], dtype=np.float32)
    Wv = np.asarray(inputs["L0_attn_Wv"], dtype=np.float32)
    Wo = np.asarray(inputs["L0_attn_Wo"], dtype=np.float32)
    bq = np.asarray(inputs["L0_attn_bq"], dtype=np.float32)
    bk = np.asarray(inputs["L0_attn_bk"], dtype=np.float32)
    bv = np.asarray(inputs["L0_attn_bv"], dtype=np.float32)
    bo = np.asarray(inputs["L0_attn_bo"], dtype=np.float32)
    gate_w = np.asarray(inputs["L0_ffn_gate_w"], dtype=np.float32)
    gate_b = np.asarray(inputs["L0_ffn_gate_b"], dtype=np.float32)
    w1_ = np.asarray(inputs["L0_ffn_w1"], dtype=np.float32)
    b1_ = np.asarray(inputs["L0_ffn_b1"], dtype=np.float32)
    w2_ = np.asarray(inputs["L0_ffn_w2"], dtype=np.float32)
    b2v = np.asarray(inputs["L0_ffn_b2"], dtype=np.float32)

    shared = dict(
        wqT=np.ascontiguousarray(Wq.T),
        wkT=np.ascontiguousarray(Wk.T),
        wvT=np.ascontiguousarray(Wv.T),
        woT=np.ascontiguousarray(Wo.T),
        w1=np.ascontiguousarray(w1_),
        w2=np.ascontiguousarray(w2_),
        bqc=np.ascontiguousarray(bq.reshape(NCK, 128).T),
        bk_r=bk.reshape(1, DK),
        bv_r=bv.reshape(1, DK),
        boc=np.ascontiguousarray(bo.reshape(NCK, 128).T),
        b1r=np.ascontiguousarray(
            b1_.reshape(E, NCK, 128).transpose(2, 0, 1).reshape(128, E * NCK)),
        b2_=np.ascontiguousarray(b2v),
        gwr=np.ascontiguousarray(
            gate_w[:D].reshape(NCK, 128, E).transpose(1, 0, 2).reshape(128, NCK * E)),
        gwlast=np.ascontiguousarray(gate_w[D:D + 1]),
        gateb=gate_b.reshape(1, E),
        ident=np.eye(128, dtype=np.float32),
    )
    in_maps = []
    for c in range(8):
        bidx, s = divmod(c, 4)
        sl = x[bidx, s * T:(s + 1) * T, :]
        vis = np.zeros((1, T), dtype=np.float32)
        lo = s * T
        n_vis = min(max(VIS - lo, 0), T)
        vis[0, :n_vis] = 1.0
        m = dict(shared)
        m["x0T"] = np.ascontiguousarray(sl.T)
        m["vism"] = vis
        m["g0s"] = np.array([[1.0 if s == 0 else 0.0]], dtype=np.float32)
        in_maps.append(m)

    nc = _get_nc()
    res = run_bass_kernel_spmd(nc, in_maps, core_ids=list(range(8)))
    out = np.empty((2, SEQ, D), dtype=np.float32)
    for c in range(8):
        bidx, s = divmod(c, 4)
        out[bidx, s * T:(s + 1) * T, :] = res.results[c]["outT"].T
    return out
